# version 10
# baseline (speedup 1.0000x reference)
import math
import numpy as np
from contextlib import ExitStack
from collections import defaultdict

import concourse.bass as bass
import concourse.mybir as mybir
from concourse.bass_utils import run_bass_kernel_spmd

F32 = mybir.dt.float32
BF16 = mybir.dt.bfloat16
AF = mybir.ActivationFunctionType
ALU = mybir.AluOpType
AX = mybir.AxisListType

ENGINES = ("tensor", "vector", "scalar", "gpsimd", "sync")

LAMBDA_INIT = 0.8 - 0.6 * math.exp(-0.3 * 1)
EXP_BIAS = 24.0
LN_EPS = 1e-5
SUBLN_EPS = 1e-5


class Prog:
    def __init__(self, nc):
        self.nc = nc
        self.actions = {e: [] for e in ENGINES}
        self.count = {e: 0 for e in ENGINES}
        self.dma_slot_count = {}
        self.marked = {e: set() for e in ENGINES}
        self.writer = {}
        self.readers = defaultdict(list)
        self.water = {e: defaultdict(int) for e in ENGINES}

    def _need(self, deps, tok):
        kind, eng, val = tok
        key = (kind, eng)
        if val > deps.get(key, 0):
            deps[key] = val

    def _gather(self, engine, reads, writes):
        deps = {}
        for b in reads:
            w = self.writer.get(b)
            if w is not None:
                self._need(deps, w)
        for b in writes:
            w = self.writer.get(b)
            if w is not None:
                self._need(deps, w)
            for r in self.readers.get(b, ()):
                self._need(deps, r)
        waits = []
        for (kind, eng), val in deps.items():
            if kind == "eng" and eng == engine == "tensor":
                continue
            if self.water[engine][(kind, eng)] >= val:
                continue
            if kind == "eng" and eng == engine:
                assert val <= self.count[engine], "self-wait on future op"
            waits.append(((kind, eng), val))
            self.water[engine][(kind, eng)] = val
            if kind == "eng":
                self.marked[eng].add(val)
        return waits

    def op(self, engine, fn, reads=(), writes=(), tag=""):
        waits = self._gather(engine, reads, writes)
        self.count[engine] += 1
        tok = ("eng", engine, self.count[engine])
        for b in writes:
            self.writer[b] = tok
            self.readers[b] = []
        for b in reads:
            self.readers[b].append(tok)
        self.actions[engine].append(dict(waits=waits, fn=fn, kind="eng",
                                         tag=tag, seq=self.count[engine]))
        return tok

    def dma(self, engine, fn, reads=(), writes=(), tag="", slot=None):
        if slot is None:
            slot = writes[0] if writes else "out"
        key = (engine, slot)
        waits = self._gather(engine, reads, writes)
        prev = self.dma_slot_count.get(key, 0)
        if prev > 0 and self.water[engine][("dma", key)] < prev:
            waits.append((("dma", key), prev))
            self.water[engine][("dma", key)] = prev
        self.dma_slot_count[key] = prev + 16
        tok = ("dma", key, self.dma_slot_count[key])
        for b in writes:
            self.writer[b] = tok
            self.readers[b] = []
        for b in reads:
            self.readers[b].append(tok)
        self.actions[engine].append(dict(waits=waits, fn=fn, kind="dma",
                                         tag=tag, seq=self.dma_slot_count[key],
                                         slot=key))
        return tok

    def fence_all(self, tag="fence", dma=False):
        for engine in ENGINES:
            waits = []
            for e in ENGINES:
                if self.count[e] > 0 \
                        and self.water[engine][("eng", e)] < self.count[e]:
                    waits.append((("eng", e), self.count[e]))
                    self.water[engine][("eng", e)] = self.count[e]
                    self.marked[e].add(self.count[e])
            if dma:
                for k, v in self.dma_slot_count.items():
                    if self.water[engine][("dma", k)] < v:
                        waits.append((("dma", k), v))
                        self.water[engine][("dma", k)] = v
            if waits:
                self.actions[engine].append(dict(
                    waits=waits, fn=lambda e: None, kind="none", tag=tag, seq=-1))

    def wait_all(self, engine, tag="final"):
        waits = []
        for e in ENGINES:
            if e != engine and self.count[e] > 0 \
                    and self.water[engine][("eng", e)] < self.count[e]:
                waits.append((("eng", e), self.count[e]))
                self.water[engine][("eng", e)] = self.count[e]
                self.marked[e].add(self.count[e])
        for k, v in self.dma_slot_count.items():
            if self.water[engine][("dma", k)] < v:
                waits.append((("dma", k), v))
                self.water[engine][("dma", k)] = v
        self.actions[engine].append(dict(waits=waits, fn=lambda e: None,
                                         kind="none", tag=tag, seq=-1))

    def build(self, es):
        nc = self.nc
        inc_val = {}
        for e in ENGINES:
            marked = sorted(self.marked[e])
            m, c = {}, 0
            it = iter(marked)
            nxt = next(it, None)
            for s in range(1, self.count[e] + 1):
                if nxt == s:
                    c += 1
                    nxt = next(it, None)
                m[s] = c
            inc_val[e] = m
        sems = {}
        for e in ENGINES:
            if self.marked[e]:
                sems[("eng", e)] = es.enter_context(nc.semaphore(f"se_{e}"))
        for i, k in enumerate(self.dma_slot_count):
            sems[("dma", k)] = es.enter_context(nc.semaphore(f"sd_{i}"))
        block = es.enter_context(nc.Block())

        def run_stream(eng_name):
            def _(eng):
                own = sems.get(("eng", eng_name))
                mark = self.marked[eng_name]
                for a in self.actions[eng_name]:
                    for (kind, pk), val in a["waits"]:
                        if kind == "eng":
                            eng.wait_ge(sems[("eng", pk)], inc_val[pk][val])
                        else:
                            eng.wait_ge(sems[("dma", pk)], val)
                    inst = a["fn"](eng)
                    if a["kind"] == "eng" and a["seq"] in mark:
                        inst.then_inc(own, 1)
                    elif a["kind"] == "dma":
                        inst.then_inc(sems[("dma", a["slot"])], 16)
            return _

        for e in ENGINES:
            if self.actions[e]:
                getattr(block, e)(run_stream(e))
        return sems


class Overlay:

    def __init__(self, nc, es, total_bytes, name="ovl"):
        self.nc = nc
        self.backing = es.enter_context(
            nc.sbuf_tensor(name, [128, total_bytes], mybir.dt.uint8))
        self.base = nc.lookup_mloc(self.backing).addr
        self.total = total_bytes

    def at(self, name, off, shape, dtype):
        assert off % 32 == 0, (name, off)
        per_part = int(np.prod(shape[1:])) * mybir.dt.size(dtype)
        assert off + per_part <= self.total, (name, off, per_part, self.total)
        h = self.nc.virtual_sbuf_tensor(name, shape, dtype)
        self.nc._tensor(h.name, h.shape, h.dtype, type="SB")
        mloc = self.nc.lookup_mloc(h)
        mloc.allocated = not self.nc.target_bir_lowering
        mloc.addr = self.base + off
        return h


def make_cfg(mini=False):
    if mini:
        return dict(D=512, S=1024, NH=4, NKV=2, HID=256, SHID=256, E=8,
                    B=1, CORES=2, debug=True)
    return dict(D=2048, S=2048, NH=8, NKV=4, HID=1024, SHID=1024, E=8,
                B=4, CORES=8, debug=False)


def own_qbs(p, SB):
    out = []
    for a in range(SB // 4):
        out += ([4 * a, 4 * a + 3] if p == 0 else [4 * a + 1, 4 * a + 2])
    return sorted(out)


def build_program(cfg):
    D, S = cfg["D"], cfg["S"]
    NH, NKV, E = cfg["NH"], cfg["NKV"], cfg["E"]
    HID, SHID = cfg["HID"], cfg["SHID"]
    KC, SB = D // 128, S // 128
    NJ = SB // 2
    OT = NJ * 128
    QD, KD, VD = NH * 256, NKV * 256, NKV * 256
    QC, KCo, VC = QD // 128, KD // 128, VD // 128
    NOC = QC + KCo + VC
    HC, SHC = HID // 128, SHID // 128
    TCW = min(512, OT)
    TC = OT // TCW
    VS = NKV * 257
    HTOK = min(1024, S)
    NHALF = S // HTOK
    OH = min(512, OT)
    NOH = OT // OH
    SCALE = 128.0 ** -0.5
    dbg = cfg["debug"]

    nc = bass.Bass("TRN2", target_bir_lowering=False,
                   detect_race_conditions=False)
    p = Prog(nc)
    p.stg_rr = 0

    x_fm = nc.dram_tensor("x_fm", [KC, 128, S], F32, kind="ExternalInput").ap()
    x_own = nc.dram_tensor("x_own", [KC, 128, OT], F32, kind="ExternalInput").ap()
    wqkvT = nc.dram_tensor("wqkvT", [NOC, KC, 128, 128], F32, kind="ExternalInput").ap()
    bqkv_h = nc.dram_tensor("bqkv", [128, NOC], F32, kind="ExternalInput").ap()
    woT = nc.dram_tensor("woT", [KC, QC, 128, 128], F32, kind="ExternalInput").ap()
    bo_h = nc.dram_tensor("bo", [128, KC], F32, kind="ExternalInput").ap()
    lam_in = nc.dram_tensor("lam_in", [128, 4], F32, kind="ExternalInput").ap()
    subw_bc = nc.dram_tensor("subw_bc", [128, 256], F32, kind="ExternalInput").ap()
    masks_h = nc.dram_tensor("masks", [NJ, 2, 128, 128], BF16, kind="ExternalInput").ap()
    ident_f = nc.dram_tensor("ident_f", [128, 128], F32, kind="ExternalInput").ap()
    ident_b = nc.dram_tensor("ident_b", [128, 128], BF16, kind="ExternalInput").ap()
    gwT_h = nc.dram_tensor("gwT", [KC, 128, 8], F32, kind="ExternalInput").ap()
    w1T_h = nc.dram_tensor("w1T", [E, KC, 128, HID], F32, kind="ExternalInput").ap()
    w3T_h = nc.dram_tensor("w3T", [E, KC, 128, HID], F32, kind="ExternalInput").ap()
    w2T_h = nc.dram_tensor("w2T", [E, KC, HC, 128, 128], F32, kind="ExternalInput").ap()
    sw1T_h = nc.dram_tensor("sw1T", [KC, 128, SHID], F32, kind="ExternalInput").ap()
    sw3T_h = nc.dram_tensor("sw3T", [KC, 128, SHID], F32, kind="ExternalInput").ap()
    sw2T_h = nc.dram_tensor("sw2T", [KC, SHC, 128, 128], F32, kind="ExternalInput").ap()
    out_h = nc.dram_tensor("out_fm", [KC, 128, OT], F32, kind="ExternalOutput").ap()
    dbg_h = {}
    if dbg:
        for nm, sh in [("d_xln", [KC, 128, S]), ("d_q", [QC, 128, OT]),
                       ("d_k", [KCo, 128, S]), ("d_v", [128, SB * VS]),
                       ("d_otm", [128, NJ * QD]), ("d_Xmid", [KC, 128, OT]),
                       ("d_xln2", [KC, 128, OT]), ("d_bfm", [128, OT])]:
            dbg_h[nm] = nc.dram_tensor(nm, sh, F32, kind="ExternalOutput").ap()

    es = ExitStack()
    with es:
        consts = es.enter_context(nc.sbuf_tensor("consts", [128, 8], F32))
        lam_sb = es.enter_context(nc.sbuf_tensor("lam_sb", [128, 8], F32))
        onesb = es.enter_context(nc.sbuf_tensor("onesb", [128, 128], BF16))
        onesf = es.enter_context(nc.sbuf_tensor("onesf", [128, 128], F32))
        selS = es.enter_context(nc.sbuf_tensor("selS", [128, 128], F32))
        id8 = es.enter_context(nc.sbuf_tensor("id8", [128, 8], F32))
        stg = es.enter_context(nc.sbuf_tensor("stg", [128, 2 * 1024], F32))
        SLOT = [(slice(i * 1024, (i + 1) * 1024), f"stg{i}") for i in range(2)]
        psum = [es.enter_context(nc.psum_tensor(f"pb{i}", [128, 512], F32))
                for i in range(8)]
        PK = [f"pb{i}" for i in range(8)]

        def psum_alias(i, name, dtype):
            h = nc.virtual_psum_tensor(name, [128, 512], dtype)
            nc._tensor(h.name, h.shape, h.dtype, type="PSUM")
            mloc = nc.lookup_mloc(h)
            src = nc.lookup_mloc(psum[i])
            mloc.allocated = not nc.target_bir_lowering
            mloc.addr = src.addr
            mloc.bank = src.bank
            return h
        psumb = [psum_alias(i, f"pbb{i}", BF16) for i in range(8)]

        KB = 1024

        def rnd(x):
            return (int(x) + 31) & ~31
        WSTN = max(KC, QC) * 128
        HTOKB = HTOK
        specs = [
            ("xln",     KC * S * 2,        1, 2),
            ("x_half",  KC * HTOK * 4,     1, 1),
            ("statA_m", HTOK * 4,          1, 3),
            ("statA_s", HTOK * 4,          1, 3),
            ("statA_t", HTOK * 4,          1, 3),
            ("statA_xb", HTOK * 2,         1, 3),
            ("statA_sq", HTOK * 2,         1, 3),
            ("wstrip",  2 * WSTN * 2,      1, 7),
            ("masks",   NJ * 2 * 128 * 2,  0, 4),
            ("idf",     128 * 4,           0, 9),
            ("idb",     128 * 2,           0, 6),
            ("weff",    256 * 4,           0, 6),
            ("bqkv",    NOC * 4,           0, 3),
            ("bo",      KC * 4,            0, 7),
            ("vfm",     2 * S * 2,         2, 2),
            ("kbuf",    KCo * S * 2,       2, 4),
            ("vbuf",    SB * VS * 2,       2, 4),
            ("xo_half", KC * OH * 4,       3, 3),
            ("xow",     KC * OT * 2,       3, 3),
            ("qbuf",    QC * OT * 2,       3, 4),
            ("psb",     1536 * 2,          4, 4),
            ("osc",     max(QD, 2048) * 4,  4, 6),
            ("rsc",     16 * 4,            4, 6),
            ("ms_sb",   NJ * NH * 4,       4, 6),
            ("otm",     NJ * QD * 2,       4, 6),
            ("O_fm",    QC * OT * 2,       5, 7),
            ("Xres",    KC * OT * 4,       7, 11),
            ("w1b",     KC * HID * 2,      7, 10),
            ("w3b",     KC * HID * 2,      7, 10),
            ("statB_m", OT * 4,            7, 8),
            ("statB_s", OT * 4,            7, 8),
            ("statB_t", OT * 4,            7, 8),
            ("statB_xb", OT * 2,           7, 8),
            ("statB_sq", OT * 2,           7, 8),
            ("xln2",    KC * OT * 2,       8, 10),
            ("gwb",     KC * 8 * 2,        7, 10),
            ("xstg",    2 * TCW * 4,       7, 7),
            ("scr8",    96 * 4,            9, 9),
            ("b_fm",    OT * 4,            9, 10),
            ("Hb",      HC * OT * 2,       10, 10),
            ("B_sb",    OT * 4,            10, 10),
            ("w2b",     3 * HC * 128 * 2,  10, 10),
            ("sUg",     OT * 4,            10, 10),
            ("escr",    2 * TCW * 4,       10, 10),
        ]
        def pack(order):
            placed, offs = [], {}
            for name, size, lo, hi in order:
                size = rnd(size)
                cands = sorted({0} | {o + sz for o, sz, l, h in placed
                                if not (h < lo or l > hi)})
                pos = None
                for cand in cands:
                    if all(h < lo or l > hi or cand + size <= o or cand >= o + sz
                           for o, sz, l, h in placed):
                        pos = cand
                        break
                offs[name] = pos
                placed.append((pos, size, lo, hi))
            return offs, rnd(max(o + sz for o, sz, _, _ in placed))
        best = None
        for keyf in (lambda t: -t[1],
                     lambda t: (-(t[3] - t[2]), -t[1]),
                     lambda t: (t[2], -t[1]),
                     lambda t: (-(t[3] - t[2]) * 1000000 - t[1],)):
            offs, tot = pack(sorted(specs, key=keyf))
            if best is None or tot < best[1]:
                best = (offs, tot)
        off, TOTAL = best
        ovl = Overlay(nc, es, TOTAL)

        xln = ovl.at("xln", off["xln"], [128, KC * S], BF16)
        x_half = ovl.at("x_half", off["x_half"], [128, KC * HTOK], F32)
        mA = ovl.at("mA", off["statA_m"], [128, HTOK], F32)
        sA = ovl.at("sA", off["statA_s"], [128, HTOK], F32)
        tA = ovl.at("tA", off["statA_t"], [128, HTOK], F32)
        xbA = ovl.at("xbA", off["statA_xb"], [128, HTOK], BF16)
        sqA = ovl.at("sqA", off["statA_sq"], [128, HTOK], BF16)
        mB = ovl.at("mB", off["statB_m"], [128, OT], F32)
        sB = ovl.at("sB", off["statB_s"], [128, OT], F32)
        tB = ovl.at("tB", off["statB_t"], [128, OT], F32)
        xbB = ovl.at("xbB", off["statB_xb"], [128, OT], BF16)
        sqB = ovl.at("sqB", off["statB_sq"], [128, OT], BF16)
        wstrip = ovl.at("wstrip", off["wstrip"], [128, 2 * WSTN], BF16)
        WST = [(slice(i * WSTN, (i + 1) * WSTN), f"wst{i}") for i in range(2)]
        mask_sb = ovl.at("mask_sb", off["masks"], [128, NJ * 2 * 128], BF16)
        idf = ovl.at("idf", off["idf"], [128, 128], F32)
        idb = ovl.at("idb", off["idb"], [128, 128], BF16)
        weff = ovl.at("weff", off["weff"], [128, 256], F32)
        bqkv = ovl.at("bqkv_sb", off["bqkv"], [128, NOC], F32)
        bo_sb = ovl.at("bo_sb", off["bo"], [128, KC], F32)
        vfm = ovl.at("vfm", off["vfm"], [128, 2 * S], BF16)
        kbuf = ovl.at("kbuf", off["kbuf"], [128, KCo * S], BF16)
        vbuf = ovl.at("vbuf", off["vbuf"], [128, SB * VS], BF16)
        xo_half = ovl.at("xo_half", off["xo_half"], [128, KC * OH], F32)
        xow = ovl.at("xow", off["xow"], [128, KC * OT], BF16)
        qbuf = ovl.at("qbuf", off["qbuf"], [128, QC * OT], BF16)
        psb = ovl.at("psb", off["psb"], [128, 1536], BF16)
        osc = ovl.at("osc", off["osc"], [128, max(QD, 2048)], F32)
        rsc = ovl.at("rsc", off["rsc"], [128, 16], F32)
        ms_sb = ovl.at("ms_sb", off["ms_sb"], [128, NJ * NH], F32)
        otm = ovl.at("otm", off["otm"], [128, NJ * QD], BF16)
        O_fm = ovl.at("O_fm", off["O_fm"], [128, QC * OT], BF16)
        X = ovl.at("Xres", off["Xres"], [128, KC * OT], F32)
        w1b = ovl.at("w1b", off["w1b"], [128, KC * HID], BF16)
        w3b = ovl.at("w3b", off["w3b"], [128, KC * HID], BF16)
        xln2 = ovl.at("xln2", off["xln2"], [128, KC * OT], BF16)
        gwb = ovl.at("gwb", off["gwb"], [128, KC * 8], BF16)
        xstg = ovl.at("xstg", off["xstg"], [128, 2 * TCW], F32)
        scr8 = ovl.at("scr8", off["scr8"], [128, 96], F32)
        b_fm = ovl.at("b_fm", off["b_fm"], [128, OT], F32)
        Hb = ovl.at("Hb", off["Hb"], [128, HC * OT], BF16)
        B_sb = ovl.at("B_sb", off["B_sb"], [128, OT], F32)
        w2b = ovl.at("w2b", off["w2b"], [128, 3 * HC * 128], BF16)
        sUg = ovl.at("sUg", off["sUg"], [128, OT], F32)
        escr = ovl.at("escr", off["escr"], [128, 2 * TCW], F32)

        p.dma("scalar", lambda e: e.dma_start(out=idf[:, :], in_=ident_f[:, :]), writes=[("idf", 0)])
        p.dma("scalar", lambda e: e.dma_start(out=id8[:, :], in_=ident_f[:, 0:8]), writes=[("id8", 0)])
        p.dma("scalar", lambda e: e.dma_start(out=idb[:, :], in_=ident_b[:, :]), writes=[("idb", 0)])
        p.dma("scalar", lambda e: e.dma_start(out=lam_sb[:, 0:4], in_=lam_in[:, :]), writes=[("lamin", 0)])
        p.dma("scalar", lambda e: e.dma_start(out=weff[:, :], in_=subw_bc[:, :]), writes=[("weff", 0)])
        p.dma("scalar", lambda e: e.dma_start(out=bqkv[:, :], in_=bqkv_h[:, :]), writes=[("bqkv", 0)])
        p.dma("scalar", lambda e: e.dma_start(out=bo_sb[:, :], in_=bo_h[:, :]), writes=[("bo", 0)])
        p.dma("scalar", lambda e: e.dma_start(
            out=mask_sb[:, :].rearrange("p (j t n) -> p j t n", j=NJ, t=2),
            in_=masks_h.rearrange("j t p n -> p j t n")), writes=[("masks", 0)])
        p.op("vector", lambda e: e.memset(onesb[:, :], 1.0), writes=[("onesb", 0)])
        p.op("vector", lambda e: e.memset(onesf[:, :], 1.0), writes=[("onesf", 0)])
        p.op("vector", lambda e: e.memset(consts[:, 0:1], -EXP_BIAS), writes=[("c0", 0)])
        p.op("vector", lambda e: e.tensor_scalar_mul(weff[:, :], weff[:, :], 1.0 - LAMBDA_INIT),
             reads=[("weff", 0)], writes=[("weff", 0)])
        p.op("vector", lambda e: e.tensor_mul(lam_sb[:, 4:5], lam_sb[:, 0:1], lam_sb[:, 1:2]),
             reads=[("lamin", 0)], writes=[("lp", 0)])
        p.op("vector", lambda e: e.tensor_mul(lam_sb[:, 5:6], lam_sb[:, 2:3], lam_sb[:, 3:4]),
             reads=[("lamin", 0)], writes=[("lp", 1)])
        p.op("tensor", lambda e: e.matmul(psum[0][:, 0:1], onesf[:, :], lam_sb[:, 4:5], start=True, stop=True),
             reads=[("onesf", 0), ("lp", 0)], writes=[(PK[0], 0)])
        p.op("tensor", lambda e: e.matmul(psum[0][:, 1:2], onesf[:, :], lam_sb[:, 5:6], start=True, stop=True),
             reads=[("onesf", 0), ("lp", 1)], writes=[(PK[0], 1)])
        p.op("scalar", lambda e: e.activation(out=lam_sb[:, 6:8], in_=psum[0][:, 0:2], func=AF.Exp),
             reads=[(PK[0], 0), (PK[0], 1)], writes=[("lexp", 0)])
        p.op("vector", lambda e: e.tensor_sub(lam_sb[:, 4:5], lam_sb[:, 6:7], lam_sb[:, 7:8]),
             reads=[("lexp", 0)], writes=[("lam", 0)])
        p.op("vector", lambda e: e.tensor_scalar_add(lam_sb[:, 4:5], lam_sb[:, 4:5], LAMBDA_INIT),
             reads=[("lam", 0)], writes=[("lam", 0)])

        def emit_ln(kp, ncols, xsrc_ap, xsrc_keys, dst_write, bufs):
            m_sb, s_sb, t_sb, xb_sb, sq_sb = bufs
            NT5 = ncols // 512 if ncols >= 512 else 1
            W5 = min(512, ncols)
            for kc in range(KC):
                p.op("scalar", lambda e, kc=kc: e.copy(
                    out=xb_sb[:, 0:ncols], in_=xsrc_ap(kc)),
                    reads=xsrc_keys(kc), writes=[("LNxbf", 0)])
                p.op("vector", lambda e: e.tensor_mul(
                    sq_sb[:, 0:ncols], xb_sb[:, 0:ncols], xb_sb[:, 0:ncols]),
                    reads=[("LNxbf", 0)], writes=[("LNsqb", 0)])
                for t5 in range(NT5):
                    sl = slice(t5 * W5, (t5 + 1) * W5)
                    p.op("tensor", lambda e, kc=kc, sl=sl, t5=t5: e.matmul(
                        psum[t5][:, 0:W5], onesb[:, :], xb_sb[:, sl],
                        start=(kc == 0), stop=(kc == KC - 1)),
                        reads=[("LNxbf", 0), ("onesb", 0)], writes=[(PK[t5], 0)])
                    p.op("tensor", lambda e, kc=kc, sl=sl, t5=t5: e.matmul(
                        psum[4 + t5][:, 0:W5], onesb[:, :], sq_sb[:, sl],
                        start=(kc == 0), stop=(kc == KC - 1)),
                        reads=[("LNsqb", 0), ("onesb", 0)], writes=[(PK[4 + t5], 0)])
            for t5 in range(NT5):
                sl = slice(t5 * W5, (t5 + 1) * W5)
                p.op("vector", lambda e, sl=sl, t5=t5: e.tensor_scalar_mul(
                    m_sb[:, sl], psum[t5][:, 0:W5], 1.0 / D),
                    reads=[(PK[t5], 0)], writes=[("LNm", t5)])
                p.op("vector", lambda e, sl=sl, t5=t5: e.tensor_scalar_mul(
                    t_sb[:, sl], psum[4 + t5][:, 0:W5], 1.0 / D),
                    reads=[(PK[4 + t5], 0)], writes=[("LNt", t5)])
            mk = [("LNm", t5) for t5 in range(NT5)]
            tk = [("LNt", t5) for t5 in range(NT5)]
            cs = slice(0, ncols)
            p.op("vector", lambda e: e.tensor_mul(s_sb[:, cs], m_sb[:, cs], m_sb[:, cs]),
                 reads=mk, writes=[("LNs", 0)])
            p.op("vector", lambda e: e.tensor_sub(t_sb[:, cs], t_sb[:, cs], s_sb[:, cs]),
                 reads=tk + [("LNs", 0)], writes=tk)
            p.op("vector", lambda e: e.tensor_scalar_add(t_sb[:, cs], t_sb[:, cs], LN_EPS),
                 reads=tk, writes=tk)
            p.op("scalar", lambda e: e.activation(out=s_sb[:, cs], in_=t_sb[:, cs], func=AF.Sqrt),
                 reads=tk, writes=[("LNs", 0)])
            p.op("vector", lambda e: e.reciprocal(s_sb[:, cs], s_sb[:, cs]),
                 reads=[("LNs", 0)], writes=[("LNs", 0)])
            for kc in range(KC):
                p.op("vector", lambda e, kc=kc: e.tensor_sub(
                    t_sb[:, cs], xsrc_ap(kc), m_sb[:, cs]),
                    reads=xsrc_keys(kc) + mk, writes=[("LNt", 0)])
                dst_write(kc)

        def load_strip_hwdge(oc, wi, src, nk, cast_eng="scalar"):
            wsl, wk = WST[wi]
            ncol = nk * 128
            keys = []
            for half in range((ncol + 1023) // 1024):
                w = min(1024, ncol - half * 1024)
                sl, sk = SLOT[p.stg_rr % len(SLOT)]
                p.stg_rr += 1
                klo, khi = half * 8, half * 8 + w // 128
                p.dma("sync", lambda e, oc=oc, sl=sl, w=w, klo=klo, khi=khi, src=src: e.dma_start(
                    out=stg[:, sl][:, 0:w].rearrange("p (k n) -> p k n", n=128),
                    in_=src[oc, klo:khi].rearrange("k p n -> p k n")),
                    writes=[(sk, 0)], tag="wdma")
                if cast_eng == "scalar":
                    p.op("scalar", lambda e, sl=sl, wsl=wsl, half=half, w=w: e.copy(
                        out=wstrip[:, wsl][:, half * 1024:half * 1024 + w],
                        in_=stg[:, sl][:, 0:w]),
                        reads=[(sk, 0)], writes=[(wk, half)], tag="wcast")
                else:
                    p.op("vector", lambda e, sl=sl, wsl=wsl, half=half, w=w: e.tensor_copy(
                        out=wstrip[:, wsl][:, half * 1024:half * 1024 + w],
                        in_=stg[:, sl][:, 0:w]),
                        reads=[(sk, 0)], writes=[(wk, half)], tag="wcast")
                keys.append((wk, half))
            return keys

        def load_qkv_strip(oc, wi, cast_eng=None, src=None):
            hbm = src if src is not None else wqkvT
            wsl, wk = WST[wi]
            p.dma("gpsimd", lambda e, oc=oc, wsl=wsl, hbm=hbm: e.dma_start(
                out=wstrip[:, wsl][:, 0:KC * 128].rearrange("p (k n) -> p k n", n=128),
                in_=hbm[oc].rearrange("k p n -> p k n")),
                writes=[(wk, 0)], tag="wdma")
            return [(wk, 0)]


        for h in range(NHALF):
            tok0 = h * HTOK
            kp = f"L1h{h}_"
            for kc in range(KC):
                p.dma("sync", lambda e, kc=kc, tok0=tok0: e.dma_start(
                    out=x_half[:, kc * HTOK:(kc + 1) * HTOK],
                    in_=x_fm[kc][:, tok0:tok0 + HTOK]),
                    writes=[("xh", kc)], slot=f"xh{kc}", tag="ldx")

            def dst(kc, tok0=tok0, kp=kp):
                p.op("vector", lambda e, kc=kc: e.tensor_mul(
                    xln[:, kc * S + tok0: kc * S + tok0 + HTOK],
                    tA[:, 0:HTOK], sA[:, 0:HTOK]),
                    reads=[("LNt", 0), ("LNs", 0)],
                    writes=[("xln", (kc, tok0 // HTOK))], tag="nrm")
            emit_ln(kp, HTOK,
                    lambda kc: x_half[:, kc * HTOK:(kc + 1) * HTOK],
                    lambda kc: [("xh", kc)], dst, (mA, sA, tA, xbA, sqA))

        def xlnkey(kc, col):
            return ("xln", (kc, col // HTOK))
        if dbg:
            p.dma("gpsimd", lambda e: e.dma_start(
                out=dbg_h["d_xln"].rearrange("k p s -> p k s"), in_=xln[:].rearrange("p (k s) -> p k s", s=S)),
                reads=[("xln", (kc, h)) for kc in range(KC) for h in range(NHALF)], tag="dbg")

        _pre_strip = {}
        for i in range(2):
            _pre_strip[i] = load_qkv_strip(QC + i, i)

        p.fence_all("endP1", dma=dbg)

        NS4 = S // 512
        for i, oc in enumerate(range(QC, QC + KCo)):
            wi = i % 2
            wkeys = _pre_strip.pop(i) if i in _pre_strip else load_qkv_strip(oc, wi)
            wsl = WST[wi][0]
            for kc in range(KC):
                for t4 in range(NS4):
                    bank = psum[(i % 2) * 4 + t4 % 4]
                    p.op("tensor", lambda e, kc=kc, t4=t4, bank=bank, wsl=wsl: e.matmul(
                        bank[:, 0:512], wstrip[:, wsl][:, kc * 128:(kc + 1) * 128],
                        xln[:, kc * S + t4 * 512: kc * S + (t4 + 1) * 512],
                        start=(kc == 0), stop=(kc == KC - 1)),
                        reads=[xlnkey(kc, t4 * 512)] + wkeys,
                        writes=[(bank.name, 0)], tag="kmm")
            for t4 in range(NS4):
                bank = psum[(i % 2) * 4 + t4 % 4]
                p.op("scalar", lambda e, i=i, oc=oc, t4=t4, bank=bank: e.activation(
                    out=kbuf[:, i * S + t4 * 512: i * S + (t4 + 1) * 512],
                    in_=bank[:, 0:512], func=AF.Identity, bias=bqkv[:, oc:oc + 1]),
                    reads=[(bank.name, 0), ("bqkv", 0)], writes=[("k", (i, t4))], tag="kevac")
        kkeys = [("k", (i, t4)) for i in range(KCo) for t4 in range(NS4)]

        for vi in range(VC):
            oc = QC + KCo + vi
            wi = vi % 2
            wkeys = load_qkv_strip(oc, wi, cast_eng="vector")
            wsl = WST[wi][0]
            vsl, vkey = (slice(0, S), "vfm0") if vi % 2 == 0 else (slice(S, 2 * S), "vfm1")
            for kc in range(KC):
                for t4 in range(NS4):
                    bank = psum[(vi % 2) * 4 + t4 % 4]
                    p.op("tensor", lambda e, kc=kc, t4=t4, bank=bank, wsl=wsl: e.matmul(
                        bank[:, 0:512], wstrip[:, wsl][:, kc * 128:(kc + 1) * 128],
                        xln[:, kc * S + t4 * 512: kc * S + (t4 + 1) * 512],
                        start=(kc == 0), stop=(kc == KC - 1)),
                        reads=[xlnkey(kc, t4 * 512)] + wkeys,
                        writes=[(bank.name, 0)], tag="vmm")
            for t4 in range(NS4):
                bank = psum[(vi % 2) * 4 + t4 % 4]
                p.op("scalar", lambda e, vi=vi, oc=oc, t4=t4, bank=bank, vsl=vsl: e.activation(
                    out=vfm[:, vsl][:, t4 * 512:(t4 + 1) * 512],
                    in_=bank[:, 0:512], func=AF.Identity, bias=bqkv[:, oc:oc + 1]),
                    reads=[(bank.name, 0), ("bqkv", 0)], writes=[(vkey, t4)], tag="vevac")
            kv = vi // 2
            sub = vi % 2
            for sc in range(SB):
                bi = 6 + sc % 2
                bank = psumb[bi]
                p.op("tensor", lambda e, sc=sc, bank=bank, vsl=vsl: e.transpose(
                    bank[:, 0:128], vfm[:, vsl][:, sc * 128:(sc + 1) * 128], idb[:, :]),
                    reads=[(vkey, (sc * 128) // 512), ("idb", 0)],
                    writes=[(PK[bi], 0)], tag="vT")
                p.op("vector", lambda e, sc=sc, kv=kv, sub=sub, bank=bank: e.tensor_copy(
                    out=vbuf[:, sc * VS + kv * 257 + sub * 128: sc * VS + kv * 257 + sub * 128 + 128],
                    in_=bank[:, 0:128]),
                    reads=[(PK[bi], 0)], writes=[("v", (sc, vi))], tag="vTe")
        for sc in range(SB):
            for kv in range(NKV):
                p.op("vector", lambda e, sc=sc, kv=kv: e.memset(
                    vbuf[:, sc * VS + kv * 257 + 256: sc * VS + kv * 257 + 257], 1.0),
                    writes=[("v", (sc, VC + kv))], tag="vones")
        vkeys = [("v", (sc, i)) for sc in range(SB) for i in range(VC + NKV)]

        if dbg:
            p.dma("gpsimd", lambda e: e.dma_start(
                out=dbg_h["d_k"].rearrange("k p s -> p k s"), in_=kbuf[:].rearrange("p (k s) -> p k s", s=S)),
                reads=kkeys, tag="dbg")
            p.dma("gpsimd", lambda e: e.dma_start(out=dbg_h["d_v"], in_=vbuf[:]),
                  reads=vkeys, tag="dbg")

        p.fence_all("endP2a", dma=dbg)

        for oh in range(NOH):
            t0 = oh * OH
            kp = f"LOh{oh}_"
            for kc in range(KC):
                p.dma("sync", lambda e, kc=kc, t0=t0: e.dma_start(
                    out=xo_half[:, kc * OH:(kc + 1) * OH],
                    in_=x_own[kc][:, t0:t0 + OH]),
                    writes=[("xo", kc)], slot=f"xo{kc}", tag="ldxo")

            def dsto(kc, t0=t0, oh=oh, kp=kp):
                p.op("vector", lambda e, kc=kc, t0=t0: e.tensor_mul(
                    xow[:, kc * OT + t0: kc * OT + t0 + OH], tA[:, 0:OH], sA[:, 0:OH]),
                    reads=[("LNt", 0), ("LNs", 0)],
                    writes=[("xow", (kc, oh))], tag="nrmo")
            emit_ln(kp, OH,
                    lambda kc: xo_half[:, kc * OH:(kc + 1) * OH],
                    lambda kc: [("xo", kc)], dsto, (mA, sA, tA, xbA, sqA))
        for oc in range(QC):
            wi = oc % 2
            wkeys = load_strip_hwdge(oc, wi, wqkvT, KC,
                                     cast_eng=("vector" if oc % 2 else "scalar"))
            wsl = WST[wi][0]
            for oh in range(NOH):
                t0 = oh * OH
                bank = psum[(oc % 2) * 2 + oh % 2]
                for kc in range(KC):
                    p.op("tensor", lambda e, kc=kc, t0=t0, bank=bank, wsl=wsl: e.matmul(
                        bank[:, 0:OH], wstrip[:, wsl][:, kc * 128:(kc + 1) * 128],
                        xow[:, kc * OT + t0: kc * OT + t0 + OH],
                        start=(kc == 0), stop=(kc == KC - 1)),
                        reads=[("xow", (kc, oh))] + wkeys,
                        writes=[(bank.name, 0)], tag="qmm")
                p.op("scalar", lambda e, oc=oc, t0=t0, bank=bank: e.activation(
                    out=qbuf[:, oc * OT + t0: oc * OT + t0 + OH],
                    in_=bank[:, 0:OH], func=AF.Identity, bias=bqkv[:, oc:oc + 1]),
                    reads=[(bank.name, 0), ("bqkv", 0)], writes=[("q", (oc, oh, 0))], tag="qevac")
        qkeys = [("q", (oc, oh, 0)) for oc in range(QC) for oh in range(NOH)]
        if dbg:
            p.dma("gpsimd", lambda e: e.dma_start(
                out=dbg_h["d_q"].rearrange("k p s -> p k s"), in_=qbuf[:].rearrange("p (k s) -> p k s", s=OT)),
                reads=qkeys, tag="dbg")

        p.fence_all("endP2b", dma=dbg)

        _pre_wo = {oc: load_strip_hwdge(oc, oc % 2, woT, QC,
                                        cast_eng=("vector" if oc % 2 else "scalar"))
                   for oc in range(2)}

        for h in range(NH):
            kv = h // (NH // NKV) if NKV > 0 else 0
            koff = [(kv * 2 + c) * S for c in range(2)]
            qoffc = [(h * 2 + c) * OT for c in range(2)]
            for jp in range(NJ // 2):
                j0, j1 = 2 * jp, 2 * jp + 1
                ctx1 = 2 * j0 + 2
                ctx = 2 * j1 + 2
                obank = [psum[4 + t] for t in range(4)]

                def emit_pv(kc):
                    jlist = [j0, j1] if kc < ctx1 else [j1]
                    pp = kc % 3
                    for j in jlist:
                        for c in range(2):
                            ob = obank[2 * (j - j0) + c]
                            poff = pp * 512 + c * 256 + (j - j0 if kc < ctx1 else 0) * 128
                            p.op("tensor", lambda e, kc=kc, j=j, c=c, ob=ob, poff=poff, kv=kv: e.matmul(
                                ob[:, 0:257], psb[:, poff:poff + 128],
                                vbuf[:, kc * VS + kv * 257: kc * VS + (kv + 1) * 257],
                                start=(kc == 0), stop=(kc == 2 * j + 1)),
                                reads=[("P", (pp, c))] + vkeys,
                                writes=[(ob.name, 0)], tag="pv")

                for kc in range(ctx):
                    qw = 256 if kc < ctx1 else 128
                    qo = (j0 if kc < ctx1 else j1) * 128
                    sbank = psum[kc % 3]
                    pp = kc % 3
                    for c in range(2):
                        p.op("tensor", lambda e, kc=kc, c=c, qw=qw, qo=qo, sbank=sbank,
                             koff=koff, qoffc=qoffc: e.matmul(
                            sbank[:, c * 256: c * 256 + qw],
                            kbuf[:, koff[c] + kc * 128: koff[c] + (kc + 1) * 128],
                            qbuf[:, qoffc[c] + qo: qoffc[c] + qo + qw],
                            start=True, stop=True),
                            reads=kkeys + qkeys,
                            writes=[(sbank.name, 0)], tag="smm")
                    for c in range(2):
                        p.op("scalar", lambda e, kc=kc, c=c, qw=qw, pp=pp, sbank=sbank: e.activation(
                            out=psb[:, pp * 512 + c * 256: pp * 512 + c * 256 + qw],
                            in_=sbank[:, c * 256: c * 256 + qw],
                            func=AF.Exp, bias=consts[:, 0:1], scale=SCALE),
                            reads=[(sbank.name, 0), ("c0", 0)],
                            writes=[("P", (pp, c))], tag="exp")
                    for j in ([j0, j1] if kc < ctx1 else [j1]):
                        if kc >= 2 * j:
                            moff = (j * 2 + (kc - 2 * j)) * 128
                            poffj = pp * 512 + (j - j0 if kc < ctx1 else 0) * 128
                            for c in range(2):
                                p.op("vector", lambda e, c=c, poffj=poffj, moff=moff: e.tensor_mul(
                                    psb[:, poffj + c * 256: poffj + c * 256 + 128],
                                    psb[:, poffj + c * 256: poffj + c * 256 + 128],
                                    mask_sb[:, moff:moff + 128]),
                                    reads=[("P", (pp, c)), ("masks", 0)],
                                    writes=[("P", (pp, c))], tag="mask")
                    if kc > 0:
                        emit_pv(kc - 1)
                emit_pv(ctx - 1)
                for j in (j0, j1):
                    ob1 = obank[2 * (j - j0)]
                    ob2 = obank[2 * (j - j0) + 1]
                    rp = ((h * (NJ // 2) + jp) * 2 + (j - j0)) % 4
                    ro = rp * 4
                    oo = rp * 512
                    p.op("vector", lambda e, ob1=ob1, ro=ro: e.reciprocal(rsc[:, ro:ro + 1], ob1[:, 256:257]),
                         reads=[(ob1.name, 0)], writes=[("r1", rp)], tag="r1")
                    p.op("vector", lambda e, ob2=ob2, ro=ro: e.reciprocal(rsc[:, ro + 1:ro + 2], ob2[:, 256:257]),
                         reads=[(ob2.name, 0)], writes=[("r2", rp)], tag="r2")
                    p.op("vector", lambda e, ro=ro: e.tensor_mul(rsc[:, ro + 2:ro + 3], rsc[:, ro + 1:ro + 2], lam_sb[:, 4:5]),
                         reads=[("r2", rp), ("lam", 0)], writes=[("r2l", rp)], tag="r2l")
                    p.op("vector", lambda e, ob1=ob1, ro=ro, oo=oo: e.tensor_mul(
                        osc[:, oo:oo + 256], ob1[:, 0:256], rsc[:, ro:ro + 1].to_broadcast([128, 256])),
                        reads=[(ob1.name, 0), ("r1", rp)],
                        writes=[("t1", rp)], tag="t1")
                    p.op("vector", lambda e, ob2=ob2, ro=ro, oo=oo: e.tensor_mul(
                        osc[:, oo + 256:oo + 512], ob2[:, 0:256], rsc[:, ro + 2:ro + 3].to_broadcast([128, 256])),
                        reads=[(ob2.name, 0), ("r2l", rp)],
                        writes=[("t2", rp)], tag="t2")
                    p.op("vector", lambda e, j=j, h=h, oo=oo, rp=rp: e.tensor_sub(
                        otm[:, j * QD + h * 256: j * QD + h * 256 + 256],
                        osc[:, oo:oo + 256], osc[:, oo + 256:oo + 512]),
                        reads=[("t1", rp), ("t2", rp)], writes=[("otm", (j, h))], tag="ocomb")
        otmkeys = [("otm", (j, h)) for j in range(NJ) for h in range(NH)]

        p.fence_all("endP3", dma=dbg)

        for j in range(NJ):
            p.op("vector", lambda e, j=j: e.tensor_mul(
                osc[:, 0:QD], otm[:, j * QD:(j + 1) * QD], otm[:, j * QD:(j + 1) * QD]),
                reads=[("otm", (j, h)) for h in range(NH)], writes=[("osc4", 0)], tag="sq")
            p.op("vector", lambda e, j=j: e.tensor_reduce(
                out=ms_sb[:, j * NH:(j + 1) * NH],
                in_=osc[:, 0:QD].rearrange("p (h n) -> p h n", n=256),
                axis=AX.X, op=ALU.add),
                reads=[("osc4", 0)], writes=[("ms", j)], tag="msum")
        mskeys = [("ms", j) for j in range(NJ)]
        p.op("vector", lambda e: e.tensor_scalar(
            out=ms_sb[:, :], in0=ms_sb[:, :], scalar1=1.0 / 256, scalar2=SUBLN_EPS,
            op0=ALU.mult, op1=ALU.add),
            reads=mskeys, writes=mskeys, tag="msc")
        p.op("scalar", lambda e: e.activation(out=ms_sb[:, :], in_=ms_sb[:, :], func=AF.Sqrt),
             reads=mskeys, writes=mskeys, tag="msqrt")
        p.op("vector", lambda e: e.reciprocal(ms_sb[:, :], ms_sb[:, :]),
             reads=mskeys, writes=mskeys, tag="mrcp")
        for j in range(NJ):
            p.op("vector", lambda e, j=j: e.tensor_mul(
                osc[:, 0:QD].rearrange("p (h n) -> p h n", n=256),
                otm[:, j * QD:(j + 1) * QD].rearrange("p (h n) -> p h n", n=256),
                weff[:, :].rearrange("p (o n) -> p o n", o=1).to_broadcast([128, NH, 256])),
                reads=[("otm", (j, h)) for h in range(NH)] + [("weff", 0)] + mskeys,
                writes=[("osc4", 0)], tag="wmul")
            p.op("vector", lambda e, j=j: e.tensor_mul(
                otm[:, j * QD:(j + 1) * QD].rearrange("p (h n) -> p h n", n=256),
                osc[:, 0:QD].rearrange("p (h n) -> p h n", n=256),
                ms_sb[:, j * NH:(j + 1) * NH].rearrange("p (h o) -> p h o", o=1)
                .to_broadcast([128, NH, 256])),
                reads=[("osc4", 0)] + mskeys,
                writes=[("otm", (j, h)) for h in range(NH)], tag="rmul")
            for odc in range(QC):
                bi = odc % 4
                bank = psumb[bi]
                p.op("tensor", lambda e, j=j, odc=odc, bank=bank: e.transpose(
                    bank[:, 0:128], otm[:, j * QD + odc * 128: j * QD + (odc + 1) * 128], idb[:, :]),
                    reads=[("otm", (j, odc // 2)), ("idb", 0)],
                    writes=[(PK[bi], 0)], tag="oT")
                p.op("scalar", lambda e, j=j, odc=odc, bank=bank: e.copy(
                    out=O_fm[:, odc * OT + j * 128: odc * OT + (j + 1) * 128],
                    in_=bank[:, 0:128]),
                    reads=[(PK[bi], 0)], writes=[("ofm", (odc, j))], tag="oTe")
        if dbg:
            p.dma("gpsimd", lambda e: e.dma_start(out=dbg_h["d_otm"], in_=otm[:]),
                  reads=otmkeys, tag="dbg")
        ofmkeys = [("ofm", (odc, j)) for odc in range(QC) for j in range(NJ)]

        p.fence_all("endP5", dma=dbg)

        def load_w13(eidx):
            keys = {"w1": [], "w3": []}
            for nm, hbm, dst in (("w1", w1T_h, w1b), ("w3", w3T_h, w3b)):
                src = (hbm[eidx] if eidx < E else (sw1T_h if nm == "w1" else sw3T_h))
                hh = HID if eidx < E else SHID
                nq = 4 if KC % 4 == 0 else 2
                for q4 in range(nq):
                    klo, khi = q4 * (KC // nq), (q4 + 1) * (KC // nq)
                    p.dma("gpsimd", lambda e, src=src, dst=dst, klo=klo, khi=khi, hh=hh: e.dma_start(
                        out=dst[:, klo * HID: klo * HID + (khi - klo) * HID]
                            .rearrange("p (k n) -> p k n", n=HID)[:, :, 0:hh],
                        in_=src[klo:khi].rearrange("k p n -> p k n")),
                        writes=[(nm, q4)], tag="w13dma")
                    keys[nm].append((nm, q4))
            return keys

        wkeys13_pre = load_w13(0)

        sl, sk = SLOT[0]
        p.dma("scalar", lambda e, sl=sl: e.dma_start(
            out=stg[:, sl][:, 0:KC * 8].rearrange("p (k n) -> p k n", n=8),
            in_=gwT_h.rearrange("k p n -> p k n")),
            writes=[(sk, 0)], tag="gdma")
        p.op("vector", lambda e, sl=sl: e.tensor_copy(
            out=gwb[:, 0:KC * 8], in_=stg[:, sl][:, 0:KC * 8]),
            reads=[(sk, 0)], writes=[("gwb", 0)], tag="gcast")

        for oc in range(KC):
            wi = oc % 2
            wkeys = (_pre_wo.pop(oc) if oc in _pre_wo else
                     load_strip_hwdge(oc, wi, woT, QC,
                                      cast_eng=("vector" if oc % 2 else "scalar")))
            wsl = WST[wi][0]
            for kc2 in range(QC):
                for tc in range(TC):
                    bank = psum[4 + (oc % 2) * 2 + tc % 2]
                    jlo = (tc * TCW) // 128
                    jhi = min(NJ, ((tc + 1) * TCW) // 128)
                    p.op("tensor", lambda e, kc2=kc2, tc=tc, bank=bank, wsl=wsl: e.matmul(
                        bank[:, 0:TCW], wstrip[:, wsl][:, kc2 * 128:(kc2 + 1) * 128],
                        O_fm[:, kc2 * OT + tc * TCW: kc2 * OT + (tc + 1) * TCW],
                        start=(kc2 == 0), stop=(kc2 == QC - 1)),
                        reads=[("ofm", (kc2, j)) for j in range(jlo, jhi)] + wkeys,
                        writes=[(bank.name, 0)], tag="omm")
            for tc in range(TC):
                bank = psum[4 + (oc % 2) * 2 + tc % 2]
                xi = (oc * TC + tc) % 2
                p.dma("sync", lambda e, oc=oc, tc=tc, xi=xi: e.dma_start(
                    out=xstg[:, xi * TCW:(xi + 1) * TCW],
                    in_=x_own[oc][:, tc * TCW:(tc + 1) * TCW]),
                    writes=[("xstg", xi)], slot=f"xres{xi}", tag="ldres")
                p.op("scalar", lambda e, oc=oc, tc=tc, bank=bank: e.activation(
                    out=X[:, oc * OT + tc * TCW: oc * OT + (tc + 1) * TCW],
                    in_=bank[:, 0:TCW], func=AF.Identity, bias=bo_sb[:, oc:oc + 1]),
                    reads=[(bank.name, 0), ("bo", 0)], writes=[("X", (oc, tc))], tag="oevac")
                p.op("vector", lambda e, oc=oc, tc=tc, xi=xi: e.tensor_add(
                    X[:, oc * OT + tc * TCW: oc * OT + (tc + 1) * TCW],
                    X[:, oc * OT + tc * TCW: oc * OT + (tc + 1) * TCW],
                    xstg[:, xi * TCW:(xi + 1) * TCW]),
                    reads=[("xstg", xi), ("X", (oc, tc))], writes=[("X", (oc, tc))], tag="resadd")
        Xkeys = [("X", (oc, tc)) for oc in range(KC) for tc in range(TC)]
        if dbg:
            p.dma("gpsimd", lambda e: e.dma_start(
                out=dbg_h["d_Xmid"].rearrange("k p s -> p k s"), in_=X[:].rearrange("p (k s) -> p k s", s=OT)),
                reads=Xkeys, tag="dbg")

        if dbg:
            p.fence_all("endP6", dma=True)

        kp = "L2_"
        assert OT <= HTOK

        def dst2(kc, kp=kp):
            p.op("vector", lambda e, kc=kc: e.tensor_mul(
                xln2[:, kc * OT:(kc + 1) * OT], tB[:, 0:OT], sB[:, 0:OT]),
                reads=[("LNt", 0), ("LNs", 0)], writes=[("xln2", kc)], tag="nrm2")
        emit_ln(kp, OT,
                lambda kc: X[:, kc * OT:(kc + 1) * OT],
                lambda kc: [("X", (kc, tc)) for tc in range(TC)], dst2, (mB, sB, tB, xbB, sqB))
        if dbg:
            p.fence_all("endP7", dma=True)
        x2keys = [("xln2", kc) for kc in range(KC)]
        if dbg:
            p.dma("gpsimd", lambda e: e.dma_start(
                out=dbg_h["d_xln2"].rearrange("k p s -> p k s"), in_=xln2[:].rearrange("p (k s) -> p k s", s=OT)),
                reads=x2keys, tag="dbg")

        scp_sl = SLOT[1][0]
        for tcg in range(TC):
            bank = psum[tcg % 2]
            for kc in range(KC):
                p.op("tensor", lambda e, kc=kc, tcg=tcg, bank=bank: e.matmul(
                    bank[0:8, 0:TCW], gwb[:, kc * 8:(kc + 1) * 8],
                    xln2[:, kc * OT + tcg * TCW: kc * OT + (tcg + 1) * TCW],
                    start=(kc == 0), stop=(kc == KC - 1)),
                    reads=[("xln2", kc), ("gwb", 0)], writes=[(bank.name, 0)], tag="gmm")
            p.op("scalar", lambda e, tcg=tcg, bank=bank: e.activation(
                out=stg[:, scp_sl][0:8, tcg * TCW:(tcg + 1) * TCW],
                in_=bank[0:8, 0:TCW], func=AF.Sigmoid),
                reads=[(bank.name, 0)], writes=[("stg1", 0)], tag="sig")
        p.op("vector", lambda e: e.memset(b_fm[:, :], 0.0), writes=[("bfm", "z")], tag="bz")
        for tb in range(OT // 128):
            sb8 = (tb % 2) * 48
            bank = psum[2 + tb % 2]
            p.op("tensor", lambda e, tb=tb, bank=bank, sb8=sb8: e.transpose(
                bank[:, 0:8], stg[:, scp_sl][0:8, tb * 128:(tb + 1) * 128], idf[0:8, 0:8]),
                reads=[("stg1", 0), ("idf", 0)], writes=[(bank.name, 0)], tag="gT")
            p.op("vector", lambda e, tb=tb, bank=bank, sb8=sb8: e.tensor_copy(
                out=scr8[:, sb8 + 0:sb8 + 8], in_=bank[:, 0:8]),
                reads=[(bank.name, 0)], writes=[("stm", tb % 2)], tag="gTe")
            p.op("vector", lambda e, sb8=sb8, tb=tb: e.max(
                out=scr8[:, sb8 + 8:sb8 + 16], in_=scr8[:, sb8 + 0:sb8 + 8]),
                reads=[("stm", tb % 2)], writes=[("mx8", tb % 2)], tag="max8")
            p.op("vector", lambda e, sb8=sb8, tb=tb: e.tensor_tensor(
                scr8[:, sb8 + 16:sb8 + 24], scr8[:, sb8 + 0:sb8 + 8],
                scr8[:, sb8 + 9:sb8 + 10].to_broadcast([128, 8]), op=ALU.is_ge),
                reads=[("stm", tb % 2), ("mx8", tb % 2)], writes=[("msk", tb % 2)], tag="gmask")
            p.op("vector", lambda e, sb8=sb8, tb=tb: e.tensor_mul(
                scr8[:, sb8 + 24:sb8 + 32], scr8[:, sb8 + 16:sb8 + 24], scr8[:, sb8 + 0:sb8 + 8]),
                reads=[("msk", tb % 2), ("stm", tb % 2)], writes=[("btm", tb % 2)], tag="btm")
            bank2 = psum[4 + tb % 2]
            p.op("tensor", lambda e, tb=tb, bank2=bank2, sb8=sb8: e.transpose(
                bank2[0:8, 0:128], scr8[:, sb8 + 24:sb8 + 32], idf[:, :]),
                reads=[("btm", tb % 2), ("idf", 0)], writes=[(bank2.name, 0)], tag="bT")
            p.op("vector", lambda e, tb=tb, bank2=bank2: e.tensor_copy(
                out=b_fm[0:8, tb * 128:(tb + 1) * 128], in_=bank2[0:8, 0:128]),
                reads=[(bank2.name, 0), ("bfm", "z")], writes=[("bfm", tb)], tag="bTe")
        bfmkeys = [("bfm", tb) for tb in range(OT // 128)] + [("bfm", "z")]
        if dbg:
            p.dma("gpsimd", lambda e: e.dma_start(out=dbg_h["d_bfm"], in_=b_fm[:, :]),
                  reads=bfmkeys, tag="dbg")
            p.fence_all("endP8", dma=True)


        wkeys13 = wkeys13_pre
        for eidx in range(E + 1):
            shared = eidx == E
            hc_n = SHC if shared else HC
            hh = SHID if shared else HID
            if not shared:
                p.op("vector", lambda e, eidx=eidx: e.tensor_mul(
                    selS[:, :], onesf[:, :], id8[:, eidx:eidx + 1].to_broadcast([128, 128])),
                    reads=[("onesf", 0), ("id8", 0)], writes=[("sel", 0)], tag="sel")
                for bs in range(TC):
                    bank = psum[6]
                    p.op("tensor", lambda e, bs=bs, bank=bank: e.matmul(
                        bank[:, 0:TCW], selS[:, :], b_fm[:, bs * TCW:(bs + 1) * TCW],
                        start=True, stop=True),
                        reads=[("sel", 0)] + bfmkeys, writes=[(bank.name, 0)], tag="bmm")
                    p.op("vector", lambda e, bs=bs, bank=bank: e.tensor_copy(
                        out=B_sb[:, bs * TCW:(bs + 1) * TCW], in_=bank[:, 0:TCW]),
                        reads=[(bank.name, 0)], writes=[("B", bs)], tag="bev")
            Bkeys = [("B", bs) for bs in range(TC)]
            groups = [(g,) for g in range(hc_n)]
            for gi, g in enumerate(groups):
                ub = {}
                for i2, hc in enumerate(g):
                    for tc in range(TC):
                        ub[(hc, tc)] = psum[((gi % 2) * 4 + i2 * TC + tc) % 8]
                for kc in range(KC):
                    for hc in g:
                        for tc in range(TC):
                            b = ub[(hc, tc)]
                            p.op("tensor", lambda e, kc=kc, hc=hc, tc=tc, b=b: e.matmul(
                                b[:, 0:TCW], w1b[:, kc * HID + hc * 128: kc * HID + (hc + 1) * 128],
                                xln2[:, kc * OT + tc * TCW: kc * OT + (tc + 1) * TCW],
                                start=(kc == 0), stop=(kc == KC - 1)),
                                reads=[("xln2", kc), wkeys13["w1"][kc * len(wkeys13["w1"]) // KC]],
                                writes=[(b.name, 0)], tag="U")
                for i2, hc in enumerate(g):
                    for tc in range(TC):
                        b = ub[(hc, tc)]
                        so = tc * TCW
                        ei = (gi * TC + tc) % 2
                        if shared:
                            p.op("scalar", lambda e, b=b, so=so: e.activation(
                                out=sUg[:, so:so + TCW], in_=b[:, 0:TCW], func=AF.Silu),
                                reads=[(b.name, 0)], writes=[("sU", so)], tag="silu")
                        else:
                            p.op("vector", lambda e, b=b, tc=tc, ei=ei: e.tensor_mul(
                                escr[:, ei * TCW:(ei + 1) * TCW], b[:, 0:TCW],
                                B_sb[:, tc * TCW:(tc + 1) * TCW]),
                                reads=[(b.name, 0)] + Bkeys,
                                writes=[("escr", ei)], tag="ub")
                            p.op("scalar", lambda e, so=so, ei=ei: e.activation(
                                out=sUg[:, so:so + TCW], in_=escr[:, ei * TCW:(ei + 1) * TCW],
                                func=AF.Silu),
                                reads=[("escr", ei)], writes=[("sU", so)], tag="silu")
                vb = {}
                for i2, hc in enumerate(g):
                    for tc in range(TC):
                        vb[(hc, tc)] = psum[((gi % 2) * 4 + 4 + i2 * TC + tc) % 8]
                for kc in range(KC):
                    for hc in g:
                        for tc in range(TC):
                            b = vb[(hc, tc)]
                            p.op("tensor", lambda e, kc=kc, hc=hc, tc=tc, b=b: e.matmul(
                                b[:, 0:TCW], w3b[:, kc * HID + hc * 128: kc * HID + (hc + 1) * 128],
                                xln2[:, kc * OT + tc * TCW: kc * OT + (tc + 1) * TCW],
                                start=(kc == 0), stop=(kc == KC - 1)),
                                reads=[("xln2", kc), wkeys13["w3"][kc * len(wkeys13["w3"]) // KC]],
                                writes=[(b.name, 0)], tag="V")
                for i2, hc in enumerate(g):
                    for tc in range(TC):
                        b = vb[(hc, tc)]
                        so = tc * TCW
                        ei = (gi * TC + tc + 1) % 2
                        if shared:
                            p.op("vector", lambda e, b=b, so=so, hc=hc, tc=tc: e.tensor_mul(
                                Hb[:, hc * OT + tc * TCW: hc * OT + (tc + 1) * TCW],
                                sUg[:, so:so + TCW], b[:, 0:TCW]),
                                reads=[("sU", so), (b.name, 0)],
                                writes=[("H", (hc, tc))], tag="H")
                        else:
                            p.op("vector", lambda e, b=b, tc=tc, ei=ei: e.tensor_mul(
                                escr[:, ei * TCW:(ei + 1) * TCW], b[:, 0:TCW],
                                B_sb[:, tc * TCW:(tc + 1) * TCW]),
                                reads=[(b.name, 0)] + Bkeys,
                                writes=[("escr", ei)], tag="vb")
                            p.op("vector", lambda e, so=so, hc=hc, tc=tc, ei=ei: e.tensor_mul(
                                Hb[:, hc * OT + tc * TCW: hc * OT + (tc + 1) * TCW],
                                sUg[:, so:so + TCW], escr[:, ei * TCW:(ei + 1) * TCW]),
                                reads=[("sU", so), ("escr", ei)],
                                writes=[("H", (hc, tc))], tag="H")
            if eidx < E:
                wkeys13 = load_w13(eidx + 1)
            w2src = sw2T_h if shared else w2T_h[eidx]
            for oc in range(KC):
                wi2 = oc % 3
                sl, sk = SLOT[p.stg_rr % len(SLOT)]
                p.stg_rr += 1
                hhn = (hc_n + 1) // 2
                for hv in range(2):
                    h0, h1 = hv * hhn, min(hc_n, (hv + 1) * hhn)
                    if h0 >= h1:
                        continue
                    p.dma("sync", lambda e, oc=oc, sl=sl, w2src=w2src, h0=h0, h1=h1: e.dma_start(
                        out=stg[:, sl][:, h0 * 128:h1 * 128].rearrange("p (k n) -> p k n", n=128),
                        in_=w2src[oc][h0:h1].rearrange("k p n -> p k n")),
                        writes=[(sk, hv)], slot=f"{sk}h{hv}", tag="w2dma")
                p.op("scalar", lambda e, wi2=wi2, sl=sl, hc_n=hc_n: e.copy(
                    out=w2b[:, wi2 * HC * 128: wi2 * HC * 128 + hc_n * 128],
                    in_=stg[:, sl][:, 0:hc_n * 128]),
                    reads=[(sk, 0), (sk, 1)], writes=[("w2b", wi2)], tag="w2cast")
                for tc in range(TC):
                    bank = psum[(oc * TC + tc) % 4]
                    for hc in range(hc_n):
                        p.op("tensor", lambda e, oc=oc, tc=tc, hc=hc, bank=bank, wi2=wi2: e.matmul(
                            bank[:, 0:TCW], w2b[:, wi2 * HC * 128 + hc * 128: wi2 * HC * 128 + (hc + 1) * 128],
                            Hb[:, hc * OT + tc * TCW: hc * OT + (tc + 1) * TCW],
                            start=(hc == 0), stop=(hc == hc_n - 1)),
                            reads=[("H", (hc, tc)), ("w2b", wi2)],
                            writes=[(bank.name, 0)], tag="W2")
                    p.op("vector", lambda e, oc=oc, tc=tc, bank=bank: e.tensor_add(
                        X[:, oc * OT + tc * TCW: oc * OT + (tc + 1) * TCW],
                        X[:, oc * OT + tc * TCW: oc * OT + (tc + 1) * TCW], bank[:, 0:TCW]),
                        reads=[(bank.name, 0), ("X", (oc, tc))],
                        writes=[("X", (oc, tc))], tag="Xacc")

        for kc in range(KC):
            p.dma("sync", lambda e, kc=kc: e.dma_start(
                out=out_h[kc], in_=X[:, kc * OT:(kc + 1) * OT]),
                reads=[("X", (kc, tc)) for tc in range(TC)], slot="out", tag="store")
        p.wait_all("sync")
        p.build(es)
    return nc


def pack_inputs(cfg, inputs):
    D, S = cfg["D"], cfg["S"]
    NH, NKV, E = cfg["NH"], cfg["NKV"], cfg["E"]
    HID, SHID = cfg["HID"], cfg["SHID"]
    KC, SB = D // 128, S // 128
    NJ = SB // 2
    OT = NJ * 128
    QD, KD, VD = NH * 256, NKV * 256, NKV * 256
    QC, KCo, VC = QD // 128, KD // 128, VD // 128
    NOC = QC + KCo + VC
    HC, SHC = HID // 128, SHID // 128

    x = np.ascontiguousarray(inputs["L0_attn_hidden_states"], dtype=np.float32)
    wqkv = np.asarray(inputs["L0_attn_Wqkv_w"], dtype=np.float32)
    bqkv = np.asarray(inputs["L0_attn_Wqkv_b"], dtype=np.float32)
    wo = np.asarray(inputs["L0_attn_out_w"], dtype=np.float32)
    bo = np.asarray(inputs["L0_attn_out_b"], dtype=np.float32)
    lq1 = np.asarray(inputs["L0_attn_lambda_q1"], dtype=np.float32)
    lk1 = np.asarray(inputs["L0_attn_lambda_k1"], dtype=np.float32)
    lq2 = np.asarray(inputs["L0_attn_lambda_q2"], dtype=np.float32)
    lk2 = np.asarray(inputs["L0_attn_lambda_k2"], dtype=np.float32)
    subw = np.asarray(inputs["L0_attn_subln_w"], dtype=np.float32)
    gw = np.asarray(inputs["L0_ffn_gate_w"], dtype=np.float32)
    w1 = np.asarray(inputs["L0_ffn_w1"], dtype=np.float32)
    w2 = np.asarray(inputs["L0_ffn_w2"], dtype=np.float32)
    w3 = np.asarray(inputs["L0_ffn_w3"], dtype=np.float32)
    sw1 = np.asarray(inputs["L0_ffn_sw1"], dtype=np.float32)
    sw2 = np.asarray(inputs["L0_ffn_sw2"], dtype=np.float32)
    sw3 = np.asarray(inputs["L0_ffn_sw3"], dtype=np.float32)

    import ml_dtypes
    C = np.ascontiguousarray

    wqkvT = C(wqkv.reshape(NOC, 128, KC, 128).transpose(0, 2, 3, 1))
    bqkv_p = C(bqkv.reshape(NOC, 128).T)
    woT = C(wo.reshape(KC, 128, QC, 128).transpose(0, 2, 3, 1))
    bo_p = C(bo.reshape(KC, 128).T)
    lam_p = C(np.stack([lq1, lk1, lq2, lk2], axis=1))
    subw_p = C(np.broadcast_to(subw[None, :], (128, 256)))
    gwT = C(gw.T.reshape(KC, 128, 8))
    w1T = C(w1.transpose(0, 2, 1).reshape(E, KC, 128, HID))
    w3T = C(w3.transpose(0, 2, 1).reshape(E, KC, 128, HID))
    w2T = C(w2.reshape(E, KC, 128, HC, 128).transpose(0, 1, 3, 4, 2))
    sw1T = C(sw1.T.reshape(KC, 128, SHID))
    sw3T = C(sw3.T.reshape(KC, 128, SHID))
    sw2T = C(sw2.reshape(KC, 128, SHC, 128).transpose(0, 2, 3, 1))
    idf = np.eye(128, dtype=np.float32)
    idb = np.eye(128, dtype=ml_dtypes.bfloat16)

    shared = dict(wqkvT=wqkvT, bqkv=bqkv_p, woT=woT, bo=bo_p, lam_in=lam_p,
                  subw_bc=subw_p, ident_f=idf, ident_b=idb, gwT=gwT,
                  w1T=w1T, w3T=w3T, w2T=w2T, sw1T=sw1T, sw3T=sw3T, sw2T=sw2T)

    in_maps = []
    own_tok = []
    for c in range(cfg["CORES"]):
        b, par = c // 2, c % 2
        qbs = own_qbs(par, SB)
        toks = np.concatenate([np.arange(qb * 128, (qb + 1) * 128) for qb in qbs])
        own_tok.append((b, toks))
        xb = x[b]
        x_fm = C(xb.T.reshape(KC, 128, S))
        x_ow = C(xb[toks].T.reshape(KC, 128, OT))
        mk = np.zeros((NJ, 2, 128, 128), dtype=ml_dtypes.bfloat16)
        for j, qb in enumerate(qbs):
            for t in range(2):
                kcb = 2 * j + t
                if kcb < qb:
                    mk[j, t] = 1
                elif kcb == qb:
                    ki = np.arange(128)[:, None]
                    qi = np.arange(128)[None, :]
                    mk[j, t] = (ki <= qi).astype(ml_dtypes.bfloat16)
        im = dict(shared)
        im.update(x_fm=x_fm, x_own=x_ow, masks=mk)
        in_maps.append(im)
    return in_maps, own_tok


_BUILD_CACHE = {}


def kernel(**inputs):
    cfg = make_cfg(mini=False)
    key = "full"
    if key not in _BUILD_CACHE:
        _BUILD_CACHE[key] = build_program(cfg)
    nc = _BUILD_CACHE[key]
    in_maps, own_tok = pack_inputs(cfg, inputs)
    res = run_bass_kernel_spmd(nc, in_maps, core_ids=list(range(cfg["CORES"])))
    D, S = cfg["D"], cfg["S"]
    out = np.empty((cfg["B"], S, D), dtype=np.float32)
    for c in range(cfg["CORES"]):
        b, toks = own_tok[c]
        shard = np.asarray(res.results[c]["out_fm"])
        out[b, toks, :] = shard.transpose(2, 0, 1).reshape(len(toks), D)
    return out


# version 11
# speedup vs baseline: 1.0594x; 1.0594x over previous
import math
import numpy as np
from contextlib import ExitStack
from collections import defaultdict

import concourse.bass as bass
import concourse.mybir as mybir
from concourse.bass_utils import run_bass_kernel_spmd

F32 = mybir.dt.float32
BF16 = mybir.dt.bfloat16
AF = mybir.ActivationFunctionType
ALU = mybir.AluOpType
AX = mybir.AxisListType

ENGINES = ("tensor", "vector", "scalar", "gpsimd", "sync")

LAMBDA_INIT = 0.8 - 0.6 * math.exp(-0.3 * 1)
EXP_BIAS = 24.0
LN_EPS = 1e-5
SUBLN_EPS = 1e-5


class Prog:
    def __init__(self, nc):
        self.nc = nc
        self.actions = {e: [] for e in ENGINES}
        self.count = {e: 0 for e in ENGINES}
        self.dma_slot_count = {}
        self.marked = {e: set() for e in ENGINES}
        self.writer = {}
        self.readers = defaultdict(list)
        self.water = {e: defaultdict(int) for e in ENGINES}

    def _need(self, deps, tok):
        kind, eng, val = tok
        key = (kind, eng)
        if val > deps.get(key, 0):
            deps[key] = val

    def _gather(self, engine, reads, writes):
        deps = {}
        for b in reads:
            w = self.writer.get(b)
            if w is not None:
                self._need(deps, w)
        for b in writes:
            w = self.writer.get(b)
            if w is not None:
                self._need(deps, w)
            for r in self.readers.get(b, ()):
                self._need(deps, r)
        waits = []
        for (kind, eng), val in deps.items():
            if kind == "eng" and eng == engine == "tensor":
                continue
            if self.water[engine][(kind, eng)] >= val:
                continue
            if kind == "eng" and eng == engine:
                assert val <= self.count[engine], "self-wait on future op"
            waits.append(((kind, eng), val))
            self.water[engine][(kind, eng)] = val
            if kind == "eng":
                self.marked[eng].add(val)
        return waits

    def op(self, engine, fn, reads=(), writes=(), tag=""):
        waits = self._gather(engine, reads, writes)
        self.count[engine] += 1
        tok = ("eng", engine, self.count[engine])
        for b in writes:
            self.writer[b] = tok
            self.readers[b] = []
        for b in reads:
            self.readers[b].append(tok)
        self.actions[engine].append(dict(waits=waits, fn=fn, kind="eng",
                                         tag=tag, seq=self.count[engine]))
        return tok

    def dma(self, engine, fn, reads=(), writes=(), tag="", slot=None):
        if slot is None:
            slot = writes[0] if writes else "out"
        key = (engine, slot)
        waits = self._gather(engine, reads, writes)
        prev = self.dma_slot_count.get(key, 0)
        if prev > 0 and self.water[engine][("dma", key)] < prev:
            waits.append((("dma", key), prev))
            self.water[engine][("dma", key)] = prev
        self.dma_slot_count[key] = prev + 16
        tok = ("dma", key, self.dma_slot_count[key])
        for b in writes:
            self.writer[b] = tok
            self.readers[b] = []
        for b in reads:
            self.readers[b].append(tok)
        self.actions[engine].append(dict(waits=waits, fn=fn, kind="dma",
                                         tag=tag, seq=self.dma_slot_count[key],
                                         slot=key))
        return tok

    def fence_all(self, tag="fence", dma=False):
        for engine in ENGINES:
            waits = []
            for e in ENGINES:
                if self.count[e] > 0 \
                        and self.water[engine][("eng", e)] < self.count[e]:
                    waits.append((("eng", e), self.count[e]))
                    self.water[engine][("eng", e)] = self.count[e]
                    self.marked[e].add(self.count[e])
            if dma:
                for k, v in self.dma_slot_count.items():
                    if self.water[engine][("dma", k)] < v:
                        waits.append((("dma", k), v))
                        self.water[engine][("dma", k)] = v
            if waits:
                self.actions[engine].append(dict(
                    waits=waits, fn=lambda e: None, kind="none", tag=tag, seq=-1))

    def wait_all(self, engine, tag="final"):
        waits = []
        for e in ENGINES:
            if e != engine and self.count[e] > 0 \
                    and self.water[engine][("eng", e)] < self.count[e]:
                waits.append((("eng", e), self.count[e]))
                self.water[engine][("eng", e)] = self.count[e]
                self.marked[e].add(self.count[e])
        for k, v in self.dma_slot_count.items():
            if self.water[engine][("dma", k)] < v:
                waits.append((("dma", k), v))
                self.water[engine][("dma", k)] = v
        self.actions[engine].append(dict(waits=waits, fn=lambda e: None,
                                         kind="none", tag=tag, seq=-1))

    def build(self, es):
        nc = self.nc
        inc_val = {}
        for e in ENGINES:
            marked = sorted(self.marked[e])
            m, c = {}, 0
            it = iter(marked)
            nxt = next(it, None)
            for s in range(1, self.count[e] + 1):
                if nxt == s:
                    c += 1
                    nxt = next(it, None)
                m[s] = c
            inc_val[e] = m
        sems = {}
        for e in ENGINES:
            if self.marked[e]:
                sems[("eng", e)] = es.enter_context(nc.semaphore(f"se_{e}"))
        for i, k in enumerate(self.dma_slot_count):
            sems[("dma", k)] = es.enter_context(nc.semaphore(f"sd_{i}"))
        block = es.enter_context(nc.Block())

        def run_stream(eng_name):
            def _(eng):
                own = sems.get(("eng", eng_name))
                mark = self.marked[eng_name]
                for a in self.actions[eng_name]:
                    for (kind, pk), val in a["waits"]:
                        if kind == "eng":
                            eng.wait_ge(sems[("eng", pk)], inc_val[pk][val])
                        else:
                            eng.wait_ge(sems[("dma", pk)], val)
                    inst = a["fn"](eng)
                    if a["kind"] == "eng" and a["seq"] in mark:
                        inst.then_inc(own, 1)
                    elif a["kind"] == "dma":
                        inst.then_inc(sems[("dma", a["slot"])], 16)
            return _

        for e in ENGINES:
            if self.actions[e]:
                getattr(block, e)(run_stream(e))
        return sems


class Overlay:

    def __init__(self, nc, es, total_bytes, name="ovl"):
        self.nc = nc
        self.backing = es.enter_context(
            nc.sbuf_tensor(name, [128, total_bytes], mybir.dt.uint8))
        self.base = nc.lookup_mloc(self.backing).addr
        self.total = total_bytes

    def at(self, name, off, shape, dtype):
        assert off % 32 == 0, (name, off)
        per_part = int(np.prod(shape[1:])) * mybir.dt.size(dtype)
        assert off + per_part <= self.total, (name, off, per_part, self.total)
        h = self.nc.virtual_sbuf_tensor(name, shape, dtype)
        self.nc._tensor(h.name, h.shape, h.dtype, type="SB")
        mloc = self.nc.lookup_mloc(h)
        mloc.allocated = not self.nc.target_bir_lowering
        mloc.addr = self.base + off
        return h


def make_cfg(mini=False):
    if mini:
        return dict(D=512, S=1024, NH=4, NKV=2, HID=256, SHID=256, E=8,
                    B=1, CORES=2, debug=True)
    return dict(D=2048, S=2048, NH=8, NKV=4, HID=1024, SHID=1024, E=8,
                B=4, CORES=8, debug=False)


def own_qbs(p, SB):
    out = []
    for a in range(SB // 4):
        out += ([4 * a, 4 * a + 3] if p == 0 else [4 * a + 1, 4 * a + 2])
    return sorted(out)


def build_program(cfg):
    D, S = cfg["D"], cfg["S"]
    NH, NKV, E = cfg["NH"], cfg["NKV"], cfg["E"]
    HID, SHID = cfg["HID"], cfg["SHID"]
    KC, SB = D // 128, S // 128
    NJ = SB // 2
    OT = NJ * 128
    QD, KD, VD = NH * 256, NKV * 256, NKV * 256
    QC, KCo, VC = QD // 128, KD // 128, VD // 128
    NOC = QC + KCo + VC
    HC, SHC = HID // 128, SHID // 128
    TCW = min(512, OT)
    TC = OT // TCW
    VS = NKV * 257
    HTOK = min(1024, S)
    NHALF = S // HTOK
    OH = min(512, OT)
    NOH = OT // OH
    SCALE = 128.0 ** -0.5
    dbg = cfg["debug"]

    nc = bass.Bass("TRN2", target_bir_lowering=False,
                   detect_race_conditions=False)
    p = Prog(nc)
    p.stg_rr = 0

    x_fm = nc.dram_tensor("x_fm", [KC, 128, S], F32, kind="ExternalInput").ap()
    x_own = nc.dram_tensor("x_own", [KC, 128, OT], F32, kind="ExternalInput").ap()
    wqkvT = nc.dram_tensor("wqkvT", [NOC, KC, 128, 128], F32, kind="ExternalInput").ap()
    bqkv_h = nc.dram_tensor("bqkv", [128, NOC], F32, kind="ExternalInput").ap()
    woT = nc.dram_tensor("woT", [KC, QC, 128, 128], F32, kind="ExternalInput").ap()
    bo_h = nc.dram_tensor("bo", [128, KC], F32, kind="ExternalInput").ap()
    lam_in = nc.dram_tensor("lam_in", [128, 4], F32, kind="ExternalInput").ap()
    subw_bc = nc.dram_tensor("subw_bc", [128, 256], F32, kind="ExternalInput").ap()
    masks_h = nc.dram_tensor("masks", [NJ, 2, 128, 128], BF16, kind="ExternalInput").ap()
    ident_f = nc.dram_tensor("ident_f", [128, 128], F32, kind="ExternalInput").ap()
    ident_b = nc.dram_tensor("ident_b", [128, 128], BF16, kind="ExternalInput").ap()
    gwT_h = nc.dram_tensor("gwT", [KC, 128, 8], F32, kind="ExternalInput").ap()
    w1T_h = nc.dram_tensor("w1T", [E, KC, 128, HID], F32, kind="ExternalInput").ap()
    w3T_h = nc.dram_tensor("w3T", [E, KC, 128, HID], F32, kind="ExternalInput").ap()
    w2T_h = nc.dram_tensor("w2T", [E, KC, HC, 128, 128], F32, kind="ExternalInput").ap()
    sw1T_h = nc.dram_tensor("sw1T", [KC, 128, SHID], F32, kind="ExternalInput").ap()
    sw3T_h = nc.dram_tensor("sw3T", [KC, 128, SHID], F32, kind="ExternalInput").ap()
    sw2T_h = nc.dram_tensor("sw2T", [KC, SHC, 128, 128], F32, kind="ExternalInput").ap()
    out_h = nc.dram_tensor("out_fm", [KC, 128, OT], F32, kind="ExternalOutput").ap()
    dbg_h = {}
    if dbg:
        for nm, sh in [("d_xln", [KC, 128, S]), ("d_q", [QC, 128, OT]),
                       ("d_k", [KCo, 128, S]), ("d_v", [128, SB * VS]),
                       ("d_otm", [128, NJ * QD]), ("d_Xmid", [KC, 128, OT]),
                       ("d_xln2", [KC, 128, OT]), ("d_bfm", [128, OT])]:
            dbg_h[nm] = nc.dram_tensor(nm, sh, F32, kind="ExternalOutput").ap()

    es = ExitStack()
    with es:
        consts = es.enter_context(nc.sbuf_tensor("consts", [128, 8], F32))
        lam_sb = es.enter_context(nc.sbuf_tensor("lam_sb", [128, 8], F32))
        onesb = es.enter_context(nc.sbuf_tensor("onesb", [128, 128], BF16))
        onesf = es.enter_context(nc.sbuf_tensor("onesf", [128, 128], F32))
        selS = es.enter_context(nc.sbuf_tensor("selS", [128, 128], F32))
        id8 = es.enter_context(nc.sbuf_tensor("id8", [128, 8], F32))
        stg = es.enter_context(nc.sbuf_tensor("stg", [128, 2 * 1024], F32))
        SLOT = [(slice(i * 1024, (i + 1) * 1024), f"stg{i}") for i in range(2)]
        psum = [es.enter_context(nc.psum_tensor(f"pb{i}", [128, 512], F32))
                for i in range(8)]
        PK = [f"pb{i}" for i in range(8)]

        def psum_alias(i, name, dtype):
            h = nc.virtual_psum_tensor(name, [128, 512], dtype)
            nc._tensor(h.name, h.shape, h.dtype, type="PSUM")
            mloc = nc.lookup_mloc(h)
            src = nc.lookup_mloc(psum[i])
            mloc.allocated = not nc.target_bir_lowering
            mloc.addr = src.addr
            mloc.bank = src.bank
            return h
        psumb = [psum_alias(i, f"pbb{i}", BF16) for i in range(8)]

        KB = 1024

        def rnd(x):
            return (int(x) + 31) & ~31
        WSTN = max(KC, QC) * 128
        HTOKB = HTOK
        specs = [
            ("xln",     KC * S * 2,        1, 2),
            ("x_half",  KC * HTOK * 4,     1, 1),
            ("statA_m", HTOK * 4,          1, 3),
            ("statA_s", HTOK * 4,          1, 3),
            ("statA_t", HTOK * 4,          1, 3),
            ("statA_xb", HTOK * 4,         1, 3),
            ("statA_sq", HTOK * 4,         1, 3),
            ("wstrip",  2 * WSTN * 2,      1, 7),
            ("masks",   NJ * 2 * 128 * 2,  0, 4),
            ("idf",     128 * 4,           0, 9),
            ("idb",     128 * 2,           0, 6),
            ("weff",    256 * 4,           0, 6),
            ("bqkv",    NOC * 4,           0, 3),
            ("bo",      KC * 4,            0, 7),
            ("vfm",     2 * S * 2,         2, 2),
            ("kbuf",    KCo * S * 2,       2, 4),
            ("vbuf",    SB * VS * 2,       2, 4),
            ("xo_half", KC * OH * 4,       3, 3),
            ("xow",     KC * OT * 2,       3, 3),
            ("qbuf",    QC * OT * 2,       3, 4),
            ("psb",     1536 * 2,          4, 4),
            ("osc",     max(QD, 2048) * 4,  4, 6),
            ("rsc",     16 * 4,            4, 6),
            ("ms_sb",   NJ * NH * 4,       4, 6),
            ("otm",     NJ * QD * 2,       4, 6),
            ("O_fm",    QC * OT * 2,       5, 7),
            ("Xres",    KC * OT * 4,       7, 11),
            ("w1b",     KC * HID * 2,      7, 10),
            ("w3b",     KC * HID * 2,      7, 10),
            ("statB_m", OT * 4,            8, 8),
            ("statB_s", OT * 4,            8, 8),
            ("statB_t", OT * 4,            8, 8),
            ("statB_xb", OT * 4,           8, 8),
            ("statB_sq", OT * 4,           8, 8),
            ("xln2",    KC * OT * 2,       8, 10),
            ("gwb",     KC * 8 * 2,        7, 10),
            ("xstg",    2 * TCW * 4,       7, 7),
            ("scr8",    96 * 4,            9, 9),
            ("b_fm",    OT * 4,            9, 10),
            ("Hb",      HC * OT * 2,       10, 10),
            ("B_sb",    OT * 4,            10, 10),
            ("w2b",     3 * HC * 128 * 2,  10, 10),
            ("sUg",     OT * 4,            10, 10),
            ("escr",    2 * TCW * 4,       10, 10),
        ]
        def pack(order):
            placed, offs = [], {}
            for name, size, lo, hi in order:
                size = rnd(size)
                cands = sorted({0} | {o + sz for o, sz, l, h in placed
                                if not (h < lo or l > hi)})
                pos = None
                for cand in cands:
                    if all(h < lo or l > hi or cand + size <= o or cand >= o + sz
                           for o, sz, l, h in placed):
                        pos = cand
                        break
                offs[name] = pos
                placed.append((pos, size, lo, hi))
            return offs, rnd(max(o + sz for o, sz, _, _ in placed))
        best = None
        for keyf in (lambda t: -t[1],
                     lambda t: (-(t[3] - t[2]), -t[1]),
                     lambda t: (t[2], -t[1]),
                     lambda t: (-(t[3] - t[2]) * 1000000 - t[1],)):
            offs, tot = pack(sorted(specs, key=keyf))
            if best is None or tot < best[1]:
                best = (offs, tot)
        off, TOTAL = best
        ovl = Overlay(nc, es, TOTAL)

        xln = ovl.at("xln", off["xln"], [128, KC * S], BF16)
        x_half = ovl.at("x_half", off["x_half"], [128, KC * HTOK], F32)
        mA = ovl.at("mA", off["statA_m"], [128, HTOK], F32)
        sA = ovl.at("sA", off["statA_s"], [128, HTOK], F32)
        tA = ovl.at("tA", off["statA_t"], [128, HTOK], F32)
        xbA = ovl.at("xbA", off["statA_xb"], [128, 2 * HTOK], BF16)
        sqA = ovl.at("sqA", off["statA_sq"], [128, 2 * HTOK], BF16)
        mB = ovl.at("mB", off["statB_m"], [128, OT], F32)
        sB = ovl.at("sB", off["statB_s"], [128, OT], F32)
        tB = ovl.at("tB", off["statB_t"], [128, OT], F32)
        xbB = ovl.at("xbB", off["statB_xb"], [128, 2 * OT], BF16)
        sqB = ovl.at("sqB", off["statB_sq"], [128, 2 * OT], BF16)
        wstrip = ovl.at("wstrip", off["wstrip"], [128, 2 * WSTN], BF16)
        WST = [(slice(i * WSTN, (i + 1) * WSTN), f"wst{i}") for i in range(2)]
        mask_sb = ovl.at("mask_sb", off["masks"], [128, NJ * 2 * 128], BF16)
        idf = ovl.at("idf", off["idf"], [128, 128], F32)
        idb = ovl.at("idb", off["idb"], [128, 128], BF16)
        weff = ovl.at("weff", off["weff"], [128, 256], F32)
        bqkv = ovl.at("bqkv_sb", off["bqkv"], [128, NOC], F32)
        bo_sb = ovl.at("bo_sb", off["bo"], [128, KC], F32)
        vfm = ovl.at("vfm", off["vfm"], [128, 2 * S], BF16)
        kbuf = ovl.at("kbuf", off["kbuf"], [128, KCo * S], BF16)
        vbuf = ovl.at("vbuf", off["vbuf"], [128, SB * VS], BF16)
        xo_half = ovl.at("xo_half", off["xo_half"], [128, KC * OH], F32)
        xow = ovl.at("xow", off["xow"], [128, KC * OT], BF16)
        qbuf = ovl.at("qbuf", off["qbuf"], [128, QC * OT], BF16)
        psb = ovl.at("psb", off["psb"], [128, 1536], BF16)
        osc = ovl.at("osc", off["osc"], [128, max(QD, 2048)], F32)
        rsc = ovl.at("rsc", off["rsc"], [128, 16], F32)
        ms_sb = ovl.at("ms_sb", off["ms_sb"], [128, NJ * NH], F32)
        otm = ovl.at("otm", off["otm"], [128, NJ * QD], BF16)
        O_fm = ovl.at("O_fm", off["O_fm"], [128, QC * OT], BF16)
        X = ovl.at("Xres", off["Xres"], [128, KC * OT], F32)
        w1b = ovl.at("w1b", off["w1b"], [128, KC * HID], BF16)
        w3b = ovl.at("w3b", off["w3b"], [128, KC * HID], BF16)
        xln2 = ovl.at("xln2", off["xln2"], [128, KC * OT], BF16)
        gwb = ovl.at("gwb", off["gwb"], [128, KC * 8], BF16)
        xstg = ovl.at("xstg", off["xstg"], [128, 2 * TCW], F32)
        scr8 = ovl.at("scr8", off["scr8"], [128, 96], F32)
        b_fm = ovl.at("b_fm", off["b_fm"], [128, OT], F32)
        Hb = ovl.at("Hb", off["Hb"], [128, HC * OT], BF16)
        B_sb = ovl.at("B_sb", off["B_sb"], [128, OT], F32)
        w2b = ovl.at("w2b", off["w2b"], [128, 3 * HC * 128], BF16)
        sUg = ovl.at("sUg", off["sUg"], [128, OT], F32)
        escr = ovl.at("escr", off["escr"], [128, 2 * TCW], F32)

        p.dma("scalar", lambda e: e.dma_start(out=idf[:, :], in_=ident_f[:, :]), writes=[("idf", 0)])
        p.dma("scalar", lambda e: e.dma_start(out=id8[:, :], in_=ident_f[:, 0:8]), writes=[("id8", 0)])
        p.dma("scalar", lambda e: e.dma_start(out=idb[:, :], in_=ident_b[:, :]), writes=[("idb", 0)])
        p.dma("scalar", lambda e: e.dma_start(out=lam_sb[:, 0:4], in_=lam_in[:, :]), writes=[("lamin", 0)])
        p.dma("scalar", lambda e: e.dma_start(out=weff[:, :], in_=subw_bc[:, :]), writes=[("weff", 0)])
        p.dma("scalar", lambda e: e.dma_start(out=bqkv[:, :], in_=bqkv_h[:, :]), writes=[("bqkv", 0)])
        p.dma("scalar", lambda e: e.dma_start(out=bo_sb[:, :], in_=bo_h[:, :]), writes=[("bo", 0)])
        p.dma("scalar", lambda e: e.dma_start(
            out=mask_sb[:, :].rearrange("p (j t n) -> p j t n", j=NJ, t=2),
            in_=masks_h.rearrange("j t p n -> p j t n")), writes=[("masks", 0)])
        p.op("vector", lambda e: e.memset(onesb[:, :], 1.0), writes=[("onesb", 0)])
        p.op("vector", lambda e: e.memset(onesf[:, :], 1.0), writes=[("onesf", 0)])
        p.op("vector", lambda e: e.memset(consts[:, 0:1], -EXP_BIAS), writes=[("c0", 0)])
        p.op("vector", lambda e: e.tensor_scalar_mul(weff[:, :], weff[:, :], 1.0 - LAMBDA_INIT),
             reads=[("weff", 0)], writes=[("weff", 0)])
        p.op("vector", lambda e: e.tensor_mul(lam_sb[:, 4:5], lam_sb[:, 0:1], lam_sb[:, 1:2]),
             reads=[("lamin", 0)], writes=[("lp", 0)])
        p.op("vector", lambda e: e.tensor_mul(lam_sb[:, 5:6], lam_sb[:, 2:3], lam_sb[:, 3:4]),
             reads=[("lamin", 0)], writes=[("lp", 1)])
        p.op("tensor", lambda e: e.matmul(psum[0][:, 0:1], onesf[:, :], lam_sb[:, 4:5], start=True, stop=True),
             reads=[("onesf", 0), ("lp", 0)], writes=[(PK[0], 0)])
        p.op("tensor", lambda e: e.matmul(psum[0][:, 1:2], onesf[:, :], lam_sb[:, 5:6], start=True, stop=True),
             reads=[("onesf", 0), ("lp", 1)], writes=[(PK[0], 1)])
        p.op("scalar", lambda e: e.activation(out=lam_sb[:, 6:8], in_=psum[0][:, 0:2], func=AF.Exp),
             reads=[(PK[0], 0), (PK[0], 1)], writes=[("lexp", 0)])
        p.op("vector", lambda e: e.tensor_sub(lam_sb[:, 4:5], lam_sb[:, 6:7], lam_sb[:, 7:8]),
             reads=[("lexp", 0)], writes=[("lam", 0)])
        p.op("vector", lambda e: e.tensor_scalar_add(lam_sb[:, 4:5], lam_sb[:, 4:5], LAMBDA_INIT),
             reads=[("lam", 0)], writes=[("lam", 0)])

        def emit_ln(kp, ncols, xsrc_ap, xsrc_keys, dst_write, bufs):
            m_sb, s_sb, t_sb, xb_sb, sq_sb = bufs
            NT5 = ncols // 512 if ncols >= 512 else 1
            W5 = min(512, ncols)
            for kc in range(KC):
                pb = (kc % 2) * ncols
                p.op("scalar", lambda e, kc=kc, pb=pb: e.copy(
                    out=xb_sb[:, pb:pb + ncols], in_=xsrc_ap(kc)),
                    reads=xsrc_keys(kc), writes=[("LNxbf", kc % 2)])
                p.op("vector", lambda e, kc=kc, pb=pb: e.tensor_mul(
                    sq_sb[:, pb:pb + ncols], xb_sb[:, pb:pb + ncols],
                    xb_sb[:, pb:pb + ncols]),
                    reads=[("LNxbf", kc % 2)], writes=[("LNsqb", kc % 2)])
                for t5 in range(NT5):
                    sl = slice(pb + t5 * W5, pb + (t5 + 1) * W5)
                    p.op("tensor", lambda e, kc=kc, sl=sl, t5=t5: e.matmul(
                        psum[t5][:, 0:W5], onesb[:, :], xb_sb[:, sl],
                        start=(kc == 0), stop=(kc == KC - 1)),
                        reads=[("LNxbf", kc % 2), ("onesb", 0)], writes=[(PK[t5], 0)])
                    p.op("tensor", lambda e, kc=kc, sl=sl, t5=t5: e.matmul(
                        psum[4 + t5][:, 0:W5], onesb[:, :], sq_sb[:, sl],
                        start=(kc == 0), stop=(kc == KC - 1)),
                        reads=[("LNsqb", kc % 2), ("onesb", 0)], writes=[(PK[4 + t5], 0)])
            for t5 in range(NT5):
                sl = slice(t5 * W5, (t5 + 1) * W5)
                p.op("vector", lambda e, sl=sl, t5=t5: e.tensor_scalar_mul(
                    m_sb[:, sl], psum[t5][:, 0:W5], 1.0 / D),
                    reads=[(PK[t5], 0)], writes=[("LNm", t5)])
                p.op("vector", lambda e, sl=sl, t5=t5: e.tensor_scalar_mul(
                    t_sb[:, sl], psum[4 + t5][:, 0:W5], 1.0 / D),
                    reads=[(PK[4 + t5], 0)], writes=[("LNt", t5)])
            mk = [("LNm", t5) for t5 in range(NT5)]
            tk = [("LNt", t5) for t5 in range(NT5)]
            cs = slice(0, ncols)
            p.op("vector", lambda e: e.tensor_mul(s_sb[:, cs], m_sb[:, cs], m_sb[:, cs]),
                 reads=mk, writes=[("LNs", 0)])
            p.op("vector", lambda e: e.tensor_sub(t_sb[:, cs], t_sb[:, cs], s_sb[:, cs]),
                 reads=tk + [("LNs", 0)], writes=tk)
            p.op("vector", lambda e: e.tensor_scalar_add(t_sb[:, cs], t_sb[:, cs], LN_EPS),
                 reads=tk, writes=tk)
            p.op("scalar", lambda e: e.activation(out=s_sb[:, cs], in_=t_sb[:, cs], func=AF.Sqrt),
                 reads=tk, writes=[("LNs", 0)])
            p.op("vector", lambda e: e.reciprocal(s_sb[:, cs], s_sb[:, cs]),
                 reads=[("LNs", 0)], writes=[("LNs", 0)])
            for kc in range(KC):
                p.op("vector", lambda e, kc=kc: e.tensor_sub(
                    t_sb[:, cs], xsrc_ap(kc), m_sb[:, cs]),
                    reads=xsrc_keys(kc) + mk, writes=[("LNt", 0)])
                dst_write(kc)

        def load_strip_hwdge(oc, wi, src, nk, cast_eng="scalar"):
            wsl, wk = WST[wi]
            ncol = nk * 128
            keys = []
            for half in range((ncol + 1023) // 1024):
                w = min(1024, ncol - half * 1024)
                sl, sk = SLOT[p.stg_rr % len(SLOT)]
                p.stg_rr += 1
                klo, khi = half * 8, half * 8 + w // 128
                p.dma("sync", lambda e, oc=oc, sl=sl, w=w, klo=klo, khi=khi, src=src: e.dma_start(
                    out=stg[:, sl][:, 0:w].rearrange("p (k n) -> p k n", n=128),
                    in_=src[oc, klo:khi].rearrange("k p n -> p k n")),
                    writes=[(sk, 0)], tag="wdma")
                if cast_eng == "scalar":
                    p.op("scalar", lambda e, sl=sl, wsl=wsl, half=half, w=w: e.copy(
                        out=wstrip[:, wsl][:, half * 1024:half * 1024 + w],
                        in_=stg[:, sl][:, 0:w]),
                        reads=[(sk, 0)], writes=[(wk, half)], tag="wcast")
                else:
                    p.op("vector", lambda e, sl=sl, wsl=wsl, half=half, w=w: e.tensor_copy(
                        out=wstrip[:, wsl][:, half * 1024:half * 1024 + w],
                        in_=stg[:, sl][:, 0:w]),
                        reads=[(sk, 0)], writes=[(wk, half)], tag="wcast")
                keys.append((wk, half))
            return keys

        def load_qkv_strip(oc, wi, cast_eng=None, src=None):
            hbm = src if src is not None else wqkvT
            wsl, wk = WST[wi]
            p.dma("gpsimd", lambda e, oc=oc, wsl=wsl, hbm=hbm: e.dma_start(
                out=wstrip[:, wsl][:, 0:KC * 128].rearrange("p (k n) -> p k n", n=128),
                in_=hbm[oc].rearrange("k p n -> p k n")),
                writes=[(wk, 0)], tag="wdma")
            return [(wk, 0)]


        for h in range(NHALF):
            tok0 = h * HTOK
            kp = f"L1h{h}_"
            for kc in range(KC):
                p.dma("sync", lambda e, kc=kc, tok0=tok0: e.dma_start(
                    out=x_half[:, kc * HTOK:(kc + 1) * HTOK],
                    in_=x_fm[kc][:, tok0:tok0 + HTOK]),
                    writes=[("xh", kc)], slot=f"xh{kc}", tag="ldx")

            def dst(kc, tok0=tok0, kp=kp):
                p.op("vector", lambda e, kc=kc: e.tensor_mul(
                    xln[:, kc * S + tok0: kc * S + tok0 + HTOK],
                    tA[:, 0:HTOK], sA[:, 0:HTOK]),
                    reads=[("LNt", 0), ("LNs", 0)],
                    writes=[("xln", (kc, tok0 // HTOK))], tag="nrm")
            emit_ln(kp, HTOK,
                    lambda kc: x_half[:, kc * HTOK:(kc + 1) * HTOK],
                    lambda kc: [("xh", kc)], dst, (mA, sA, tA, xbA, sqA))

        def xlnkey(kc, col):
            return ("xln", (kc, col // HTOK))
        if dbg:
            p.dma("gpsimd", lambda e: e.dma_start(
                out=dbg_h["d_xln"].rearrange("k p s -> p k s"), in_=xln[:].rearrange("p (k s) -> p k s", s=S)),
                reads=[("xln", (kc, h)) for kc in range(KC) for h in range(NHALF)], tag="dbg")

        _pre_strip = {}
        for i in range(2):
            _pre_strip[i] = load_strip_hwdge(QC + i, i, wqkvT, KC,
                                             cast_eng=("vector" if i % 2 else "scalar"))

        p.fence_all("endP1", dma=dbg)

        NS4 = S // 512
        for i, oc in enumerate(range(QC, QC + KCo)):
            wi = i % 2
            wkeys = (_pre_strip.pop(i) if i in _pre_strip else
                     load_strip_hwdge(oc, wi, wqkvT, KC,
                                      cast_eng=("vector" if i % 2 else "scalar")))
            wsl = WST[wi][0]
            for kc in range(KC):
                for t4 in range(NS4):
                    bank = psum[(i % 2) * 4 + t4 % 4]
                    p.op("tensor", lambda e, kc=kc, t4=t4, bank=bank, wsl=wsl: e.matmul(
                        bank[:, 0:512], wstrip[:, wsl][:, kc * 128:(kc + 1) * 128],
                        xln[:, kc * S + t4 * 512: kc * S + (t4 + 1) * 512],
                        start=(kc == 0), stop=(kc == KC - 1)),
                        reads=[xlnkey(kc, t4 * 512)] + wkeys,
                        writes=[(bank.name, 0)], tag="kmm")
            for t4 in range(NS4):
                bank = psum[(i % 2) * 4 + t4 % 4]
                p.op("scalar", lambda e, i=i, oc=oc, t4=t4, bank=bank: e.activation(
                    out=kbuf[:, i * S + t4 * 512: i * S + (t4 + 1) * 512],
                    in_=bank[:, 0:512], func=AF.Identity, bias=bqkv[:, oc:oc + 1]),
                    reads=[(bank.name, 0), ("bqkv", 0)], writes=[("k", (i, t4))], tag="kevac")
        kkeys = [("k", (i, t4)) for i in range(KCo) for t4 in range(NS4)]

        for vi in range(VC):
            oc = QC + KCo + vi
            wi = vi % 2
            wkeys = load_strip_hwdge(oc, wi, wqkvT, KC,
                                     cast_eng=("vector" if vi % 2 else "scalar"))
            wsl = WST[wi][0]
            vsl, vkey = (slice(0, S), "vfm0") if vi % 2 == 0 else (slice(S, 2 * S), "vfm1")
            for kc in range(KC):
                for t4 in range(NS4):
                    bank = psum[(vi % 2) * 4 + t4 % 4]
                    p.op("tensor", lambda e, kc=kc, t4=t4, bank=bank, wsl=wsl: e.matmul(
                        bank[:, 0:512], wstrip[:, wsl][:, kc * 128:(kc + 1) * 128],
                        xln[:, kc * S + t4 * 512: kc * S + (t4 + 1) * 512],
                        start=(kc == 0), stop=(kc == KC - 1)),
                        reads=[xlnkey(kc, t4 * 512)] + wkeys,
                        writes=[(bank.name, 0)], tag="vmm")
            for t4 in range(NS4):
                bank = psum[(vi % 2) * 4 + t4 % 4]
                p.op("scalar", lambda e, vi=vi, oc=oc, t4=t4, bank=bank, vsl=vsl: e.activation(
                    out=vfm[:, vsl][:, t4 * 512:(t4 + 1) * 512],
                    in_=bank[:, 0:512], func=AF.Identity, bias=bqkv[:, oc:oc + 1]),
                    reads=[(bank.name, 0), ("bqkv", 0)], writes=[(vkey, t4)], tag="vevac")
            kv = vi // 2
            sub = vi % 2
            for sc in range(SB):
                bi = 6 + sc % 2
                bank = psumb[bi]
                p.op("tensor", lambda e, sc=sc, bank=bank, vsl=vsl: e.transpose(
                    bank[:, 0:128], vfm[:, vsl][:, sc * 128:(sc + 1) * 128], idb[:, :]),
                    reads=[(vkey, (sc * 128) // 512), ("idb", 0)],
                    writes=[(PK[bi], 0)], tag="vT")
                p.op("vector", lambda e, sc=sc, kv=kv, sub=sub, bank=bank: e.tensor_copy(
                    out=vbuf[:, sc * VS + kv * 257 + sub * 128: sc * VS + kv * 257 + sub * 128 + 128],
                    in_=bank[:, 0:128]),
                    reads=[(PK[bi], 0)], writes=[("v", (sc, vi))], tag="vTe")
        for sc in range(SB):
            for kv in range(NKV):
                p.op("vector", lambda e, sc=sc, kv=kv: e.memset(
                    vbuf[:, sc * VS + kv * 257 + 256: sc * VS + kv * 257 + 257], 1.0),
                    writes=[("v", (sc, VC + kv))], tag="vones")
        vkeys = [("v", (sc, i)) for sc in range(SB) for i in range(VC + NKV)]

        if dbg:
            p.dma("gpsimd", lambda e: e.dma_start(
                out=dbg_h["d_k"].rearrange("k p s -> p k s"), in_=kbuf[:].rearrange("p (k s) -> p k s", s=S)),
                reads=kkeys, tag="dbg")
            p.dma("gpsimd", lambda e: e.dma_start(out=dbg_h["d_v"], in_=vbuf[:]),
                  reads=vkeys, tag="dbg")

        p.fence_all("endP2a", dma=dbg)

        for oh in range(NOH):
            t0 = oh * OH
            kp = f"LOh{oh}_"
            for kc in range(KC):
                p.dma("sync", lambda e, kc=kc, t0=t0: e.dma_start(
                    out=xo_half[:, kc * OH:(kc + 1) * OH],
                    in_=x_own[kc][:, t0:t0 + OH]),
                    writes=[("xo", kc)], slot=f"xo{kc}", tag="ldxo")

            def dsto(kc, t0=t0, oh=oh, kp=kp):
                p.op("vector", lambda e, kc=kc, t0=t0: e.tensor_mul(
                    xow[:, kc * OT + t0: kc * OT + t0 + OH], tA[:, 0:OH], sA[:, 0:OH]),
                    reads=[("LNt", 0), ("LNs", 0)],
                    writes=[("xow", (kc, oh))], tag="nrmo")
            emit_ln(kp, OH,
                    lambda kc: xo_half[:, kc * OH:(kc + 1) * OH],
                    lambda kc: [("xo", kc)], dsto, (mA, sA, tA, xbA, sqA))
        for oc in range(QC):
            wi = oc % 2
            wkeys = load_strip_hwdge(oc, wi, wqkvT, KC,
                                     cast_eng=("vector" if oc % 2 else "scalar"))
            wsl = WST[wi][0]
            for oh in range(NOH):
                t0 = oh * OH
                bank = psum[(oc % 2) * 2 + oh % 2]
                for kc in range(KC):
                    p.op("tensor", lambda e, kc=kc, t0=t0, bank=bank, wsl=wsl: e.matmul(
                        bank[:, 0:OH], wstrip[:, wsl][:, kc * 128:(kc + 1) * 128],
                        xow[:, kc * OT + t0: kc * OT + t0 + OH],
                        start=(kc == 0), stop=(kc == KC - 1)),
                        reads=[("xow", (kc, oh))] + wkeys,
                        writes=[(bank.name, 0)], tag="qmm")
                p.op("scalar", lambda e, oc=oc, t0=t0, bank=bank: e.activation(
                    out=qbuf[:, oc * OT + t0: oc * OT + t0 + OH],
                    in_=bank[:, 0:OH], func=AF.Identity, bias=bqkv[:, oc:oc + 1]),
                    reads=[(bank.name, 0), ("bqkv", 0)], writes=[("q", (oc, oh, 0))], tag="qevac")
        qkeys = [("q", (oc, oh, 0)) for oc in range(QC) for oh in range(NOH)]
        if dbg:
            p.dma("gpsimd", lambda e: e.dma_start(
                out=dbg_h["d_q"].rearrange("k p s -> p k s"), in_=qbuf[:].rearrange("p (k s) -> p k s", s=OT)),
                reads=qkeys, tag="dbg")

        p.fence_all("endP2b", dma=dbg)

        _pre_wo = {oc: load_strip_hwdge(oc, oc % 2, woT, QC,
                                        cast_eng=("vector" if oc % 2 else "scalar"))
                   for oc in range(2)}

        for h in range(NH):
            kv = h // (NH // NKV) if NKV > 0 else 0
            koff = [(kv * 2 + c) * S for c in range(2)]
            qoffc = [(h * 2 + c) * OT for c in range(2)]
            for jp in range(NJ // 2):
                j0, j1 = 2 * jp, 2 * jp + 1
                ctx1 = 2 * j0 + 2
                ctx = 2 * j1 + 2
                obank = [psum[4 + t] for t in range(4)]

                def emit_pv(kc):
                    jlist = [j0, j1] if kc < ctx1 else [j1]
                    pp = kc % 3
                    for j in jlist:
                        for c in range(2):
                            ob = obank[2 * (j - j0) + c]
                            poff = pp * 512 + c * 256 + (j - j0 if kc < ctx1 else 0) * 128
                            p.op("tensor", lambda e, kc=kc, j=j, c=c, ob=ob, poff=poff, kv=kv: e.matmul(
                                ob[:, 0:257], psb[:, poff:poff + 128],
                                vbuf[:, kc * VS + kv * 257: kc * VS + (kv + 1) * 257],
                                start=(kc == 0), stop=(kc == 2 * j + 1)),
                                reads=[("P", (pp, c))] + vkeys,
                                writes=[(ob.name, 0)], tag="pv")

                for kc in range(ctx):
                    qw = 256 if kc < ctx1 else 128
                    qo = (j0 if kc < ctx1 else j1) * 128
                    sbank = psum[kc % 3]
                    pp = kc % 3
                    for c in range(2):
                        p.op("tensor", lambda e, kc=kc, c=c, qw=qw, qo=qo, sbank=sbank,
                             koff=koff, qoffc=qoffc: e.matmul(
                            sbank[:, c * 256: c * 256 + qw],
                            kbuf[:, koff[c] + kc * 128: koff[c] + (kc + 1) * 128],
                            qbuf[:, qoffc[c] + qo: qoffc[c] + qo + qw],
                            start=True, stop=True),
                            reads=kkeys + qkeys,
                            writes=[(sbank.name, 0)], tag="smm")
                    for c in range(2):
                        p.op("scalar", lambda e, kc=kc, c=c, qw=qw, pp=pp, sbank=sbank: e.activation(
                            out=psb[:, pp * 512 + c * 256: pp * 512 + c * 256 + qw],
                            in_=sbank[:, c * 256: c * 256 + qw],
                            func=AF.Exp, bias=consts[:, 0:1], scale=SCALE),
                            reads=[(sbank.name, 0), ("c0", 0)],
                            writes=[("P", (pp, c))], tag="exp")
                    for j in ([j0, j1] if kc < ctx1 else [j1]):
                        if kc >= 2 * j:
                            moff = (j * 2 + (kc - 2 * j)) * 128
                            poffj = pp * 512 + (j - j0 if kc < ctx1 else 0) * 128
                            for c in range(2):
                                p.op("vector", lambda e, c=c, poffj=poffj, moff=moff: e.tensor_mul(
                                    psb[:, poffj + c * 256: poffj + c * 256 + 128],
                                    psb[:, poffj + c * 256: poffj + c * 256 + 128],
                                    mask_sb[:, moff:moff + 128]),
                                    reads=[("P", (pp, c)), ("masks", 0)],
                                    writes=[("P", (pp, c))], tag="mask")
                    if kc > 0:
                        emit_pv(kc - 1)
                emit_pv(ctx - 1)
                for j in (j0, j1):
                    ob1 = obank[2 * (j - j0)]
                    ob2 = obank[2 * (j - j0) + 1]
                    rp = ((h * (NJ // 2) + jp) * 2 + (j - j0)) % 4
                    ro = rp * 4
                    oo = rp * 512
                    p.op("vector", lambda e, ob1=ob1, ro=ro: e.reciprocal(rsc[:, ro:ro + 1], ob1[:, 256:257]),
                         reads=[(ob1.name, 0)], writes=[("r1", rp)], tag="r1")
                    p.op("vector", lambda e, ob2=ob2, ro=ro: e.reciprocal(rsc[:, ro + 1:ro + 2], ob2[:, 256:257]),
                         reads=[(ob2.name, 0)], writes=[("r2", rp)], tag="r2")
                    p.op("vector", lambda e, ro=ro: e.tensor_mul(rsc[:, ro + 2:ro + 3], rsc[:, ro + 1:ro + 2], lam_sb[:, 4:5]),
                         reads=[("r2", rp), ("lam", 0)], writes=[("r2l", rp)], tag="r2l")
                    p.op("vector", lambda e, ob1=ob1, ro=ro, oo=oo: e.tensor_mul(
                        osc[:, oo:oo + 256], ob1[:, 0:256], rsc[:, ro:ro + 1].to_broadcast([128, 256])),
                        reads=[(ob1.name, 0), ("r1", rp)],
                        writes=[("t1", rp)], tag="t1")
                    p.op("vector", lambda e, ob2=ob2, ro=ro, oo=oo: e.tensor_mul(
                        osc[:, oo + 256:oo + 512], ob2[:, 0:256], rsc[:, ro + 2:ro + 3].to_broadcast([128, 256])),
                        reads=[(ob2.name, 0), ("r2l", rp)],
                        writes=[("t2", rp)], tag="t2")
                    p.op("vector", lambda e, j=j, h=h, oo=oo, rp=rp: e.tensor_sub(
                        otm[:, j * QD + h * 256: j * QD + h * 256 + 256],
                        osc[:, oo:oo + 256], osc[:, oo + 256:oo + 512]),
                        reads=[("t1", rp), ("t2", rp)], writes=[("otm", (j, h))], tag="ocomb")
        otmkeys = [("otm", (j, h)) for j in range(NJ) for h in range(NH)]

        p.fence_all("endP3", dma=dbg)

        for j in range(NJ):
            p.op("vector", lambda e, j=j: e.tensor_mul(
                osc[:, 0:QD], otm[:, j * QD:(j + 1) * QD], otm[:, j * QD:(j + 1) * QD]),
                reads=[("otm", (j, h)) for h in range(NH)], writes=[("osc4", 0)], tag="sq")
            p.op("vector", lambda e, j=j: e.tensor_reduce(
                out=ms_sb[:, j * NH:(j + 1) * NH],
                in_=osc[:, 0:QD].rearrange("p (h n) -> p h n", n=256),
                axis=AX.X, op=ALU.add),
                reads=[("osc4", 0)], writes=[("ms", j)], tag="msum")
        mskeys = [("ms", j) for j in range(NJ)]
        p.op("vector", lambda e: e.tensor_scalar(
            out=ms_sb[:, :], in0=ms_sb[:, :], scalar1=1.0 / 256, scalar2=SUBLN_EPS,
            op0=ALU.mult, op1=ALU.add),
            reads=mskeys, writes=mskeys, tag="msc")
        p.op("scalar", lambda e: e.activation(out=ms_sb[:, :], in_=ms_sb[:, :], func=AF.Sqrt),
             reads=mskeys, writes=mskeys, tag="msqrt")
        p.op("vector", lambda e: e.reciprocal(ms_sb[:, :], ms_sb[:, :]),
             reads=mskeys, writes=mskeys, tag="mrcp")
        for j in range(NJ):
            p.op("vector", lambda e, j=j: e.tensor_mul(
                osc[:, 0:QD].rearrange("p (h n) -> p h n", n=256),
                otm[:, j * QD:(j + 1) * QD].rearrange("p (h n) -> p h n", n=256),
                weff[:, :].rearrange("p (o n) -> p o n", o=1).to_broadcast([128, NH, 256])),
                reads=[("otm", (j, h)) for h in range(NH)] + [("weff", 0)] + mskeys,
                writes=[("osc4", 0)], tag="wmul")
            p.op("vector", lambda e, j=j: e.tensor_mul(
                otm[:, j * QD:(j + 1) * QD].rearrange("p (h n) -> p h n", n=256),
                osc[:, 0:QD].rearrange("p (h n) -> p h n", n=256),
                ms_sb[:, j * NH:(j + 1) * NH].rearrange("p (h o) -> p h o", o=1)
                .to_broadcast([128, NH, 256])),
                reads=[("osc4", 0)] + mskeys,
                writes=[("otm", (j, h)) for h in range(NH)], tag="rmul")
            for odc in range(QC):
                bi = odc % 4
                bank = psumb[bi]
                p.op("tensor", lambda e, j=j, odc=odc, bank=bank: e.transpose(
                    bank[:, 0:128], otm[:, j * QD + odc * 128: j * QD + (odc + 1) * 128], idb[:, :]),
                    reads=[("otm", (j, odc // 2)), ("idb", 0)],
                    writes=[(PK[bi], 0)], tag="oT")
                p.op("scalar", lambda e, j=j, odc=odc, bank=bank: e.copy(
                    out=O_fm[:, odc * OT + j * 128: odc * OT + (j + 1) * 128],
                    in_=bank[:, 0:128]),
                    reads=[(PK[bi], 0)], writes=[("ofm", (odc, j))], tag="oTe")
        if dbg:
            p.dma("gpsimd", lambda e: e.dma_start(out=dbg_h["d_otm"], in_=otm[:]),
                  reads=otmkeys, tag="dbg")
        ofmkeys = [("ofm", (odc, j)) for odc in range(QC) for j in range(NJ)]

        p.fence_all("endP5", dma=dbg)

        def load_w13(eidx):
            keys = {"w1": [], "w3": []}
            for nm, hbm, dst in (("w1", w1T_h, w1b), ("w3", w3T_h, w3b)):
                src = (hbm[eidx] if eidx < E else (sw1T_h if nm == "w1" else sw3T_h))
                hh = HID if eidx < E else SHID
                nq = 4 if KC % 4 == 0 else 2
                for q4 in range(nq):
                    klo, khi = q4 * (KC // nq), (q4 + 1) * (KC // nq)
                    p.dma("gpsimd", lambda e, src=src, dst=dst, klo=klo, khi=khi, hh=hh: e.dma_start(
                        out=dst[:, klo * HID: klo * HID + (khi - klo) * HID]
                            .rearrange("p (k n) -> p k n", n=HID)[:, :, 0:hh],
                        in_=src[klo:khi].rearrange("k p n -> p k n")),
                        writes=[(nm, q4)], tag="w13dma")
                    keys[nm].append((nm, q4))
            return keys

        wkeys13_pre = load_w13(0)

        sl, sk = SLOT[0]
        p.dma("scalar", lambda e, sl=sl: e.dma_start(
            out=stg[:, sl][:, 0:KC * 8].rearrange("p (k n) -> p k n", n=8),
            in_=gwT_h.rearrange("k p n -> p k n")),
            writes=[(sk, 0)], tag="gdma")
        p.op("vector", lambda e, sl=sl: e.tensor_copy(
            out=gwb[:, 0:KC * 8], in_=stg[:, sl][:, 0:KC * 8]),
            reads=[(sk, 0)], writes=[("gwb", 0)], tag="gcast")

        for oc in range(KC):
            wi = oc % 2
            wkeys = (_pre_wo.pop(oc) if oc in _pre_wo else
                     load_strip_hwdge(oc, wi, woT, QC,
                                      cast_eng=("vector" if oc % 2 else "scalar")))
            wsl = WST[wi][0]
            for kc2 in range(QC):
                for tc in range(TC):
                    bank = psum[4 + (oc % 2) * 2 + tc % 2]
                    jlo = (tc * TCW) // 128
                    jhi = min(NJ, ((tc + 1) * TCW) // 128)
                    p.op("tensor", lambda e, kc2=kc2, tc=tc, bank=bank, wsl=wsl: e.matmul(
                        bank[:, 0:TCW], wstrip[:, wsl][:, kc2 * 128:(kc2 + 1) * 128],
                        O_fm[:, kc2 * OT + tc * TCW: kc2 * OT + (tc + 1) * TCW],
                        start=(kc2 == 0), stop=(kc2 == QC - 1)),
                        reads=[("ofm", (kc2, j)) for j in range(jlo, jhi)] + wkeys,
                        writes=[(bank.name, 0)], tag="omm")
            for tc in range(TC):
                bank = psum[4 + (oc % 2) * 2 + tc % 2]
                xi = (oc * TC + tc) % 2
                p.dma("sync", lambda e, oc=oc, tc=tc, xi=xi: e.dma_start(
                    out=xstg[:, xi * TCW:(xi + 1) * TCW],
                    in_=x_own[oc][:, tc * TCW:(tc + 1) * TCW]),
                    writes=[("xstg", xi)], slot=f"xres{xi}", tag="ldres")
                p.op("scalar", lambda e, oc=oc, tc=tc, bank=bank: e.activation(
                    out=X[:, oc * OT + tc * TCW: oc * OT + (tc + 1) * TCW],
                    in_=bank[:, 0:TCW], func=AF.Identity, bias=bo_sb[:, oc:oc + 1]),
                    reads=[(bank.name, 0), ("bo", 0)], writes=[("X", (oc, tc))], tag="oevac")
                p.op("vector", lambda e, oc=oc, tc=tc, xi=xi: e.tensor_add(
                    X[:, oc * OT + tc * TCW: oc * OT + (tc + 1) * TCW],
                    X[:, oc * OT + tc * TCW: oc * OT + (tc + 1) * TCW],
                    xstg[:, xi * TCW:(xi + 1) * TCW]),
                    reads=[("xstg", xi), ("X", (oc, tc))], writes=[("X", (oc, tc))], tag="resadd")
        Xkeys = [("X", (oc, tc)) for oc in range(KC) for tc in range(TC)]
        if dbg:
            p.dma("gpsimd", lambda e: e.dma_start(
                out=dbg_h["d_Xmid"].rearrange("k p s -> p k s"), in_=X[:].rearrange("p (k s) -> p k s", s=OT)),
                reads=Xkeys, tag="dbg")

        p.fence_all("endP6", dma=dbg)

        kp = "L2_"
        assert OT <= HTOK

        def dst2(kc, kp=kp):
            p.op("vector", lambda e, kc=kc: e.tensor_mul(
                xln2[:, kc * OT:(kc + 1) * OT], tB[:, 0:OT], sB[:, 0:OT]),
                reads=[("LNt", 0), ("LNs", 0)], writes=[("xln2", kc)], tag="nrm2")
        emit_ln(kp, OT,
                lambda kc: X[:, kc * OT:(kc + 1) * OT],
                lambda kc: [("X", (kc, tc)) for tc in range(TC)], dst2, (mB, sB, tB, xbB, sqB))
        if dbg:
            p.fence_all("endP7", dma=True)
        x2keys = [("xln2", kc) for kc in range(KC)]
        if dbg:
            p.dma("gpsimd", lambda e: e.dma_start(
                out=dbg_h["d_xln2"].rearrange("k p s -> p k s"), in_=xln2[:].rearrange("p (k s) -> p k s", s=OT)),
                reads=x2keys, tag="dbg")

        scp_sl = SLOT[1][0]
        for tcg in range(TC):
            bank = psum[tcg % 2]
            for kc in range(KC):
                p.op("tensor", lambda e, kc=kc, tcg=tcg, bank=bank: e.matmul(
                    bank[0:8, 0:TCW], gwb[:, kc * 8:(kc + 1) * 8],
                    xln2[:, kc * OT + tcg * TCW: kc * OT + (tcg + 1) * TCW],
                    start=(kc == 0), stop=(kc == KC - 1)),
                    reads=[("xln2", kc), ("gwb", 0)], writes=[(bank.name, 0)], tag="gmm")
            p.op("scalar", lambda e, tcg=tcg, bank=bank: e.activation(
                out=stg[:, scp_sl][0:8, tcg * TCW:(tcg + 1) * TCW],
                in_=bank[0:8, 0:TCW], func=AF.Sigmoid),
                reads=[(bank.name, 0)], writes=[("stg1", 0)], tag="sig")
        p.op("vector", lambda e: e.memset(b_fm[:, :], 0.0), writes=[("bfm", "z")], tag="bz")
        for tb in range(OT // 128):
            sb8 = (tb % 2) * 48
            bank = psum[2 + tb % 2]
            p.op("tensor", lambda e, tb=tb, bank=bank, sb8=sb8: e.transpose(
                bank[:, 0:8], stg[:, scp_sl][0:8, tb * 128:(tb + 1) * 128], idf[0:8, 0:8]),
                reads=[("stg1", 0), ("idf", 0)], writes=[(bank.name, 0)], tag="gT")
            p.op("vector", lambda e, tb=tb, bank=bank, sb8=sb8: e.tensor_copy(
                out=scr8[:, sb8 + 0:sb8 + 8], in_=bank[:, 0:8]),
                reads=[(bank.name, 0)], writes=[("stm", tb % 2)], tag="gTe")
            p.op("vector", lambda e, sb8=sb8, tb=tb: e.max(
                out=scr8[:, sb8 + 8:sb8 + 16], in_=scr8[:, sb8 + 0:sb8 + 8]),
                reads=[("stm", tb % 2)], writes=[("mx8", tb % 2)], tag="max8")
            p.op("vector", lambda e, sb8=sb8, tb=tb: e.tensor_tensor(
                scr8[:, sb8 + 16:sb8 + 24], scr8[:, sb8 + 0:sb8 + 8],
                scr8[:, sb8 + 9:sb8 + 10].to_broadcast([128, 8]), op=ALU.is_ge),
                reads=[("stm", tb % 2), ("mx8", tb % 2)], writes=[("msk", tb % 2)], tag="gmask")
            p.op("vector", lambda e, sb8=sb8, tb=tb: e.tensor_mul(
                scr8[:, sb8 + 24:sb8 + 32], scr8[:, sb8 + 16:sb8 + 24], scr8[:, sb8 + 0:sb8 + 8]),
                reads=[("msk", tb % 2), ("stm", tb % 2)], writes=[("btm", tb % 2)], tag="btm")
            bank2 = psum[4 + tb % 2]
            p.op("tensor", lambda e, tb=tb, bank2=bank2, sb8=sb8: e.transpose(
                bank2[0:8, 0:128], scr8[:, sb8 + 24:sb8 + 32], idf[:, :]),
                reads=[("btm", tb % 2), ("idf", 0)], writes=[(bank2.name, 0)], tag="bT")
            p.op("vector", lambda e, tb=tb, bank2=bank2: e.tensor_copy(
                out=b_fm[0:8, tb * 128:(tb + 1) * 128], in_=bank2[0:8, 0:128]),
                reads=[(bank2.name, 0), ("bfm", "z")], writes=[("bfm", tb)], tag="bTe")
        bfmkeys = [("bfm", tb) for tb in range(OT // 128)] + [("bfm", "z")]
        if dbg:
            p.dma("gpsimd", lambda e: e.dma_start(out=dbg_h["d_bfm"], in_=b_fm[:, :]),
                  reads=bfmkeys, tag="dbg")
            p.fence_all("endP8", dma=True)


        wkeys13 = wkeys13_pre
        for eidx in range(E + 1):
            shared = eidx == E
            hc_n = SHC if shared else HC
            hh = SHID if shared else HID
            if not shared:
                p.op("vector", lambda e, eidx=eidx: e.tensor_mul(
                    selS[:, :], onesf[:, :], id8[:, eidx:eidx + 1].to_broadcast([128, 128])),
                    reads=[("onesf", 0), ("id8", 0)], writes=[("sel", 0)], tag="sel")
                for bs in range(TC):
                    bank = psum[6]
                    p.op("tensor", lambda e, bs=bs, bank=bank: e.matmul(
                        bank[:, 0:TCW], selS[:, :], b_fm[:, bs * TCW:(bs + 1) * TCW],
                        start=True, stop=True),
                        reads=[("sel", 0)] + bfmkeys, writes=[(bank.name, 0)], tag="bmm")
                    p.op("vector", lambda e, bs=bs, bank=bank: e.tensor_copy(
                        out=B_sb[:, bs * TCW:(bs + 1) * TCW], in_=bank[:, 0:TCW]),
                        reads=[(bank.name, 0)], writes=[("B", bs)], tag="bev")
            Bkeys = [("B", bs) for bs in range(TC)]
            groups = [(g,) for g in range(hc_n)]
            for gi, g in enumerate(groups):
                ub = {}
                for i2, hc in enumerate(g):
                    for tc in range(TC):
                        ub[(hc, tc)] = psum[((gi % 2) * 4 + i2 * TC + tc) % 8]
                for kc in range(KC):
                    for hc in g:
                        for tc in range(TC):
                            b = ub[(hc, tc)]
                            p.op("tensor", lambda e, kc=kc, hc=hc, tc=tc, b=b: e.matmul(
                                b[:, 0:TCW], w1b[:, kc * HID + hc * 128: kc * HID + (hc + 1) * 128],
                                xln2[:, kc * OT + tc * TCW: kc * OT + (tc + 1) * TCW],
                                start=(kc == 0), stop=(kc == KC - 1)),
                                reads=[("xln2", kc), wkeys13["w1"][kc * len(wkeys13["w1"]) // KC]],
                                writes=[(b.name, 0)], tag="U")
                for i2, hc in enumerate(g):
                    for tc in range(TC):
                        b = ub[(hc, tc)]
                        so = tc * TCW
                        ei = (gi * TC + tc) % 2
                        if shared:
                            p.op("scalar", lambda e, b=b, so=so: e.activation(
                                out=sUg[:, so:so + TCW], in_=b[:, 0:TCW], func=AF.Silu),
                                reads=[(b.name, 0)], writes=[("sU", so)], tag="silu")
                        else:
                            p.op("vector", lambda e, b=b, tc=tc, ei=ei: e.tensor_mul(
                                escr[:, ei * TCW:(ei + 1) * TCW], b[:, 0:TCW],
                                B_sb[:, tc * TCW:(tc + 1) * TCW]),
                                reads=[(b.name, 0)] + Bkeys,
                                writes=[("escr", ei)], tag="ub")
                            p.op("scalar", lambda e, so=so, ei=ei: e.activation(
                                out=sUg[:, so:so + TCW], in_=escr[:, ei * TCW:(ei + 1) * TCW],
                                func=AF.Silu),
                                reads=[("escr", ei)], writes=[("sU", so)], tag="silu")
                vb = {}
                for i2, hc in enumerate(g):
                    for tc in range(TC):
                        vb[(hc, tc)] = psum[((gi % 2) * 4 + 4 + i2 * TC + tc) % 8]
                for kc in range(KC):
                    for hc in g:
                        for tc in range(TC):
                            b = vb[(hc, tc)]
                            p.op("tensor", lambda e, kc=kc, hc=hc, tc=tc, b=b: e.matmul(
                                b[:, 0:TCW], w3b[:, kc * HID + hc * 128: kc * HID + (hc + 1) * 128],
                                xln2[:, kc * OT + tc * TCW: kc * OT + (tc + 1) * TCW],
                                start=(kc == 0), stop=(kc == KC - 1)),
                                reads=[("xln2", kc), wkeys13["w3"][kc * len(wkeys13["w3"]) // KC]],
                                writes=[(b.name, 0)], tag="V")
                for i2, hc in enumerate(g):
                    for tc in range(TC):
                        b = vb[(hc, tc)]
                        so = tc * TCW
                        ei = (gi * TC + tc + 1) % 2
                        if shared:
                            p.op("vector", lambda e, b=b, so=so, hc=hc, tc=tc: e.tensor_mul(
                                Hb[:, hc * OT + tc * TCW: hc * OT + (tc + 1) * TCW],
                                sUg[:, so:so + TCW], b[:, 0:TCW]),
                                reads=[("sU", so), (b.name, 0)],
                                writes=[("H", (hc, tc))], tag="H")
                        else:
                            p.op("vector", lambda e, b=b, tc=tc, ei=ei: e.tensor_mul(
                                escr[:, ei * TCW:(ei + 1) * TCW], b[:, 0:TCW],
                                B_sb[:, tc * TCW:(tc + 1) * TCW]),
                                reads=[(b.name, 0)] + Bkeys,
                                writes=[("escr", ei)], tag="vb")
                            p.op("vector", lambda e, so=so, hc=hc, tc=tc, ei=ei: e.tensor_mul(
                                Hb[:, hc * OT + tc * TCW: hc * OT + (tc + 1) * TCW],
                                sUg[:, so:so + TCW], escr[:, ei * TCW:(ei + 1) * TCW]),
                                reads=[("sU", so), ("escr", ei)],
                                writes=[("H", (hc, tc))], tag="H")
            if eidx < E:
                wkeys13 = load_w13(eidx + 1)
            w2src = sw2T_h if shared else w2T_h[eidx]
            for oc in range(KC):
                wi2 = oc % 3
                sl, sk = SLOT[p.stg_rr % len(SLOT)]
                p.stg_rr += 1
                p.dma("sync", lambda e, oc=oc, sl=sl, w2src=w2src, hc_n=hc_n: e.dma_start(
                    out=stg[:, sl][:, 0:hc_n * 128].rearrange("p (k n) -> p k n", n=128),
                    in_=w2src[oc][0:hc_n].rearrange("k p n -> p k n")),
                    writes=[(sk, 0)], tag="w2dma")
                p.op("scalar", lambda e, wi2=wi2, sl=sl, hc_n=hc_n: e.copy(
                    out=w2b[:, wi2 * HC * 128: wi2 * HC * 128 + hc_n * 128],
                    in_=stg[:, sl][:, 0:hc_n * 128]),
                    reads=[(sk, 0)], writes=[("w2b", wi2)], tag="w2cast")
                for tc in range(TC):
                    bank = psum[(oc * TC + tc) % 2]
                    for hc in range(hc_n):
                        p.op("tensor", lambda e, oc=oc, tc=tc, hc=hc, bank=bank, wi2=wi2: e.matmul(
                            bank[:, 0:TCW], w2b[:, wi2 * HC * 128 + hc * 128: wi2 * HC * 128 + (hc + 1) * 128],
                            Hb[:, hc * OT + tc * TCW: hc * OT + (tc + 1) * TCW],
                            start=(hc == 0), stop=(hc == hc_n - 1)),
                            reads=[("H", (hc, tc)), ("w2b", wi2)],
                            writes=[(bank.name, 0)], tag="W2")
                    p.op("vector", lambda e, oc=oc, tc=tc, bank=bank: e.tensor_add(
                        X[:, oc * OT + tc * TCW: oc * OT + (tc + 1) * TCW],
                        X[:, oc * OT + tc * TCW: oc * OT + (tc + 1) * TCW], bank[:, 0:TCW]),
                        reads=[(bank.name, 0), ("X", (oc, tc))],
                        writes=[("X", (oc, tc))], tag="Xacc")

        for kc in range(KC):
            p.dma("sync", lambda e, kc=kc: e.dma_start(
                out=out_h[kc], in_=X[:, kc * OT:(kc + 1) * OT]),
                reads=[("X", (kc, tc)) for tc in range(TC)], slot="out", tag="store")
        p.wait_all("sync")
        p.build(es)
    return nc


def pack_inputs(cfg, inputs):
    D, S = cfg["D"], cfg["S"]
    NH, NKV, E = cfg["NH"], cfg["NKV"], cfg["E"]
    HID, SHID = cfg["HID"], cfg["SHID"]
    KC, SB = D // 128, S // 128
    NJ = SB // 2
    OT = NJ * 128
    QD, KD, VD = NH * 256, NKV * 256, NKV * 256
    QC, KCo, VC = QD // 128, KD // 128, VD // 128
    NOC = QC + KCo + VC
    HC, SHC = HID // 128, SHID // 128

    x = np.ascontiguousarray(inputs["L0_attn_hidden_states"], dtype=np.float32)
    wqkv = np.asarray(inputs["L0_attn_Wqkv_w"], dtype=np.float32)
    bqkv = np.asarray(inputs["L0_attn_Wqkv_b"], dtype=np.float32)
    wo = np.asarray(inputs["L0_attn_out_w"], dtype=np.float32)
    bo = np.asarray(inputs["L0_attn_out_b"], dtype=np.float32)
    lq1 = np.asarray(inputs["L0_attn_lambda_q1"], dtype=np.float32)
    lk1 = np.asarray(inputs["L0_attn_lambda_k1"], dtype=np.float32)
    lq2 = np.asarray(inputs["L0_attn_lambda_q2"], dtype=np.float32)
    lk2 = np.asarray(inputs["L0_attn_lambda_k2"], dtype=np.float32)
    subw = np.asarray(inputs["L0_attn_subln_w"], dtype=np.float32)
    gw = np.asarray(inputs["L0_ffn_gate_w"], dtype=np.float32)
    w1 = np.asarray(inputs["L0_ffn_w1"], dtype=np.float32)
    w2 = np.asarray(inputs["L0_ffn_w2"], dtype=np.float32)
    w3 = np.asarray(inputs["L0_ffn_w3"], dtype=np.float32)
    sw1 = np.asarray(inputs["L0_ffn_sw1"], dtype=np.float32)
    sw2 = np.asarray(inputs["L0_ffn_sw2"], dtype=np.float32)
    sw3 = np.asarray(inputs["L0_ffn_sw3"], dtype=np.float32)

    import ml_dtypes
    C = np.ascontiguousarray

    wqkvT = C(wqkv.reshape(NOC, 128, KC, 128).transpose(0, 2, 3, 1))
    bqkv_p = C(bqkv.reshape(NOC, 128).T)
    woT = C(wo.reshape(KC, 128, QC, 128).transpose(0, 2, 3, 1))
    bo_p = C(bo.reshape(KC, 128).T)
    lam_p = C(np.stack([lq1, lk1, lq2, lk2], axis=1))
    subw_p = C(np.broadcast_to(subw[None, :], (128, 256)))
    gwT = C(gw.T.reshape(KC, 128, 8))
    w1T = C(w1.transpose(0, 2, 1).reshape(E, KC, 128, HID))
    w3T = C(w3.transpose(0, 2, 1).reshape(E, KC, 128, HID))
    w2T = C(w2.reshape(E, KC, 128, HC, 128).transpose(0, 1, 3, 4, 2))
    sw1T = C(sw1.T.reshape(KC, 128, SHID))
    sw3T = C(sw3.T.reshape(KC, 128, SHID))
    sw2T = C(sw2.reshape(KC, 128, SHC, 128).transpose(0, 2, 3, 1))
    idf = np.eye(128, dtype=np.float32)
    idb = np.eye(128, dtype=ml_dtypes.bfloat16)

    shared = dict(wqkvT=wqkvT, bqkv=bqkv_p, woT=woT, bo=bo_p, lam_in=lam_p,
                  subw_bc=subw_p, ident_f=idf, ident_b=idb, gwT=gwT,
                  w1T=w1T, w3T=w3T, w2T=w2T, sw1T=sw1T, sw3T=sw3T, sw2T=sw2T)

    in_maps = []
    own_tok = []
    for c in range(cfg["CORES"]):
        b, par = c // 2, c % 2
        qbs = own_qbs(par, SB)
        toks = np.concatenate([np.arange(qb * 128, (qb + 1) * 128) for qb in qbs])
        own_tok.append((b, toks))
        xb = x[b]
        x_fm = C(xb.T.reshape(KC, 128, S))
        x_ow = C(xb[toks].T.reshape(KC, 128, OT))
        mk = np.zeros((NJ, 2, 128, 128), dtype=ml_dtypes.bfloat16)
        for j, qb in enumerate(qbs):
            for t in range(2):
                kcb = 2 * j + t
                if kcb < qb:
                    mk[j, t] = 1
                elif kcb == qb:
                    ki = np.arange(128)[:, None]
                    qi = np.arange(128)[None, :]
                    mk[j, t] = (ki <= qi).astype(ml_dtypes.bfloat16)
        im = dict(shared)
        im.update(x_fm=x_fm, x_own=x_ow, masks=mk)
        in_maps.append(im)
    return in_maps, own_tok


_BUILD_CACHE = {}


def kernel(**inputs):
    cfg = make_cfg(mini=False)
    key = "full"
    if key not in _BUILD_CACHE:
        _BUILD_CACHE[key] = build_program(cfg)
    nc = _BUILD_CACHE[key]
    in_maps, own_tok = pack_inputs(cfg, inputs)
    res = run_bass_kernel_spmd(nc, in_maps, core_ids=list(range(cfg["CORES"])))
    D, S = cfg["D"], cfg["S"]
    out = np.empty((cfg["B"], S, D), dtype=np.float32)
    for c in range(cfg["CORES"]):
        b, toks = own_tok[c]
        shard = np.asarray(res.results[c]["out_fm"])
        out[b, toks, :] = shard.transpose(2, 0, 1).reshape(len(toks), D)
    return out


# version 12 (final)
# speedup vs baseline: 1.0843x; 1.0235x over previous
import math
import numpy as np
from contextlib import ExitStack
from collections import defaultdict

import concourse.bass as bass
import concourse.mybir as mybir
from concourse.bass_utils import run_bass_kernel_spmd

F32 = mybir.dt.float32
BF16 = mybir.dt.bfloat16
AF = mybir.ActivationFunctionType
ALU = mybir.AluOpType
AX = mybir.AxisListType

ENGINES = ("tensor", "vector", "scalar", "gpsimd", "sync")

LAMBDA_INIT = 0.8 - 0.6 * math.exp(-0.3 * 1)
EXP_BIAS = 24.0
LN_EPS = 1e-5
SUBLN_EPS = 1e-5


class Prog:
    def __init__(self, nc):
        self.nc = nc
        self.actions = {e: [] for e in ENGINES}
        self.count = {e: 0 for e in ENGINES}
        self.dma_slot_count = {}
        self.marked = {e: set() for e in ENGINES}
        self.writer = {}
        self.readers = defaultdict(list)
        self.water = {e: defaultdict(int) for e in ENGINES}

    def _need(self, deps, tok):
        kind, eng, val = tok
        key = (kind, eng)
        if val > deps.get(key, 0):
            deps[key] = val

    def _gather(self, engine, reads, writes):
        deps = {}
        for b in reads:
            w = self.writer.get(b)
            if w is not None:
                self._need(deps, w)
        for b in writes:
            w = self.writer.get(b)
            if w is not None:
                self._need(deps, w)
            for r in self.readers.get(b, ()):
                self._need(deps, r)
        waits = []
        for (kind, eng), val in deps.items():
            if kind == "eng" and eng == engine == "tensor":
                continue
            if self.water[engine][(kind, eng)] >= val:
                continue
            if kind == "eng" and eng == engine:
                assert val <= self.count[engine], "self-wait on future op"
            waits.append(((kind, eng), val))
            self.water[engine][(kind, eng)] = val
            if kind == "eng":
                self.marked[eng].add(val)
        return waits

    def op(self, engine, fn, reads=(), writes=(), tag=""):
        waits = self._gather(engine, reads, writes)
        self.count[engine] += 1
        tok = ("eng", engine, self.count[engine])
        for b in writes:
            self.writer[b] = tok
            self.readers[b] = []
        for b in reads:
            self.readers[b].append(tok)
        self.actions[engine].append(dict(waits=waits, fn=fn, kind="eng",
                                         tag=tag, seq=self.count[engine]))
        return tok

    def dma(self, engine, fn, reads=(), writes=(), tag="", slot=None):
        if slot is None:
            slot = writes[0] if writes else "out"
        key = (engine, slot)
        waits = self._gather(engine, reads, writes)
        prev = self.dma_slot_count.get(key, 0)
        if prev > 0 and self.water[engine][("dma", key)] < prev:
            waits.append((("dma", key), prev))
            self.water[engine][("dma", key)] = prev
        self.dma_slot_count[key] = prev + 16
        tok = ("dma", key, self.dma_slot_count[key])
        for b in writes:
            self.writer[b] = tok
            self.readers[b] = []
        for b in reads:
            self.readers[b].append(tok)
        self.actions[engine].append(dict(waits=waits, fn=fn, kind="dma",
                                         tag=tag, seq=self.dma_slot_count[key],
                                         slot=key))
        return tok

    def fence_all(self, tag="fence", dma=False):
        for engine in ENGINES:
            waits = []
            for e in ENGINES:
                if self.count[e] > 0 \
                        and self.water[engine][("eng", e)] < self.count[e]:
                    waits.append((("eng", e), self.count[e]))
                    self.water[engine][("eng", e)] = self.count[e]
                    self.marked[e].add(self.count[e])
            if dma:
                for k, v in self.dma_slot_count.items():
                    if self.water[engine][("dma", k)] < v:
                        waits.append((("dma", k), v))
                        self.water[engine][("dma", k)] = v
            if waits:
                self.actions[engine].append(dict(
                    waits=waits, fn=lambda e: None, kind="none", tag=tag, seq=-1))

    def wait_all(self, engine, tag="final"):
        waits = []
        for e in ENGINES:
            if e != engine and self.count[e] > 0 \
                    and self.water[engine][("eng", e)] < self.count[e]:
                waits.append((("eng", e), self.count[e]))
                self.water[engine][("eng", e)] = self.count[e]
                self.marked[e].add(self.count[e])
        for k, v in self.dma_slot_count.items():
            if self.water[engine][("dma", k)] < v:
                waits.append((("dma", k), v))
                self.water[engine][("dma", k)] = v
        self.actions[engine].append(dict(waits=waits, fn=lambda e: None,
                                         kind="none", tag=tag, seq=-1))

    def build(self, es):
        nc = self.nc
        inc_val = {}
        for e in ENGINES:
            marked = sorted(self.marked[e])
            m, c = {}, 0
            it = iter(marked)
            nxt = next(it, None)
            for s in range(1, self.count[e] + 1):
                if nxt == s:
                    c += 1
                    nxt = next(it, None)
                m[s] = c
            inc_val[e] = m
        sems = {}
        for e in ENGINES:
            if self.marked[e]:
                sems[("eng", e)] = es.enter_context(nc.semaphore(f"se_{e}"))
        for i, k in enumerate(self.dma_slot_count):
            sems[("dma", k)] = es.enter_context(nc.semaphore(f"sd_{i}"))
        block = es.enter_context(nc.Block())

        def run_stream(eng_name):
            def _(eng):
                own = sems.get(("eng", eng_name))
                mark = self.marked[eng_name]
                for a in self.actions[eng_name]:
                    for (kind, pk), val in a["waits"]:
                        if kind == "eng":
                            eng.wait_ge(sems[("eng", pk)], inc_val[pk][val])
                        else:
                            eng.wait_ge(sems[("dma", pk)], val)
                    inst = a["fn"](eng)
                    if a["kind"] == "eng" and a["seq"] in mark:
                        inst.then_inc(own, 1)
                    elif a["kind"] == "dma":
                        inst.then_inc(sems[("dma", a["slot"])], 16)
            return _

        for e in ENGINES:
            if self.actions[e]:
                getattr(block, e)(run_stream(e))
        return sems


class Overlay:

    def __init__(self, nc, es, total_bytes, name="ovl"):
        self.nc = nc
        self.backing = es.enter_context(
            nc.sbuf_tensor(name, [128, total_bytes], mybir.dt.uint8))
        self.base = nc.lookup_mloc(self.backing).addr
        self.total = total_bytes

    def at(self, name, off, shape, dtype):
        assert off % 32 == 0, (name, off)
        per_part = int(np.prod(shape[1:])) * mybir.dt.size(dtype)
        assert off + per_part <= self.total, (name, off, per_part, self.total)
        h = self.nc.virtual_sbuf_tensor(name, shape, dtype)
        self.nc._tensor(h.name, h.shape, h.dtype, type="SB")
        mloc = self.nc.lookup_mloc(h)
        mloc.allocated = not self.nc.target_bir_lowering
        mloc.addr = self.base + off
        return h


def make_cfg(mini=False):
    if mini:
        return dict(D=512, S=1024, NH=4, NKV=2, HID=256, SHID=256, E=8,
                    B=1, CORES=2, debug=True)
    return dict(D=2048, S=2048, NH=8, NKV=4, HID=1024, SHID=1024, E=8,
                B=4, CORES=8, debug=False)


def own_qbs(p, SB):
    out = []
    for a in range(SB // 4):
        out += ([4 * a, 4 * a + 3] if p == 0 else [4 * a + 1, 4 * a + 2])
    return sorted(out)


def build_program(cfg):
    D, S = cfg["D"], cfg["S"]
    NH, NKV, E = cfg["NH"], cfg["NKV"], cfg["E"]
    HID, SHID = cfg["HID"], cfg["SHID"]
    KC, SB = D // 128, S // 128
    NJ = SB // 2
    OT = NJ * 128
    QD, KD, VD = NH * 256, NKV * 256, NKV * 256
    QC, KCo, VC = QD // 128, KD // 128, VD // 128
    NOC = QC + KCo + VC
    HC, SHC = HID // 128, SHID // 128
    TCW = min(512, OT)
    TC = OT // TCW
    VS = NKV * 257
    HTOK = min(1024, S)
    NHALF = S // HTOK
    OH = min(512, OT)
    NOH = OT // OH
    SCALE = 128.0 ** -0.5
    dbg = cfg["debug"]

    nc = bass.Bass("TRN2", target_bir_lowering=False,
                   detect_race_conditions=False)
    p = Prog(nc)
    p.stg_rr = 0

    x_fm = nc.dram_tensor("x_fm", [KC, 128, S], F32, kind="ExternalInput").ap()
    x_own = nc.dram_tensor("x_own", [KC, 128, OT], F32, kind="ExternalInput").ap()
    wqkvT = nc.dram_tensor("wqkvT", [NOC, KC, 128, 128], F32, kind="ExternalInput").ap()
    bqkv_h = nc.dram_tensor("bqkv", [128, NOC], F32, kind="ExternalInput").ap()
    woT = nc.dram_tensor("woT", [KC, QC, 128, 128], F32, kind="ExternalInput").ap()
    bo_h = nc.dram_tensor("bo", [128, KC], F32, kind="ExternalInput").ap()
    lam_in = nc.dram_tensor("lam_in", [128, 4], F32, kind="ExternalInput").ap()
    subw_bc = nc.dram_tensor("subw_bc", [128, 256], F32, kind="ExternalInput").ap()
    masks_h = nc.dram_tensor("masks", [NJ, 2, 128, 128], BF16, kind="ExternalInput").ap()
    ident_f = nc.dram_tensor("ident_f", [128, 128], F32, kind="ExternalInput").ap()
    ident_b = nc.dram_tensor("ident_b", [128, 128], BF16, kind="ExternalInput").ap()
    gwT_h = nc.dram_tensor("gwT", [KC, 128, 8], F32, kind="ExternalInput").ap()
    w1T_h = nc.dram_tensor("w1T", [E, KC, 128, HID], F32, kind="ExternalInput").ap()
    w3T_h = nc.dram_tensor("w3T", [E, KC, 128, HID], F32, kind="ExternalInput").ap()
    w2T_h = nc.dram_tensor("w2T", [E, KC, HC, 128, 128], F32, kind="ExternalInput").ap()
    sw1T_h = nc.dram_tensor("sw1T", [KC, 128, SHID], F32, kind="ExternalInput").ap()
    sw3T_h = nc.dram_tensor("sw3T", [KC, 128, SHID], F32, kind="ExternalInput").ap()
    sw2T_h = nc.dram_tensor("sw2T", [KC, SHC, 128, 128], F32, kind="ExternalInput").ap()
    out_h = nc.dram_tensor("out_fm", [KC, 128, OT], F32, kind="ExternalOutput").ap()
    dbg_h = {}
    if dbg:
        for nm, sh in [("d_xln", [KC, 128, S]), ("d_q", [QC, 128, OT]),
                       ("d_k", [KCo, 128, S]), ("d_v", [128, SB * VS]),
                       ("d_otm", [128, NJ * QD]), ("d_Xmid", [KC, 128, OT]),
                       ("d_xln2", [KC, 128, OT]), ("d_bfm", [128, OT])]:
            dbg_h[nm] = nc.dram_tensor(nm, sh, F32, kind="ExternalOutput").ap()

    es = ExitStack()
    with es:
        consts = es.enter_context(nc.sbuf_tensor("consts", [128, 8], F32))
        lam_sb = es.enter_context(nc.sbuf_tensor("lam_sb", [128, 8], F32))
        onesb = es.enter_context(nc.sbuf_tensor("onesb", [128, 128], BF16))
        onesf = es.enter_context(nc.sbuf_tensor("onesf", [128, 128], F32))
        selS = es.enter_context(nc.sbuf_tensor("selS", [128, 128], F32))
        id8 = es.enter_context(nc.sbuf_tensor("id8", [128, 8], F32))
        stg = es.enter_context(nc.sbuf_tensor("stg", [128, 2 * 1024], F32))
        SLOT = [(slice(i * 1024, (i + 1) * 1024), f"stg{i}") for i in range(2)]
        psum = [es.enter_context(nc.psum_tensor(f"pb{i}", [128, 512], F32))
                for i in range(8)]
        PK = [f"pb{i}" for i in range(8)]

        def psum_alias(i, name, dtype):
            h = nc.virtual_psum_tensor(name, [128, 512], dtype)
            nc._tensor(h.name, h.shape, h.dtype, type="PSUM")
            mloc = nc.lookup_mloc(h)
            src = nc.lookup_mloc(psum[i])
            mloc.allocated = not nc.target_bir_lowering
            mloc.addr = src.addr
            mloc.bank = src.bank
            return h
        psumb = [psum_alias(i, f"pbb{i}", BF16) for i in range(8)]

        KB = 1024

        def rnd(x):
            return (int(x) + 31) & ~31
        WSTN = max(KC, QC) * 128
        HTOKB = HTOK
        specs = [
            ("xln",     KC * S * 2,        1, 2),
            ("x_half",  KC * HTOK * 4,     1, 1),
            ("statA_m", HTOK * 4,          1, 3),
            ("statA_s", HTOK * 4,          1, 3),
            ("statA_t", HTOK * 4,          1, 3),
            ("statA_xb", HTOK * 4,         1, 3),
            ("statA_sq", HTOK * 4,         1, 3),
            ("wstrip",  2 * WSTN * 2,      1, 7),
            ("masks",   NJ * 2 * 128 * 2,  0, 4),
            ("idf",     128 * 4,           0, 9),
            ("idb",     128 * 2,           0, 6),
            ("weff",    256 * 4,           0, 6),
            ("bqkv",    NOC * 4,           0, 3),
            ("bo",      KC * 4,            0, 7),
            ("vfm",     2 * S * 2,         2, 2),
            ("kbuf",    KCo * S * 2,       2, 4),
            ("vbuf",    SB * VS * 2,       2, 4),
            ("xo_half", KC * OH * 4,       3, 3),
            ("xow",     KC * OT * 2,       3, 3),
            ("qbuf",    QC * OT * 2,       3, 4),
            ("psb",     1536 * 2,          4, 4),
            ("osc",     max(QD, 2048) * 4,  4, 6),
            ("rsc",     16 * 4,            4, 6),
            ("ms_sb",   NJ * NH * 4,       4, 6),
            ("otm",     NJ * QD * 2,       4, 6),
            ("O_fm",    QC * OT * 2,       5, 7),
            ("Xres",    KC * OT * 4,       7, 11),
            ("w1b",     KC * HID * 2,      7, 10),
            ("w3b",     KC * HID * 2,      7, 10),
            ("statB_m", OT * 4,            8, 8),
            ("statB_s", OT * 4,            8, 8),
            ("statB_t", OT * 4,            8, 8),
            ("statB_xb", OT * 4,           8, 8),
            ("statB_sq", OT * 4,           8, 8),
            ("xln2",    KC * OT * 2,       8, 10),
            ("gwb",     KC * 8 * 2,        7, 10),
            ("xstg",    2 * TCW * 4,       7, 7),
            ("scr8",    96 * 4,            9, 9),
            ("b_fm",    OT * 4,            9, 10),
            ("Hb",      HC * OT * 2,       10, 10),
            ("B_sb",    OT * 4,            10, 10),
            ("w2b",     3 * HC * 128 * 2,  10, 10),
            ("sUg",     OT * 4,            10, 10),
            ("escr",    2 * TCW * 4,       10, 10),
        ]
        def pack(order):
            placed, offs = [], {}
            for name, size, lo, hi in order:
                size = rnd(size)
                cands = sorted({0} | {o + sz for o, sz, l, h in placed
                                if not (h < lo or l > hi)})
                pos = None
                for cand in cands:
                    if all(h < lo or l > hi or cand + size <= o or cand >= o + sz
                           for o, sz, l, h in placed):
                        pos = cand
                        break
                offs[name] = pos
                placed.append((pos, size, lo, hi))
            return offs, rnd(max(o + sz for o, sz, _, _ in placed))
        best = None
        for keyf in (lambda t: -t[1],
                     lambda t: (-(t[3] - t[2]), -t[1]),
                     lambda t: (t[2], -t[1]),
                     lambda t: (-(t[3] - t[2]) * 1000000 - t[1],)):
            offs, tot = pack(sorted(specs, key=keyf))
            if best is None or tot < best[1]:
                best = (offs, tot)
        off, TOTAL = best
        ovl = Overlay(nc, es, TOTAL)

        xln = ovl.at("xln", off["xln"], [128, KC * S], BF16)
        x_half = ovl.at("x_half", off["x_half"], [128, KC * HTOK], F32)
        mA = ovl.at("mA", off["statA_m"], [128, HTOK], F32)
        sA = ovl.at("sA", off["statA_s"], [128, HTOK], F32)
        tA = ovl.at("tA", off["statA_t"], [128, HTOK], F32)
        xbA = ovl.at("xbA", off["statA_xb"], [128, 2 * HTOK], BF16)
        sqA = ovl.at("sqA", off["statA_sq"], [128, 2 * HTOK], BF16)
        mB = ovl.at("mB", off["statB_m"], [128, OT], F32)
        sB = ovl.at("sB", off["statB_s"], [128, OT], F32)
        tB = ovl.at("tB", off["statB_t"], [128, OT], F32)
        xbB = ovl.at("xbB", off["statB_xb"], [128, 2 * OT], BF16)
        sqB = ovl.at("sqB", off["statB_sq"], [128, 2 * OT], BF16)
        wstrip = ovl.at("wstrip", off["wstrip"], [128, 2 * WSTN], BF16)
        WST = [(slice(i * WSTN, (i + 1) * WSTN), f"wst{i}") for i in range(2)]
        mask_sb = ovl.at("mask_sb", off["masks"], [128, NJ * 2 * 128], BF16)
        idf = ovl.at("idf", off["idf"], [128, 128], F32)
        idb = ovl.at("idb", off["idb"], [128, 128], BF16)
        weff = ovl.at("weff", off["weff"], [128, 256], F32)
        bqkv = ovl.at("bqkv_sb", off["bqkv"], [128, NOC], F32)
        bo_sb = ovl.at("bo_sb", off["bo"], [128, KC], F32)
        vfm = ovl.at("vfm", off["vfm"], [128, 2 * S], BF16)
        kbuf = ovl.at("kbuf", off["kbuf"], [128, KCo * S], BF16)
        vbuf = ovl.at("vbuf", off["vbuf"], [128, SB * VS], BF16)
        xo_half = ovl.at("xo_half", off["xo_half"], [128, KC * OH], F32)
        xow = ovl.at("xow", off["xow"], [128, KC * OT], BF16)
        qbuf = ovl.at("qbuf", off["qbuf"], [128, QC * OT], BF16)
        psb = ovl.at("psb", off["psb"], [128, 1536], BF16)
        osc = ovl.at("osc", off["osc"], [128, max(QD, 2048)], F32)
        rsc = ovl.at("rsc", off["rsc"], [128, 16], F32)
        ms_sb = ovl.at("ms_sb", off["ms_sb"], [128, NJ * NH], F32)
        otm = ovl.at("otm", off["otm"], [128, NJ * QD], BF16)
        O_fm = ovl.at("O_fm", off["O_fm"], [128, QC * OT], BF16)
        X = ovl.at("Xres", off["Xres"], [128, KC * OT], F32)
        w1b = ovl.at("w1b", off["w1b"], [128, KC * HID], BF16)
        w3b = ovl.at("w3b", off["w3b"], [128, KC * HID], BF16)
        xln2 = ovl.at("xln2", off["xln2"], [128, KC * OT], BF16)
        gwb = ovl.at("gwb", off["gwb"], [128, KC * 8], BF16)
        xstg = ovl.at("xstg", off["xstg"], [128, 2 * TCW], F32)
        scr8 = ovl.at("scr8", off["scr8"], [128, 96], F32)
        b_fm = ovl.at("b_fm", off["b_fm"], [128, OT], F32)
        Hb = ovl.at("Hb", off["Hb"], [128, HC * OT], BF16)
        B_sb = ovl.at("B_sb", off["B_sb"], [128, OT], F32)
        w2b = ovl.at("w2b", off["w2b"], [128, 3 * HC * 128], BF16)
        sUg = ovl.at("sUg", off["sUg"], [128, OT], F32)
        escr = ovl.at("escr", off["escr"], [128, 2 * TCW], F32)

        p.dma("scalar", lambda e: e.dma_start(out=idf[:, :], in_=ident_f[:, :]), writes=[("idf", 0)])
        p.dma("scalar", lambda e: e.dma_start(out=id8[:, :], in_=ident_f[:, 0:8]), writes=[("id8", 0)])
        p.dma("scalar", lambda e: e.dma_start(out=idb[:, :], in_=ident_b[:, :]), writes=[("idb", 0)])
        p.dma("scalar", lambda e: e.dma_start(out=lam_sb[:, 0:4], in_=lam_in[:, :]), writes=[("lamin", 0)])
        p.dma("scalar", lambda e: e.dma_start(out=weff[:, :], in_=subw_bc[:, :]), writes=[("weff", 0)])
        p.dma("scalar", lambda e: e.dma_start(out=bqkv[:, :], in_=bqkv_h[:, :]), writes=[("bqkv", 0)])
        p.dma("scalar", lambda e: e.dma_start(out=bo_sb[:, :], in_=bo_h[:, :]), writes=[("bo", 0)])
        p.dma("scalar", lambda e: e.dma_start(
            out=mask_sb[:, :].rearrange("p (j t n) -> p j t n", j=NJ, t=2),
            in_=masks_h.rearrange("j t p n -> p j t n")), writes=[("masks", 0)])
        p.op("vector", lambda e: e.memset(onesb[:, :], 1.0), writes=[("onesb", 0)])
        p.op("vector", lambda e: e.memset(onesf[:, :], 1.0), writes=[("onesf", 0)])
        p.op("vector", lambda e: e.memset(consts[:, 0:1], -EXP_BIAS), writes=[("c0", 0)])
        p.op("vector", lambda e: e.tensor_scalar_mul(weff[:, :], weff[:, :], 1.0 - LAMBDA_INIT),
             reads=[("weff", 0)], writes=[("weff", 0)])
        p.op("vector", lambda e: e.tensor_mul(lam_sb[:, 4:5], lam_sb[:, 0:1], lam_sb[:, 1:2]),
             reads=[("lamin", 0)], writes=[("lp", 0)])
        p.op("vector", lambda e: e.tensor_mul(lam_sb[:, 5:6], lam_sb[:, 2:3], lam_sb[:, 3:4]),
             reads=[("lamin", 0)], writes=[("lp", 1)])
        p.op("tensor", lambda e: e.matmul(psum[0][:, 0:1], onesf[:, :], lam_sb[:, 4:5], start=True, stop=True),
             reads=[("onesf", 0), ("lp", 0)], writes=[(PK[0], 0)])
        p.op("tensor", lambda e: e.matmul(psum[0][:, 1:2], onesf[:, :], lam_sb[:, 5:6], start=True, stop=True),
             reads=[("onesf", 0), ("lp", 1)], writes=[(PK[0], 1)])
        p.op("scalar", lambda e: e.activation(out=lam_sb[:, 6:8], in_=psum[0][:, 0:2], func=AF.Exp),
             reads=[(PK[0], 0), (PK[0], 1)], writes=[("lexp", 0)])
        p.op("vector", lambda e: e.tensor_sub(lam_sb[:, 4:5], lam_sb[:, 6:7], lam_sb[:, 7:8]),
             reads=[("lexp", 0)], writes=[("lam", 0)])
        p.op("vector", lambda e: e.tensor_scalar_add(lam_sb[:, 4:5], lam_sb[:, 4:5], LAMBDA_INIT),
             reads=[("lam", 0)], writes=[("lam", 0)])

        def emit_ln(kp, ncols, xsrc_ap, xsrc_keys, dst_write, bufs):
            m_sb, s_sb, t_sb, xb_sb, sq_sb = bufs
            NT5 = ncols // 512 if ncols >= 512 else 1
            W5 = min(512, ncols)
            for kc in range(KC):
                pb = (kc % 2) * ncols
                p.op("scalar", lambda e, kc=kc, pb=pb: e.copy(
                    out=xb_sb[:, pb:pb + ncols], in_=xsrc_ap(kc)),
                    reads=xsrc_keys(kc), writes=[("LNxbf", kc % 2)])
                p.op("vector", lambda e, kc=kc, pb=pb: e.tensor_mul(
                    sq_sb[:, pb:pb + ncols], xb_sb[:, pb:pb + ncols],
                    xb_sb[:, pb:pb + ncols]),
                    reads=[("LNxbf", kc % 2)], writes=[("LNsqb", kc % 2)])
                for t5 in range(NT5):
                    sl = slice(pb + t5 * W5, pb + (t5 + 1) * W5)
                    p.op("tensor", lambda e, kc=kc, sl=sl, t5=t5: e.matmul(
                        psum[t5][:, 0:W5], onesb[:, :], xb_sb[:, sl],
                        start=(kc == 0), stop=(kc == KC - 1)),
                        reads=[("LNxbf", kc % 2), ("onesb", 0)], writes=[(PK[t5], 0)])
                    p.op("tensor", lambda e, kc=kc, sl=sl, t5=t5: e.matmul(
                        psum[4 + t5][:, 0:W5], onesb[:, :], sq_sb[:, sl],
                        start=(kc == 0), stop=(kc == KC - 1)),
                        reads=[("LNsqb", kc % 2), ("onesb", 0)], writes=[(PK[4 + t5], 0)])
            for t5 in range(NT5):
                sl = slice(t5 * W5, (t5 + 1) * W5)
                p.op("vector", lambda e, sl=sl, t5=t5: e.tensor_scalar_mul(
                    m_sb[:, sl], psum[t5][:, 0:W5], 1.0 / D),
                    reads=[(PK[t5], 0)], writes=[("LNm", t5)])
                p.op("vector", lambda e, sl=sl, t5=t5: e.tensor_scalar_mul(
                    t_sb[:, sl], psum[4 + t5][:, 0:W5], 1.0 / D),
                    reads=[(PK[4 + t5], 0)], writes=[("LNt", t5)])
            mk = [("LNm", t5) for t5 in range(NT5)]
            tk = [("LNt", t5) for t5 in range(NT5)]
            cs = slice(0, ncols)
            p.op("vector", lambda e: e.tensor_mul(s_sb[:, cs], m_sb[:, cs], m_sb[:, cs]),
                 reads=mk, writes=[("LNs", 0)])
            p.op("vector", lambda e: e.tensor_sub(t_sb[:, cs], t_sb[:, cs], s_sb[:, cs]),
                 reads=tk + [("LNs", 0)], writes=tk)
            p.op("vector", lambda e: e.tensor_scalar_add(t_sb[:, cs], t_sb[:, cs], LN_EPS),
                 reads=tk, writes=tk)
            p.op("scalar", lambda e: e.activation(out=s_sb[:, cs], in_=t_sb[:, cs], func=AF.Sqrt),
                 reads=tk, writes=[("LNs", 0)])
            p.op("vector", lambda e: e.reciprocal(s_sb[:, cs], s_sb[:, cs]),
                 reads=[("LNs", 0)], writes=[("LNs", 0)])
            for kc in range(KC):
                p.op("vector", lambda e, kc=kc: e.tensor_sub(
                    t_sb[:, cs], xsrc_ap(kc), m_sb[:, cs]),
                    reads=xsrc_keys(kc) + mk, writes=[("LNt", 0)])
                dst_write(kc)

        def load_strip_hwdge(oc, wi, src, nk, cast_eng="scalar"):
            wsl, wk = WST[wi]
            ncol = nk * 128
            keys = []
            for half in range((ncol + 1023) // 1024):
                w = min(1024, ncol - half * 1024)
                sl, sk = SLOT[p.stg_rr % len(SLOT)]
                p.stg_rr += 1
                klo, khi = half * 8, half * 8 + w // 128
                p.dma("sync", lambda e, oc=oc, sl=sl, w=w, klo=klo, khi=khi, src=src: e.dma_start(
                    out=stg[:, sl][:, 0:w].rearrange("p (k n) -> p k n", n=128),
                    in_=src[oc, klo:khi].rearrange("k p n -> p k n")),
                    writes=[(sk, 0)], tag="wdma")
                if cast_eng == "scalar":
                    p.op("scalar", lambda e, sl=sl, wsl=wsl, half=half, w=w: e.copy(
                        out=wstrip[:, wsl][:, half * 1024:half * 1024 + w],
                        in_=stg[:, sl][:, 0:w]),
                        reads=[(sk, 0)], writes=[(wk, half)], tag="wcast")
                else:
                    p.op("vector", lambda e, sl=sl, wsl=wsl, half=half, w=w: e.tensor_copy(
                        out=wstrip[:, wsl][:, half * 1024:half * 1024 + w],
                        in_=stg[:, sl][:, 0:w]),
                        reads=[(sk, 0)], writes=[(wk, half)], tag="wcast")
                keys.append((wk, half))
            return keys

        def load_qkv_strip(oc, wi, cast_eng=None, src=None):
            hbm = src if src is not None else wqkvT
            wsl, wk = WST[wi]
            p.dma("gpsimd", lambda e, oc=oc, wsl=wsl, hbm=hbm: e.dma_start(
                out=wstrip[:, wsl][:, 0:KC * 128].rearrange("p (k n) -> p k n", n=128),
                in_=hbm[oc].rearrange("k p n -> p k n")),
                writes=[(wk, 0)], tag="wdma")
            return [(wk, 0)]


        for h in range(NHALF):
            tok0 = h * HTOK
            kp = f"L1h{h}_"
            for kc in range(KC):
                p.dma("sync", lambda e, kc=kc, tok0=tok0: e.dma_start(
                    out=x_half[:, kc * HTOK:(kc + 1) * HTOK],
                    in_=x_fm[kc][:, tok0:tok0 + HTOK]),
                    writes=[("xh", kc)], slot=f"xh{kc}", tag="ldx")

            def dst(kc, tok0=tok0, kp=kp):
                p.op("vector", lambda e, kc=kc: e.tensor_mul(
                    xln[:, kc * S + tok0: kc * S + tok0 + HTOK],
                    tA[:, 0:HTOK], sA[:, 0:HTOK]),
                    reads=[("LNt", 0), ("LNs", 0)],
                    writes=[("xln", (kc, tok0 // HTOK))], tag="nrm")
            emit_ln(kp, HTOK,
                    lambda kc: x_half[:, kc * HTOK:(kc + 1) * HTOK],
                    lambda kc: [("xh", kc)], dst, (mA, sA, tA, xbA, sqA))

        def xlnkey(kc, col):
            return ("xln", (kc, col // HTOK))
        if dbg:
            p.dma("gpsimd", lambda e: e.dma_start(
                out=dbg_h["d_xln"].rearrange("k p s -> p k s"), in_=xln[:].rearrange("p (k s) -> p k s", s=S)),
                reads=[("xln", (kc, h)) for kc in range(KC) for h in range(NHALF)], tag="dbg")

        _pre_strip = {}
        for i in range(2):
            _pre_strip[i] = load_qkv_strip(QC + i, i)

        p.fence_all("endP1", dma=dbg)

        NS4 = S // 512
        for i, oc in enumerate(range(QC, QC + KCo)):
            wi = i % 2
            wkeys = _pre_strip.pop(i) if i in _pre_strip else load_qkv_strip(oc, wi)
            wsl = WST[wi][0]
            for kc in range(KC):
                for t4 in range(NS4):
                    bank = psum[(i % 2) * 4 + t4 % 4]
                    p.op("tensor", lambda e, kc=kc, t4=t4, bank=bank, wsl=wsl: e.matmul(
                        bank[:, 0:512], wstrip[:, wsl][:, kc * 128:(kc + 1) * 128],
                        xln[:, kc * S + t4 * 512: kc * S + (t4 + 1) * 512],
                        start=(kc == 0), stop=(kc == KC - 1)),
                        reads=[xlnkey(kc, t4 * 512)] + wkeys,
                        writes=[(bank.name, 0)], tag="kmm")
            for t4 in range(NS4):
                bank = psum[(i % 2) * 4 + t4 % 4]
                p.op("scalar", lambda e, i=i, oc=oc, t4=t4, bank=bank: e.activation(
                    out=kbuf[:, i * S + t4 * 512: i * S + (t4 + 1) * 512],
                    in_=bank[:, 0:512], func=AF.Identity, bias=bqkv[:, oc:oc + 1]),
                    reads=[(bank.name, 0), ("bqkv", 0)], writes=[("k", (i, t4))], tag="kevac")
        kkeys = [("k", (i, t4)) for i in range(KCo) for t4 in range(NS4)]

        for vi in range(VC):
            oc = QC + KCo + vi
            wi = vi % 2
            wkeys = load_qkv_strip(oc, wi, cast_eng="vector")
            wsl = WST[wi][0]
            vsl, vkey = (slice(0, S), "vfm0") if vi % 2 == 0 else (slice(S, 2 * S), "vfm1")
            for kc in range(KC):
                for t4 in range(NS4):
                    bank = psum[(vi % 2) * 4 + t4 % 4]
                    p.op("tensor", lambda e, kc=kc, t4=t4, bank=bank, wsl=wsl: e.matmul(
                        bank[:, 0:512], wstrip[:, wsl][:, kc * 128:(kc + 1) * 128],
                        xln[:, kc * S + t4 * 512: kc * S + (t4 + 1) * 512],
                        start=(kc == 0), stop=(kc == KC - 1)),
                        reads=[xlnkey(kc, t4 * 512)] + wkeys,
                        writes=[(bank.name, 0)], tag="vmm")
            for t4 in range(NS4):
                bank = psum[(vi % 2) * 4 + t4 % 4]
                p.op("scalar", lambda e, vi=vi, oc=oc, t4=t4, bank=bank, vsl=vsl: e.activation(
                    out=vfm[:, vsl][:, t4 * 512:(t4 + 1) * 512],
                    in_=bank[:, 0:512], func=AF.Identity, bias=bqkv[:, oc:oc + 1]),
                    reads=[(bank.name, 0), ("bqkv", 0)], writes=[(vkey, t4)], tag="vevac")
            kv = vi // 2
            sub = vi % 2
            for sc in range(SB):
                bi = 6 + sc % 2
                bank = psumb[bi]
                p.op("tensor", lambda e, sc=sc, bank=bank, vsl=vsl: e.transpose(
                    bank[:, 0:128], vfm[:, vsl][:, sc * 128:(sc + 1) * 128], idb[:, :]),
                    reads=[(vkey, (sc * 128) // 512), ("idb", 0)],
                    writes=[(PK[bi], 0)], tag="vT")
                p.op("vector", lambda e, sc=sc, kv=kv, sub=sub, bank=bank: e.tensor_copy(
                    out=vbuf[:, sc * VS + kv * 257 + sub * 128: sc * VS + kv * 257 + sub * 128 + 128],
                    in_=bank[:, 0:128]),
                    reads=[(PK[bi], 0)], writes=[("v", (sc, vi))], tag="vTe")
        for sc in range(SB):
            for kv in range(NKV):
                p.op("vector", lambda e, sc=sc, kv=kv: e.memset(
                    vbuf[:, sc * VS + kv * 257 + 256: sc * VS + kv * 257 + 257], 1.0),
                    writes=[("v", (sc, VC + kv))], tag="vones")
        vkeys = [("v", (sc, i)) for sc in range(SB) for i in range(VC + NKV)]

        if dbg:
            p.dma("gpsimd", lambda e: e.dma_start(
                out=dbg_h["d_k"].rearrange("k p s -> p k s"), in_=kbuf[:].rearrange("p (k s) -> p k s", s=S)),
                reads=kkeys, tag="dbg")
            p.dma("gpsimd", lambda e: e.dma_start(out=dbg_h["d_v"], in_=vbuf[:]),
                  reads=vkeys, tag="dbg")

        p.fence_all("endP2a", dma=dbg)

        for oh in range(NOH):
            t0 = oh * OH
            kp = f"LOh{oh}_"
            for kc in range(KC):
                p.dma("sync", lambda e, kc=kc, t0=t0: e.dma_start(
                    out=xo_half[:, kc * OH:(kc + 1) * OH],
                    in_=x_own[kc][:, t0:t0 + OH]),
                    writes=[("xo", kc)], slot=f"xo{kc}", tag="ldxo")

            def dsto(kc, t0=t0, oh=oh, kp=kp):
                p.op("vector", lambda e, kc=kc, t0=t0: e.tensor_mul(
                    xow[:, kc * OT + t0: kc * OT + t0 + OH], tA[:, 0:OH], sA[:, 0:OH]),
                    reads=[("LNt", 0), ("LNs", 0)],
                    writes=[("xow", (kc, oh))], tag="nrmo")
            emit_ln(kp, OH,
                    lambda kc: xo_half[:, kc * OH:(kc + 1) * OH],
                    lambda kc: [("xo", kc)], dsto, (mA, sA, tA, xbA, sqA))
        for oc in range(QC):
            wi = oc % 2
            wkeys = load_strip_hwdge(oc, wi, wqkvT, KC,
                                     cast_eng=("vector" if oc % 2 else "scalar"))
            wsl = WST[wi][0]
            for oh in range(NOH):
                t0 = oh * OH
                bank = psum[(oc % 2) * 2 + oh % 2]
                for kc in range(KC):
                    p.op("tensor", lambda e, kc=kc, t0=t0, bank=bank, wsl=wsl: e.matmul(
                        bank[:, 0:OH], wstrip[:, wsl][:, kc * 128:(kc + 1) * 128],
                        xow[:, kc * OT + t0: kc * OT + t0 + OH],
                        start=(kc == 0), stop=(kc == KC - 1)),
                        reads=[("xow", (kc, oh))] + wkeys,
                        writes=[(bank.name, 0)], tag="qmm")
                p.op("scalar", lambda e, oc=oc, t0=t0, bank=bank: e.activation(
                    out=qbuf[:, oc * OT + t0: oc * OT + t0 + OH],
                    in_=bank[:, 0:OH], func=AF.Identity, bias=bqkv[:, oc:oc + 1]),
                    reads=[(bank.name, 0), ("bqkv", 0)], writes=[("q", (oc, oh, 0))], tag="qevac")
        qkeys = [("q", (oc, oh, 0)) for oc in range(QC) for oh in range(NOH)]
        if dbg:
            p.dma("gpsimd", lambda e: e.dma_start(
                out=dbg_h["d_q"].rearrange("k p s -> p k s"), in_=qbuf[:].rearrange("p (k s) -> p k s", s=OT)),
                reads=qkeys, tag="dbg")

        p.fence_all("endP2b", dma=dbg)

        _pre_wo = {oc: load_strip_hwdge(oc, oc % 2, woT, QC,
                                        cast_eng=("vector" if oc % 2 else "scalar"))
                   for oc in range(2)}

        for h in range(NH):
            kv = h // (NH // NKV) if NKV > 0 else 0
            koff = [(kv * 2 + c) * S for c in range(2)]
            qoffc = [(h * 2 + c) * OT for c in range(2)]
            for jp in range(NJ // 2):
                j0, j1 = 2 * jp, 2 * jp + 1
                ctx1 = 2 * j0 + 2
                ctx = 2 * j1 + 2
                obank = [psum[4 + t] for t in range(4)]

                def emit_pv(kc):
                    jlist = [j0, j1] if kc < ctx1 else [j1]
                    pp = kc % 3
                    for j in jlist:
                        for c in range(2):
                            ob = obank[2 * (j - j0) + c]
                            poff = pp * 512 + c * 256 + (j - j0 if kc < ctx1 else 0) * 128
                            p.op("tensor", lambda e, kc=kc, j=j, c=c, ob=ob, poff=poff, kv=kv: e.matmul(
                                ob[:, 0:257], psb[:, poff:poff + 128],
                                vbuf[:, kc * VS + kv * 257: kc * VS + (kv + 1) * 257],
                                start=(kc == 0), stop=(kc == 2 * j + 1)),
                                reads=[("P", (pp, c))] + vkeys,
                                writes=[(ob.name, 0)], tag="pv")

                for kc in range(ctx):
                    qw = 256 if kc < ctx1 else 128
                    qo = (j0 if kc < ctx1 else j1) * 128
                    sbank = psum[kc % 3]
                    pp = kc % 3
                    for c in range(2):
                        p.op("tensor", lambda e, kc=kc, c=c, qw=qw, qo=qo, sbank=sbank,
                             koff=koff, qoffc=qoffc: e.matmul(
                            sbank[:, c * 256: c * 256 + qw],
                            kbuf[:, koff[c] + kc * 128: koff[c] + (kc + 1) * 128],
                            qbuf[:, qoffc[c] + qo: qoffc[c] + qo + qw],
                            start=True, stop=True),
                            reads=kkeys + qkeys,
                            writes=[(sbank.name, 0)], tag="smm")
                    for c in range(2):
                        p.op("scalar", lambda e, kc=kc, c=c, qw=qw, pp=pp, sbank=sbank: e.activation(
                            out=psb[:, pp * 512 + c * 256: pp * 512 + c * 256 + qw],
                            in_=sbank[:, c * 256: c * 256 + qw],
                            func=AF.Exp, bias=consts[:, 0:1], scale=SCALE),
                            reads=[(sbank.name, 0), ("c0", 0)],
                            writes=[("P", (pp, c))], tag="exp")
                    for j in ([j0, j1] if kc < ctx1 else [j1]):
                        if kc >= 2 * j:
                            moff = (j * 2 + (kc - 2 * j)) * 128
                            poffj = pp * 512 + (j - j0 if kc < ctx1 else 0) * 128
                            for c in range(2):
                                p.op("vector", lambda e, c=c, poffj=poffj, moff=moff: e.tensor_mul(
                                    psb[:, poffj + c * 256: poffj + c * 256 + 128],
                                    psb[:, poffj + c * 256: poffj + c * 256 + 128],
                                    mask_sb[:, moff:moff + 128]),
                                    reads=[("P", (pp, c)), ("masks", 0)],
                                    writes=[("P", (pp, c))], tag="mask")
                    if kc > 0:
                        emit_pv(kc - 1)
                emit_pv(ctx - 1)
                for j in (j0, j1):
                    ob1 = obank[2 * (j - j0)]
                    ob2 = obank[2 * (j - j0) + 1]
                    rp = ((h * (NJ // 2) + jp) * 2 + (j - j0)) % 4
                    ro = rp * 4
                    oo = rp * 512
                    p.op("vector", lambda e, ob1=ob1, ro=ro: e.reciprocal(rsc[:, ro:ro + 1], ob1[:, 256:257]),
                         reads=[(ob1.name, 0)], writes=[("r1", rp)], tag="r1")
                    p.op("vector", lambda e, ob2=ob2, ro=ro: e.reciprocal(rsc[:, ro + 1:ro + 2], ob2[:, 256:257]),
                         reads=[(ob2.name, 0)], writes=[("r2", rp)], tag="r2")
                    p.op("vector", lambda e, ro=ro: e.tensor_mul(rsc[:, ro + 2:ro + 3], rsc[:, ro + 1:ro + 2], lam_sb[:, 4:5]),
                         reads=[("r2", rp), ("lam", 0)], writes=[("r2l", rp)], tag="r2l")
                    p.op("vector", lambda e, ob1=ob1, ro=ro, oo=oo: e.tensor_mul(
                        osc[:, oo:oo + 256], ob1[:, 0:256], rsc[:, ro:ro + 1].to_broadcast([128, 256])),
                        reads=[(ob1.name, 0), ("r1", rp)],
                        writes=[("t1", rp)], tag="t1")
                    p.op("vector", lambda e, ob2=ob2, ro=ro, oo=oo: e.tensor_mul(
                        osc[:, oo + 256:oo + 512], ob2[:, 0:256], rsc[:, ro + 2:ro + 3].to_broadcast([128, 256])),
                        reads=[(ob2.name, 0), ("r2l", rp)],
                        writes=[("t2", rp)], tag="t2")
                    p.op("vector", lambda e, j=j, h=h, oo=oo, rp=rp: e.tensor_sub(
                        otm[:, j * QD + h * 256: j * QD + h * 256 + 256],
                        osc[:, oo:oo + 256], osc[:, oo + 256:oo + 512]),
                        reads=[("t1", rp), ("t2", rp)], writes=[("otm", (j, h))], tag="ocomb")
        otmkeys = [("otm", (j, h)) for j in range(NJ) for h in range(NH)]

        p.fence_all("endP3", dma=dbg)

        for j in range(NJ):
            p.op("vector", lambda e, j=j: e.tensor_mul(
                osc[:, 0:QD], otm[:, j * QD:(j + 1) * QD], otm[:, j * QD:(j + 1) * QD]),
                reads=[("otm", (j, h)) for h in range(NH)], writes=[("osc4", 0)], tag="sq")
            p.op("vector", lambda e, j=j: e.tensor_reduce(
                out=ms_sb[:, j * NH:(j + 1) * NH],
                in_=osc[:, 0:QD].rearrange("p (h n) -> p h n", n=256),
                axis=AX.X, op=ALU.add),
                reads=[("osc4", 0)], writes=[("ms", j)], tag="msum")
        mskeys = [("ms", j) for j in range(NJ)]
        p.op("vector", lambda e: e.tensor_scalar(
            out=ms_sb[:, :], in0=ms_sb[:, :], scalar1=1.0 / 256, scalar2=SUBLN_EPS,
            op0=ALU.mult, op1=ALU.add),
            reads=mskeys, writes=mskeys, tag="msc")
        p.op("scalar", lambda e: e.activation(out=ms_sb[:, :], in_=ms_sb[:, :], func=AF.Sqrt),
             reads=mskeys, writes=mskeys, tag="msqrt")
        p.op("vector", lambda e: e.reciprocal(ms_sb[:, :], ms_sb[:, :]),
             reads=mskeys, writes=mskeys, tag="mrcp")
        for j in range(NJ):
            p.op("vector", lambda e, j=j: e.tensor_mul(
                osc[:, 0:QD].rearrange("p (h n) -> p h n", n=256),
                otm[:, j * QD:(j + 1) * QD].rearrange("p (h n) -> p h n", n=256),
                weff[:, :].rearrange("p (o n) -> p o n", o=1).to_broadcast([128, NH, 256])),
                reads=[("otm", (j, h)) for h in range(NH)] + [("weff", 0)] + mskeys,
                writes=[("osc4", 0)], tag="wmul")
            p.op("vector", lambda e, j=j: e.tensor_mul(
                otm[:, j * QD:(j + 1) * QD].rearrange("p (h n) -> p h n", n=256),
                osc[:, 0:QD].rearrange("p (h n) -> p h n", n=256),
                ms_sb[:, j * NH:(j + 1) * NH].rearrange("p (h o) -> p h o", o=1)
                .to_broadcast([128, NH, 256])),
                reads=[("osc4", 0)] + mskeys,
                writes=[("otm", (j, h)) for h in range(NH)], tag="rmul")
            for odc in range(QC):
                bi = odc % 4
                bank = psumb[bi]
                p.op("tensor", lambda e, j=j, odc=odc, bank=bank: e.transpose(
                    bank[:, 0:128], otm[:, j * QD + odc * 128: j * QD + (odc + 1) * 128], idb[:, :]),
                    reads=[("otm", (j, odc // 2)), ("idb", 0)],
                    writes=[(PK[bi], 0)], tag="oT")
                p.op("scalar", lambda e, j=j, odc=odc, bank=bank: e.copy(
                    out=O_fm[:, odc * OT + j * 128: odc * OT + (j + 1) * 128],
                    in_=bank[:, 0:128]),
                    reads=[(PK[bi], 0)], writes=[("ofm", (odc, j))], tag="oTe")
        if dbg:
            p.dma("gpsimd", lambda e: e.dma_start(out=dbg_h["d_otm"], in_=otm[:]),
                  reads=otmkeys, tag="dbg")
        ofmkeys = [("ofm", (odc, j)) for odc in range(QC) for j in range(NJ)]

        p.fence_all("endP5", dma=dbg)

        def load_w13(eidx):
            keys = {"w1": [], "w3": []}
            for nm, hbm, dst in (("w1", w1T_h, w1b), ("w3", w3T_h, w3b)):
                src = (hbm[eidx] if eidx < E else (sw1T_h if nm == "w1" else sw3T_h))
                hh = HID if eidx < E else SHID
                nq = 4 if KC % 4 == 0 else 2
                for q4 in range(nq):
                    klo, khi = q4 * (KC // nq), (q4 + 1) * (KC // nq)
                    p.dma("gpsimd", lambda e, src=src, dst=dst, klo=klo, khi=khi, hh=hh: e.dma_start(
                        out=dst[:, klo * HID: klo * HID + (khi - klo) * HID]
                            .rearrange("p (k n) -> p k n", n=HID)[:, :, 0:hh],
                        in_=src[klo:khi].rearrange("k p n -> p k n")),
                        writes=[(nm, q4)], tag="w13dma")
                    keys[nm].append((nm, q4))
            return keys

        wkeys13_pre = load_w13(0)

        sl, sk = SLOT[0]
        p.dma("scalar", lambda e, sl=sl: e.dma_start(
            out=stg[:, sl][:, 0:KC * 8].rearrange("p (k n) -> p k n", n=8),
            in_=gwT_h.rearrange("k p n -> p k n")),
            writes=[(sk, 0)], tag="gdma")
        p.op("vector", lambda e, sl=sl: e.tensor_copy(
            out=gwb[:, 0:KC * 8], in_=stg[:, sl][:, 0:KC * 8]),
            reads=[(sk, 0)], writes=[("gwb", 0)], tag="gcast")

        for oc in range(KC):
            wi = oc % 2
            wkeys = (_pre_wo.pop(oc) if oc in _pre_wo else
                     load_strip_hwdge(oc, wi, woT, QC,
                                      cast_eng=("vector" if oc % 2 else "scalar")))
            wsl = WST[wi][0]
            for kc2 in range(QC):
                for tc in range(TC):
                    bank = psum[4 + (oc % 2) * 2 + tc % 2]
                    jlo = (tc * TCW) // 128
                    jhi = min(NJ, ((tc + 1) * TCW) // 128)
                    p.op("tensor", lambda e, kc2=kc2, tc=tc, bank=bank, wsl=wsl: e.matmul(
                        bank[:, 0:TCW], wstrip[:, wsl][:, kc2 * 128:(kc2 + 1) * 128],
                        O_fm[:, kc2 * OT + tc * TCW: kc2 * OT + (tc + 1) * TCW],
                        start=(kc2 == 0), stop=(kc2 == QC - 1)),
                        reads=[("ofm", (kc2, j)) for j in range(jlo, jhi)] + wkeys,
                        writes=[(bank.name, 0)], tag="omm")
            for tc in range(TC):
                bank = psum[4 + (oc % 2) * 2 + tc % 2]
                xi = (oc * TC + tc) % 2
                p.dma("sync", lambda e, oc=oc, tc=tc, xi=xi: e.dma_start(
                    out=xstg[:, xi * TCW:(xi + 1) * TCW],
                    in_=x_own[oc][:, tc * TCW:(tc + 1) * TCW]),
                    writes=[("xstg", xi)], slot=f"xres{xi}", tag="ldres")
                p.op("scalar", lambda e, oc=oc, tc=tc, bank=bank: e.activation(
                    out=X[:, oc * OT + tc * TCW: oc * OT + (tc + 1) * TCW],
                    in_=bank[:, 0:TCW], func=AF.Identity, bias=bo_sb[:, oc:oc + 1]),
                    reads=[(bank.name, 0), ("bo", 0)], writes=[("X", (oc, tc))], tag="oevac")
                p.op("vector", lambda e, oc=oc, tc=tc, xi=xi: e.tensor_add(
                    X[:, oc * OT + tc * TCW: oc * OT + (tc + 1) * TCW],
                    X[:, oc * OT + tc * TCW: oc * OT + (tc + 1) * TCW],
                    xstg[:, xi * TCW:(xi + 1) * TCW]),
                    reads=[("xstg", xi), ("X", (oc, tc))], writes=[("X", (oc, tc))], tag="resadd")
        Xkeys = [("X", (oc, tc)) for oc in range(KC) for tc in range(TC)]
        if dbg:
            p.dma("gpsimd", lambda e: e.dma_start(
                out=dbg_h["d_Xmid"].rearrange("k p s -> p k s"), in_=X[:].rearrange("p (k s) -> p k s", s=OT)),
                reads=Xkeys, tag="dbg")

        p.fence_all("endP6", dma=dbg)

        kp = "L2_"
        assert OT <= HTOK

        def dst2(kc, kp=kp):
            p.op("vector", lambda e, kc=kc: e.tensor_mul(
                xln2[:, kc * OT:(kc + 1) * OT], tB[:, 0:OT], sB[:, 0:OT]),
                reads=[("LNt", 0), ("LNs", 0)], writes=[("xln2", kc)], tag="nrm2")
        emit_ln(kp, OT,
                lambda kc: X[:, kc * OT:(kc + 1) * OT],
                lambda kc: [("X", (kc, tc)) for tc in range(TC)], dst2, (mB, sB, tB, xbB, sqB))
        if dbg:
            p.fence_all("endP7", dma=True)
        x2keys = [("xln2", kc) for kc in range(KC)]
        if dbg:
            p.dma("gpsimd", lambda e: e.dma_start(
                out=dbg_h["d_xln2"].rearrange("k p s -> p k s"), in_=xln2[:].rearrange("p (k s) -> p k s", s=OT)),
                reads=x2keys, tag="dbg")

        scp_sl = SLOT[1][0]
        for tcg in range(TC):
            bank = psum[tcg % 2]
            for kc in range(KC):
                p.op("tensor", lambda e, kc=kc, tcg=tcg, bank=bank: e.matmul(
                    bank[0:8, 0:TCW], gwb[:, kc * 8:(kc + 1) * 8],
                    xln2[:, kc * OT + tcg * TCW: kc * OT + (tcg + 1) * TCW],
                    start=(kc == 0), stop=(kc == KC - 1)),
                    reads=[("xln2", kc), ("gwb", 0)], writes=[(bank.name, 0)], tag="gmm")
            p.op("scalar", lambda e, tcg=tcg, bank=bank: e.activation(
                out=stg[:, scp_sl][0:8, tcg * TCW:(tcg + 1) * TCW],
                in_=bank[0:8, 0:TCW], func=AF.Sigmoid),
                reads=[(bank.name, 0)], writes=[("stg1", 0)], tag="sig")
        p.op("vector", lambda e: e.memset(b_fm[:, :], 0.0), writes=[("bfm", "z")], tag="bz")
        for tb in range(OT // 128):
            sb8 = (tb % 2) * 48
            bank = psum[2 + tb % 2]
            p.op("tensor", lambda e, tb=tb, bank=bank, sb8=sb8: e.transpose(
                bank[:, 0:8], stg[:, scp_sl][0:8, tb * 128:(tb + 1) * 128], idf[0:8, 0:8]),
                reads=[("stg1", 0), ("idf", 0)], writes=[(bank.name, 0)], tag="gT")
            p.op("vector", lambda e, tb=tb, bank=bank, sb8=sb8: e.tensor_copy(
                out=scr8[:, sb8 + 0:sb8 + 8], in_=bank[:, 0:8]),
                reads=[(bank.name, 0)], writes=[("stm", tb % 2)], tag="gTe")
            p.op("vector", lambda e, sb8=sb8, tb=tb: e.max(
                out=scr8[:, sb8 + 8:sb8 + 16], in_=scr8[:, sb8 + 0:sb8 + 8]),
                reads=[("stm", tb % 2)], writes=[("mx8", tb % 2)], tag="max8")
            p.op("vector", lambda e, sb8=sb8, tb=tb: e.tensor_tensor(
                scr8[:, sb8 + 16:sb8 + 24], scr8[:, sb8 + 0:sb8 + 8],
                scr8[:, sb8 + 9:sb8 + 10].to_broadcast([128, 8]), op=ALU.is_ge),
                reads=[("stm", tb % 2), ("mx8", tb % 2)], writes=[("msk", tb % 2)], tag="gmask")
            p.op("vector", lambda e, sb8=sb8, tb=tb: e.tensor_mul(
                scr8[:, sb8 + 24:sb8 + 32], scr8[:, sb8 + 16:sb8 + 24], scr8[:, sb8 + 0:sb8 + 8]),
                reads=[("msk", tb % 2), ("stm", tb % 2)], writes=[("btm", tb % 2)], tag="btm")
            bank2 = psum[4 + tb % 2]
            p.op("tensor", lambda e, tb=tb, bank2=bank2, sb8=sb8: e.transpose(
                bank2[0:8, 0:128], scr8[:, sb8 + 24:sb8 + 32], idf[:, :]),
                reads=[("btm", tb % 2), ("idf", 0)], writes=[(bank2.name, 0)], tag="bT")
            p.op("vector", lambda e, tb=tb, bank2=bank2: e.tensor_copy(
                out=b_fm[0:8, tb * 128:(tb + 1) * 128], in_=bank2[0:8, 0:128]),
                reads=[(bank2.name, 0), ("bfm", "z")], writes=[("bfm", tb)], tag="bTe")
        bfmkeys = [("bfm", tb) for tb in range(OT // 128)] + [("bfm", "z")]
        if dbg:
            p.dma("gpsimd", lambda e: e.dma_start(out=dbg_h["d_bfm"], in_=b_fm[:, :]),
                  reads=bfmkeys, tag="dbg")
            p.fence_all("endP8", dma=True)


        wkeys13 = wkeys13_pre
        for eidx in range(E + 1):
            shared = eidx == E
            hc_n = SHC if shared else HC
            hh = SHID if shared else HID
            if not shared:
                p.op("vector", lambda e, eidx=eidx: e.tensor_mul(
                    selS[:, :], onesf[:, :], id8[:, eidx:eidx + 1].to_broadcast([128, 128])),
                    reads=[("onesf", 0), ("id8", 0)], writes=[("sel", 0)], tag="sel")
                for bs in range(TC):
                    bank = psum[6]
                    p.op("tensor", lambda e, bs=bs, bank=bank: e.matmul(
                        bank[:, 0:TCW], selS[:, :], b_fm[:, bs * TCW:(bs + 1) * TCW],
                        start=True, stop=True),
                        reads=[("sel", 0)] + bfmkeys, writes=[(bank.name, 0)], tag="bmm")
                    p.op("vector", lambda e, bs=bs, bank=bank: e.tensor_copy(
                        out=B_sb[:, bs * TCW:(bs + 1) * TCW], in_=bank[:, 0:TCW]),
                        reads=[(bank.name, 0)], writes=[("B", bs)], tag="bev")
            Bkeys = [("B", bs) for bs in range(TC)]
            groups = [(g,) for g in range(hc_n)]
            for gi, g in enumerate(groups):
                ub = {}
                for i2, hc in enumerate(g):
                    for tc in range(TC):
                        ub[(hc, tc)] = psum[((gi % 2) * 4 + i2 * TC + tc) % 8]
                for kc in range(KC):
                    for hc in g:
                        for tc in range(TC):
                            b = ub[(hc, tc)]
                            p.op("tensor", lambda e, kc=kc, hc=hc, tc=tc, b=b: e.matmul(
                                b[:, 0:TCW], w1b[:, kc * HID + hc * 128: kc * HID + (hc + 1) * 128],
                                xln2[:, kc * OT + tc * TCW: kc * OT + (tc + 1) * TCW],
                                start=(kc == 0), stop=(kc == KC - 1)),
                                reads=[("xln2", kc), wkeys13["w1"][kc * len(wkeys13["w1"]) // KC]],
                                writes=[(b.name, 0)], tag="U")
                for i2, hc in enumerate(g):
                    for tc in range(TC):
                        b = ub[(hc, tc)]
                        so = tc * TCW
                        ei = (gi * TC + tc) % 2
                        if shared:
                            p.op("scalar", lambda e, b=b, so=so: e.activation(
                                out=sUg[:, so:so + TCW], in_=b[:, 0:TCW], func=AF.Silu),
                                reads=[(b.name, 0)], writes=[("sU", so)], tag="silu")
                        else:
                            p.op("vector", lambda e, b=b, tc=tc, ei=ei: e.tensor_mul(
                                escr[:, ei * TCW:(ei + 1) * TCW], b[:, 0:TCW],
                                B_sb[:, tc * TCW:(tc + 1) * TCW]),
                                reads=[(b.name, 0)] + Bkeys,
                                writes=[("escr", ei)], tag="ub")
                            p.op("scalar", lambda e, so=so, ei=ei: e.activation(
                                out=sUg[:, so:so + TCW], in_=escr[:, ei * TCW:(ei + 1) * TCW],
                                func=AF.Silu),
                                reads=[("escr", ei)], writes=[("sU", so)], tag="silu")
                vb = {}
                for i2, hc in enumerate(g):
                    for tc in range(TC):
                        vb[(hc, tc)] = psum[((gi % 2) * 4 + 4 + i2 * TC + tc) % 8]
                for kc in range(KC):
                    for hc in g:
                        for tc in range(TC):
                            b = vb[(hc, tc)]
                            p.op("tensor", lambda e, kc=kc, hc=hc, tc=tc, b=b: e.matmul(
                                b[:, 0:TCW], w3b[:, kc * HID + hc * 128: kc * HID + (hc + 1) * 128],
                                xln2[:, kc * OT + tc * TCW: kc * OT + (tc + 1) * TCW],
                                start=(kc == 0), stop=(kc == KC - 1)),
                                reads=[("xln2", kc), wkeys13["w3"][kc * len(wkeys13["w3"]) // KC]],
                                writes=[(b.name, 0)], tag="V")
                for i2, hc in enumerate(g):
                    for tc in range(TC):
                        b = vb[(hc, tc)]
                        so = tc * TCW
                        ei = (gi * TC + tc + 1) % 2
                        if shared:
                            p.op("vector", lambda e, b=b, so=so, hc=hc, tc=tc: e.tensor_mul(
                                Hb[:, hc * OT + tc * TCW: hc * OT + (tc + 1) * TCW],
                                sUg[:, so:so + TCW], b[:, 0:TCW]),
                                reads=[("sU", so), (b.name, 0)],
                                writes=[("H", (hc, tc))], tag="H")
                        else:
                            p.op("vector", lambda e, b=b, tc=tc, ei=ei: e.tensor_mul(
                                escr[:, ei * TCW:(ei + 1) * TCW], b[:, 0:TCW],
                                B_sb[:, tc * TCW:(tc + 1) * TCW]),
                                reads=[(b.name, 0)] + Bkeys,
                                writes=[("escr", ei)], tag="vb")
                            p.op("vector", lambda e, so=so, hc=hc, tc=tc, ei=ei: e.tensor_mul(
                                Hb[:, hc * OT + tc * TCW: hc * OT + (tc + 1) * TCW],
                                sUg[:, so:so + TCW], escr[:, ei * TCW:(ei + 1) * TCW]),
                                reads=[("sU", so), ("escr", ei)],
                                writes=[("H", (hc, tc))], tag="H")
            if eidx < E:
                wkeys13 = load_w13(eidx + 1)
            w2src = sw2T_h if shared else w2T_h[eidx]
            for oc in range(KC):
                wi2 = oc % 3
                sl, sk = SLOT[p.stg_rr % len(SLOT)]
                p.stg_rr += 1
                p.dma("sync", lambda e, oc=oc, sl=sl, w2src=w2src, hc_n=hc_n: e.dma_start(
                    out=stg[:, sl][:, 0:hc_n * 128].rearrange("p (k n) -> p k n", n=128),
                    in_=w2src[oc][0:hc_n].rearrange("k p n -> p k n")),
                    writes=[(sk, 0)], tag="w2dma")
                p.op("scalar", lambda e, wi2=wi2, sl=sl, hc_n=hc_n: e.copy(
                    out=w2b[:, wi2 * HC * 128: wi2 * HC * 128 + hc_n * 128],
                    in_=stg[:, sl][:, 0:hc_n * 128]),
                    reads=[(sk, 0)], writes=[("w2b", wi2)], tag="w2cast")
                for tc in range(TC):
                    bank = psum[(oc * TC + tc) % 2]
                    for hc in range(hc_n):
                        p.op("tensor", lambda e, oc=oc, tc=tc, hc=hc, bank=bank, wi2=wi2: e.matmul(
                            bank[:, 0:TCW], w2b[:, wi2 * HC * 128 + hc * 128: wi2 * HC * 128 + (hc + 1) * 128],
                            Hb[:, hc * OT + tc * TCW: hc * OT + (tc + 1) * TCW],
                            start=(hc == 0), stop=(hc == hc_n - 1)),
                            reads=[("H", (hc, tc)), ("w2b", wi2)],
                            writes=[(bank.name, 0)], tag="W2")
                    p.op("vector", lambda e, oc=oc, tc=tc, bank=bank: e.tensor_add(
                        X[:, oc * OT + tc * TCW: oc * OT + (tc + 1) * TCW],
                        X[:, oc * OT + tc * TCW: oc * OT + (tc + 1) * TCW], bank[:, 0:TCW]),
                        reads=[(bank.name, 0), ("X", (oc, tc))],
                        writes=[("X", (oc, tc))], tag="Xacc")

        for kc in range(KC):
            p.dma("sync", lambda e, kc=kc: e.dma_start(
                out=out_h[kc], in_=X[:, kc * OT:(kc + 1) * OT]),
                reads=[("X", (kc, tc)) for tc in range(TC)], slot="out", tag="store")
        p.wait_all("sync")
        p.build(es)
    return nc


def pack_inputs(cfg, inputs):
    D, S = cfg["D"], cfg["S"]
    NH, NKV, E = cfg["NH"], cfg["NKV"], cfg["E"]
    HID, SHID = cfg["HID"], cfg["SHID"]
    KC, SB = D // 128, S // 128
    NJ = SB // 2
    OT = NJ * 128
    QD, KD, VD = NH * 256, NKV * 256, NKV * 256
    QC, KCo, VC = QD // 128, KD // 128, VD // 128
    NOC = QC + KCo + VC
    HC, SHC = HID // 128, SHID // 128

    x = np.ascontiguousarray(inputs["L0_attn_hidden_states"], dtype=np.float32)
    wqkv = np.asarray(inputs["L0_attn_Wqkv_w"], dtype=np.float32)
    bqkv = np.asarray(inputs["L0_attn_Wqkv_b"], dtype=np.float32)
    wo = np.asarray(inputs["L0_attn_out_w"], dtype=np.float32)
    bo = np.asarray(inputs["L0_attn_out_b"], dtype=np.float32)
    lq1 = np.asarray(inputs["L0_attn_lambda_q1"], dtype=np.float32)
    lk1 = np.asarray(inputs["L0_attn_lambda_k1"], dtype=np.float32)
    lq2 = np.asarray(inputs["L0_attn_lambda_q2"], dtype=np.float32)
    lk2 = np.asarray(inputs["L0_attn_lambda_k2"], dtype=np.float32)
    subw = np.asarray(inputs["L0_attn_subln_w"], dtype=np.float32)
    gw = np.asarray(inputs["L0_ffn_gate_w"], dtype=np.float32)
    w1 = np.asarray(inputs["L0_ffn_w1"], dtype=np.float32)
    w2 = np.asarray(inputs["L0_ffn_w2"], dtype=np.float32)
    w3 = np.asarray(inputs["L0_ffn_w3"], dtype=np.float32)
    sw1 = np.asarray(inputs["L0_ffn_sw1"], dtype=np.float32)
    sw2 = np.asarray(inputs["L0_ffn_sw2"], dtype=np.float32)
    sw3 = np.asarray(inputs["L0_ffn_sw3"], dtype=np.float32)

    import ml_dtypes
    C = np.ascontiguousarray

    wqkvT = C(wqkv.reshape(NOC, 128, KC, 128).transpose(0, 2, 3, 1))
    bqkv_p = C(bqkv.reshape(NOC, 128).T)
    woT = C(wo.reshape(KC, 128, QC, 128).transpose(0, 2, 3, 1))
    bo_p = C(bo.reshape(KC, 128).T)
    lam_p = C(np.stack([lq1, lk1, lq2, lk2], axis=1))
    subw_p = C(np.broadcast_to(subw[None, :], (128, 256)))
    gwT = C(gw.T.reshape(KC, 128, 8))
    w1T = C(w1.transpose(0, 2, 1).reshape(E, KC, 128, HID))
    w3T = C(w3.transpose(0, 2, 1).reshape(E, KC, 128, HID))
    w2T = C(w2.reshape(E, KC, 128, HC, 128).transpose(0, 1, 3, 4, 2))
    sw1T = C(sw1.T.reshape(KC, 128, SHID))
    sw3T = C(sw3.T.reshape(KC, 128, SHID))
    sw2T = C(sw2.reshape(KC, 128, SHC, 128).transpose(0, 2, 3, 1))
    idf = np.eye(128, dtype=np.float32)
    idb = np.eye(128, dtype=ml_dtypes.bfloat16)

    shared = dict(wqkvT=wqkvT, bqkv=bqkv_p, woT=woT, bo=bo_p, lam_in=lam_p,
                  subw_bc=subw_p, ident_f=idf, ident_b=idb, gwT=gwT,
                  w1T=w1T, w3T=w3T, w2T=w2T, sw1T=sw1T, sw3T=sw3T, sw2T=sw2T)

    in_maps = []
    own_tok = []
    for c in range(cfg["CORES"]):
        b, par = c // 2, c % 2
        qbs = own_qbs(par, SB)
        toks = np.concatenate([np.arange(qb * 128, (qb + 1) * 128) for qb in qbs])
        own_tok.append((b, toks))
        xb = x[b]
        x_fm = C(xb.T.reshape(KC, 128, S))
        x_ow = C(xb[toks].T.reshape(KC, 128, OT))
        mk = np.zeros((NJ, 2, 128, 128), dtype=ml_dtypes.bfloat16)
        for j, qb in enumerate(qbs):
            for t in range(2):
                kcb = 2 * j + t
                if kcb < qb:
                    mk[j, t] = 1
                elif kcb == qb:
                    ki = np.arange(128)[:, None]
                    qi = np.arange(128)[None, :]
                    mk[j, t] = (ki <= qi).astype(ml_dtypes.bfloat16)
        im = dict(shared)
        im.update(x_fm=x_fm, x_own=x_ow, masks=mk)
        in_maps.append(im)
    return in_maps, own_tok


_BUILD_CACHE = {}


def kernel(**inputs):
    cfg = make_cfg(mini=False)
    key = "full"
    if key not in _BUILD_CACHE:
        _BUILD_CACHE[key] = build_program(cfg)
    nc = _BUILD_CACHE[key]
    in_maps, own_tok = pack_inputs(cfg, inputs)
    res = run_bass_kernel_spmd(nc, in_maps, core_ids=list(range(cfg["CORES"])))
    D, S = cfg["D"], cfg["S"]
    out = np.empty((cfg["B"], S, D), dtype=np.float32)
    for c in range(cfg["CORES"]):
        b, toks = own_tok[c]
        shard = np.asarray(res.results[c]["out_fm"])
        out[b, toks, :] = shard.transpose(2, 0, 1).reshape(len(toks), D)
    return out
